# version 16
# baseline (speedup 1.0000x reference)
import numpy as np
import ml_dtypes
from contextlib import ExitStack

import concourse.bass as bass
import concourse.mybir as mybir
from concourse.bass_utils import run_bass_kernel_spmd

F32 = mybir.dt.float32
BF16 = mybir.dt.bfloat16
I32 = mybir.dt.int32
AF = mybir.ActivationFunctionType
ALU = mybir.AluOpType
AX = mybir.AxisListType
BF16NP = ml_dtypes.bfloat16

NTOK = 2048
DM = 4096
DH = 1024
EPS = 1e-5


def ap2(t, off, dims):
    return bass.AP(t, off, dims)


def build_nc(tap=None, detect_races=False, exchange=False):
    upto = {"xnt": 1, "kvq": 3, "rg": 4, "hg": 5, "ffn": 6, "pv": 7}.get(tap, 8)
    KT = NTOK if exchange else 2 * NTOK
    nc = bass.Bass("TRN2", target_bir_lowering=False, debug=False,
                   detect_race_conditions=detect_races,
                   dynamic_dma_scratch_size=4096)

    x_in = nc.dram_tensor("x", [NTOK, DM], F32, kind="ExternalInput")
    if not exchange:
        xp_in = nc.dram_tensor("xp", [NTOK, DM], F32, kind="ExternalInput")
    wqk = nc.dram_tensor("wqk", [16, 128, 4096], F32, kind="ExternalInput")
    wv = nc.dram_tensor("wv", [2, 128, 16384], F32, kind="ExternalInput")
    wr = nc.dram_tensor("wr", [128, 2048], F32, kind="ExternalInput")
    at_w = nc.dram_tensor("at", [8, 128, 4096], F32, kind="ExternalInput")
    bf_w = nc.dram_tensor("bfl", [8, 128, 4096], F32, kind="ExternalInput")
    wo_w = nc.dram_tensor("wo", [8, 128, 4096], F32, kind="ExternalInput")
    bqk_in = nc.dram_tensor("bqk", [128, 16], F32, kind="ExternalInput")
    bv_in = nc.dram_tensor("bvr", [1, 1024], F32, kind="ExternalInput")
    bo_in = nc.dram_tensor("bor", [1, 4096], F32, kind="ExternalInput")
    idb_in = nc.dram_tensor("idb", [128, 128], BF16, kind="ExternalInput")
    idf_in = nc.dram_tensor("idf", [128, 128], F32, kind="ExternalInput")
    onc_in = nc.dram_tensor("onc", [128, 1], BF16, kind="ExternalInput")
    onr_in = nc.dram_tensor("onr", [1, 128], F32, kind="ExternalInput")
    em_in = nc.dram_tensor("emat", [64, 1024], BF16, kind="ExternalInput")
    non_in = (nc.dram_tensor("non", [1, 1], I32, kind="ExternalInput")
              if exchange else None)
    y_out = nc.dram_tensor("y", [NTOK, DM], F32, kind="ExternalOutput")

    q_dr = nc.dram_tensor("q_dr", [DH, NTOK], BF16)
    k_loc = nc.dram_tensor("k_loc", [DH, KT], BF16)
    v_loc = nc.dram_tensor("v_loc", [KT, DH], BF16)
    pv_dr = nc.dram_tensor("pv_dr", [DH, NTOK], BF16)
    hg_dr = nc.dram_tensor("hg_dr", [1024, NTOK], BF16)
    ffn_dr = nc.dram_tensor("ffn_dr", [NTOK, DM], BF16)
    if exchange:
        k_sh = nc.dram_tensor("k_sh", [2, DH, NTOK], BF16, addr_space="Shared")
        v_sh = nc.dram_tensor("v_sh", [2, NTOK, DH], BF16, addr_space="Shared")
        fl_sh = nc.dram_tensor("fl_sh", [2, 128], I32, addr_space="Shared")

    taps = {}

    def tap_out(name, shape, dt):
        taps[name] = nc.dram_tensor("t_" + name, shape, dt, kind="ExternalOutput")

    if tap == "xnt":
        tap_out("xnt", [128, 65536], BF16)
    elif tap == "kvq":
        tap_out("k", [2 * DH, NTOK], BF16)
        tap_out("v", [2 * NTOK, DH], BF16)
        tap_out("q", [DH, NTOK], BF16)
    elif tap == "rg":
        tap_out("sr", [128, 1024], F32)
        tap_out("g", [128, 1024], BF16)
    elif tap == "hg":
        tap_out("hg", [1024, NTOK], BF16)
    elif tap == "ffn":
        tap_out("ffn", [NTOK, DM], BF16)
    elif tap == "pv":
        tap_out("pv", [DH, NTOK], BF16)

    main = ExitStack()

    S = {}
    base_sems = ['cst', 'ini', 'ax0', 'ax1', 'ax2', 'avs', 'avr', 'aac', 'aap', 'atm', 'aev', 'aevv', 'abvm', 'abvev', 'cwf0', 'cwf1', 'cwb', 'ckm', 'ckev', 'ckst0', 'ckst1', 'cqm', 'cqev', 'cqst0', 'cqst1', 'cvf0', 'cvf1', 'cvb', 'cvm', 'cvev', 'cvst0', 'cvst1', 'brf', 'brb', 'brm', 'brev', 'brtm', 'brtev', 'bdv', 'bpl', 'bac', 'bgm', 'bgev', 'bgem', 'bgeev', 'baf0', 'baf1', 'bab', 'bhm', 'bhev', 'bhst0', 'bhst1', 'bhg', 'bbor', 'bbom', 'bboev', 'bff0', 'bff1', 'bfb', 'bfm', 'bfev', 'bfst0', 'bfst1', 'dk', 'dv', 'dq', 'dsm', 'dexp', 'dzm', 'dzr', 'dzbm', 'dzbe', 'dpvm', 'dpve', 'dpvst0', 'dpvst1', 'ep', 'ep1', 'ep2', 'ep3', 'ewf0', 'ewf1', 'ewb', 'epm', 'et1', 'exc0', 'exc1', 'efc0', 'efc1', 'exf', 'exfp', 'eyst0', 'eyst1', 'eyst2', 'eyst3', 'tapd']
    x_sems = ['cflw', 'csv', 'dkv']
    r_sems = []
    for name in base_sems + (x_sems if exchange else r_sems):
        S[name] = main.enter_context(nc.semaphore(name))

    idb = main.enter_context(nc.sbuf_tensor("idb_s", [128, 128], BF16))
    idf = main.enter_context(nc.sbuf_tensor("idf_s", [128, 128], F32))
    onc = main.enter_context(nc.sbuf_tensor("onc_s", [128, 1], BF16))
    onr = main.enter_context(nc.sbuf_tensor("onr_s", [1, 128], F32))
    emat = main.enter_context(nc.sbuf_tensor("emat_s", [64, 1024], BF16))
    bqk = main.enter_context(nc.sbuf_tensor("bqk_s", [128, 16], F32))
    snon = main.enter_context(nc.sbuf_tensor("snon_s", [1, 2], I32))
    czero = main.enter_context(nc.sbuf_tensor("czero_s", [128, 2], F32))
    NCONST = 8 if exchange else 7
    nc.const_aps.register(F32, 0.0, czero[:, 0:1])
    nc.const_aps.register(F32, EPS, czero[:, 1:2])

    s1 = ExitStack()
    xnT = s1.enter_context(nc.sbuf_tensor("xnT", [128, 65536], BF16))
    bv_b = s1.enter_context(nc.sbuf_tensor("bv_b", [128, 1024], F32))
    g_tm = s1.enter_context(nc.sbuf_tensor("g_tm", [128, 1024], BF16))
    g_Ts = s1.enter_context(nc.sbuf_tensor("g_Ts", [64, 2048], BF16))

    if upto >= 1:
        with ExitStack() as st:
            x_sb = [st.enter_context(nc.sbuf_tensor(f"x_sb{i}", [128, 4096], F32)) for i in range(3)]
            xn_sb = [st.enter_context(nc.sbuf_tensor(f"xn_sb{i}", [128, 4096], BF16)) for i in range(2)]
            stats = [st.enter_context(nc.sbuf_tensor(f"stats{i}", [128, 48], F32)) for i in range(2)]
            mv = [st.enter_context(nc.sbuf_tensor(f"mv{i}", [128, 2], F32)) for i in range(4)]
            rsd = [st.enter_context(nc.sbuf_tensor(f"rsd{i}", [128, 1], F32)) for i in range(4)]
            nmr = [st.enter_context(nc.sbuf_tensor(f"nmr{i}", [128, 1], F32)) for i in range(4)]
            lnv = [st.enter_context(nc.sbuf_tensor(f"lnv{i}", [128, 1], F32)) for i in range(4)]
            ps_t = [st.enter_context(nc.psum_tensor(f"ps_t{i}", [128, 512], F32)) for i in range(4)]
            ps_bv = st.enter_context(nc.psum_tensor("ps_bv", [128, 512], F32))
            bvr = st.enter_context(nc.sbuf_tensor("bvr_s", [1, 1024], F32))

            def ev_dst(i, j4):
                return ap2(xnT, (j4 * 4) * 2048 + i * 128,
                           [[65536, 128], [2048, 4], [1, 128]])

            with nc.named_scope("phA"), nc.Block() as blk:
                @blk.sync
                def _(sy):
                    sy.dma_start(out=idb[:], in_=idb_in[:, :]).then_inc(S["cst"], 16)
                    sy.dma_start(out=idf[:], in_=idf_in[:, :]).then_inc(S["cst"], 16)
                    sy.dma_start(out=onc[:], in_=onc_in[:, :]).then_inc(S["cst"], 16)
                    sy.dma_start(out=onr[:], in_=onr_in[:, :]).then_inc(S["cst"], 16)
                    sy.dma_start(out=emat[:], in_=em_in[:, :]).then_inc(S["cst"], 16)
                    sy.dma_start(out=bqk[:], in_=bqk_in[:, :]).then_inc(S["cst"], 16)
                    sy.dma_start(out=bvr[:], in_=bv_in[:, :]).then_inc(S["cst"], 16)
                    if exchange:
                        sy.dma_start(out=snon[0:1, 0:1], in_=non_in[:, :]).then_inc(S["cst"], 16)
                    for i in range(16):
                        if i >= 3:
                            sy.wait_ge(S["aap"], i - 2)
                        sy.dma_start(out=x_sb[i % 3][:],
                                     in_=x_in[i * 128:(i + 1) * 128, :]
                                     ).then_inc(S[f"ax{i % 3}"], 16)

                @blk.gpsimd
                def _(g):
                    if exchange:
                        g.wait_ge(S["cst"], 16 * NCONST)
                        with g.register("rn") as rn:
                            g.reg_load(rn, snon[0:1, 0:1])
                            g.reg_save(snon[0:1, 1:2], rn).then_inc(S["csv"], 1)

                @blk.vector
                def _(v):
                    v.memset(czero[:, 0:1], 0.0).then_inc(S["ini"], 1)
                    v.memset(czero[:, 1:2], EPS).then_inc(S["ini"], 1)
                    for cc in range(2):
                        v.wait_ge(S["abvm"], cc + 1)
                        v.tensor_copy(bv_b[:, cc * 512:(cc + 1) * 512],
                                      ps_bv[:]).then_inc(S["abvev"], 1)
                    for p in range(8):
                        a, bb = 2 * p, 2 * p + 1
                        for i in (a, bb):
                            v.wait_ge(S[f"ax{i % 3}"], 16 * (i // 3 + 1))
                            if p >= 1:
                                v.wait_ge(S["avs"], 18 * (p - 1) + 17 + (i - 2 * p))
                            if p >= 2:
                                v.wait_ge(S["aac"], 4 * p - 6)
                            for j in range(8):
                                v.bn_stats(stats[i % 2][:, 6 * j:6 * j + 6],
                                           x_sb[i % 3][:, 512 * j:512 * (j + 1)]
                                           ).then_inc(S["avs"], 1)
                        v.wait_ge(S["avs"], 18 * p + 16)
                        for i in (a, bb):
                            v.bn_aggr(mv[i % 4][:], stats[i % 2][:, 0:48]).then_inc(S["avs"], 1)
                        for i in (a, bb):
                            v.wait_ge(S["aac"], 4 * p + (3 if i == a else 4))
                            v.tensor_scalar(nmr[i % 4][:], mv[i % 4][:, 0:1],
                                            rsd[i % 4][:, 0:1], -1.0,
                                            ALU.mult, ALU.mult).then_inc(S["avr"], 1)
                        for i in (a, bb):
                            for j4 in range(1, 8, 2):
                                g_id = 8 * i + j4
                                v.wait_ge(S["atm"], g_id + 1)
                                v.tensor_copy(ev_dst(i, j4),
                                              ps_t[g_id % 4][:]).then_inc(S["aevv"], 1)

                @blk.scalar
                def _(s):
                    s.wait_ge(S["ini"], 2)
                    for p in range(8):
                        a, bb = 2 * p, 2 * p + 1
                        for i in (a, bb):
                            s.wait_ge(S["avs"], 18 * p + 17 + (i - a))
                            s.activation(lnv[i % 4][:], mv[i % 4][:, 1:2], AF.Ln,
                                         bias=EPS, scale=1.0).then_inc(S["aac"], 1)
                        for i in (a, bb):
                            s.wait_ge(S["aac"], 4 * p + 2)
                            s.activation(rsd[i % 4][:], lnv[i % 4][:], AF.Exp,
                                         bias=0.0, scale=-0.5).then_inc(S["aac"], 1)
                        for i in (a, bb):
                            s.wait_ge(S["avr"], 2 * p + 1 + (i - a))
                            if i >= 2:
                                s.wait_ge(S["atm"], 8 * (i - 1))
                            s.activation(xn_sb[i % 2][:], x_sb[i % 3][:], AF.Identity,
                                         bias=nmr[i % 4][:, 0:1],
                                         scale=rsd[i % 4][:, 0:1]).then_inc(S["aap"], 1)
                        for i in (a, bb):
                            for j4 in range(0, 8, 2):
                                g_id = 8 * i + j4
                                s.wait_ge(S["atm"], g_id + 1)
                                s.activation(ev_dst(i, j4), ps_t[g_id % 4][:], AF.Identity,
                                             bias=0.0, scale=1.0).then_inc(S["aev"], 1)

                @blk.tensor
                def _(t):
                    t.wait_ge(S["cst"], 16 * NCONST)
                    for cc in range(2):
                        if cc >= 1:
                            t.wait_ge(S["abvev"], cc)
                        t.matmul(ps_bv[:], onr[0:1, :], bvr[0:1, cc * 512:(cc + 1) * 512],
                                 start=True, stop=True).then_inc(S["abvm"], 1)
                    for i in range(16):
                        t.wait_ge(S["aap"], i + 1)
                        for j4 in range(8):
                            g_id = 8 * i + j4
                            if g_id >= 4:
                                if g_id % 2 == 0:
                                    t.wait_ge(S["aev"], g_id // 2 - 1)
                                else:
                                    t.wait_ge(S["aevv"], (g_id - 1) // 2 - 1)
                            for jj in range(4):
                                j = j4 * 4 + jj
                                mm = t.matmul(ps_t[g_id % 4][:, jj * 128:(jj + 1) * 128],
                                              xn_sb[i % 2][:, j * 128:(j + 1) * 128],
                                              idb[:], start=True, stop=True)
                                if jj == 3:
                                    mm.then_inc(S["atm"], 1)

    if tap == "xnt":
        with nc.Block() as blk:
            @blk.sync
            def _(sy):
                sy.wait_ge(S["aev"], 64)
                sy.wait_ge(S["aevv"], 64)
                sy.dma_start(out=taps["xnt"][:, :], in_=xnT[:]).then_inc(S["tapd"], 16)
                sy.wait_ge(S["tapd"], 16)

    if upto >= 2:
        with ExitStack() as st:
            wqk_f = [st.enter_context(nc.sbuf_tensor(f"wqk_f{i}", [128, 1024], F32)) for i in range(2)]
            wqk_b = [st.enter_context(nc.sbuf_tensor(f"wqk_b{i}", [128, 4096], BF16)) for i in range(2)]
            stg_k = [st.enter_context(nc.sbuf_tensor(f"stg_k{i}", [128, 512], BF16)) for i in range(2)]
            stg_q = [st.enter_context(nc.sbuf_tensor(f"stg_q{i}", [128, 512], BF16)) for i in range(2)]
            vq_f = [st.enter_context(nc.sbuf_tensor(f"vq_f{i}", [128, 2048], F32)) for i in range(2)]
            v_bf = st.enter_context(nc.sbuf_tensor("v_bf", [128, 16384], BF16))
            stg_v = [st.enter_context(nc.sbuf_tensor(f"stg_v{i}", [128, 512], BF16)) for i in range(2)]
            ps_qk = [st.enter_context(nc.psum_tensor(f"ps_qk{i}", [128, 512], F32)) for i in range(2)]
            ps_v = [st.enter_context(nc.psum_tensor(f"ps_v{i}", [128, 512], F32)) for i in range(2)]

            with nc.named_scope("phC"), nc.Block() as blk:
                @blk.sync
                def _(sy):
                    sy.wait_ge(S["aap"], 16)
                    sy.wait_ge(S["atm"], 128)
                    sy.wait_ge(S["avs"], 144)
                    sy.wait_ge(S["abvev"], 2)
                    for c in range(32):
                        k, qt = divmod(c, 4)
                        if c >= 2:
                            sy.wait_ge(S["cwb"], c - 1)
                        sy.dma_start(out=wqk_f[c % 2][:],
                                     in_=wqk[8 + k, :, qt * 1024:(qt + 1) * 1024]
                                     ).then_inc(S[f"cwf{c % 2}"], 16)
                    for c2 in range(8):
                        if c2 >= 2:
                            sy.wait_ge(S["cvb"], c2 - 1)
                        sy.dma_start(out=vq_f[c2 % 2][:],
                                     in_=wv[0, :, c2 * 2048:(c2 + 1) * 2048]
                                     ).then_inc(S[f"cvf{c2 % 2}"], 16)
                    for c in range(32, 42):
                        k, qt = divmod(c, 4)
                        sy.wait_ge(S["cwb"], c - 1)
                        sy.dma_start(out=wqk_f[c % 2][:],
                                     in_=wqk[k - 8, :, qt * 1024:(qt + 1) * 1024]
                                     ).then_inc(S[f"cwf{c % 2}"], 16)
                    for c2 in range(8, 16):
                        sy.wait_ge(S["cvb"], c2 - 1)
                        sy.dma_start(out=vq_f[c2 % 2][:],
                                     in_=wv[1, :, (c2 % 8) * 2048:(c2 % 8 + 1) * 2048]
                                     ).then_inc(S[f"cvf{c2 % 2}"], 16)
                    for c in range(42, 64):
                        k, qt = divmod(c, 4)
                        sy.wait_ge(S["cwb"], c - 1)
                        sy.dma_start(out=wqk_f[c % 2][:],
                                     in_=wqk[k - 8, :, qt * 1024:(qt + 1) * 1024]
                                     ).then_inc(S[f"cwf{c % 2}"], 16)

                @blk.gpsimd
                def _(g):
                    if exchange:
                        g.wait_ge(S["ckst0"], 16 * 16)
                        g.wait_ge(S["ckst1"], 16 * 16)
                        g.wait_ge(S["cvst0"], 16 * 16)
                        g.wait_ge(S["cvst1"], 16 * 16)
                        g.wait_ge(S["csv"], 1)
                        with g.register("rh") as rh, g.register("rko") as rko, \
                                g.register("rvo") as rvo, g.register("rfo") as rfo:
                            g.reg_alu(rh, g.to_reg(g.partition_id()), 1, ALU.bitwise_and)
                            g.reg_alu(rko, rh, DH * NTOK, ALU.mult)
                            g.reg_alu(rvo, rh, NTOK * DH, ALU.mult)
                            g.reg_alu(rfo, rh, 128, ALU.mult)
                            g.dma_start(out=ap2(k_sh, rko, [[NTOK, DH], [1, NTOK]]),
                                        in_=k_loc[:, :]).then_inc(S["cflw"], 16)
                            g.dma_start(out=ap2(v_sh, rvo, [[DH, NTOK], [1, DH]]),
                                        in_=v_loc[:, :]).then_inc(S["cflw"], 16)
                            g.wait_ge(S["cflw"], 32)
                            g.dma_start(out=ap2(fl_sh, rfo, [[1, 1], [1, 1]]),
                                        in_=snon[0:1, 1:2]).then_inc(S["cflw"], 16)
                            g.wait_ge(S["cflw"], 48)

                @blk.tensor
                def _(t):
                    t.wait_ge(S["aev"], 64)
                    t.wait_ge(S["aevv"], 64)

                    def qk_slab(k):
                        is_k = k < 8
                        kk = k if is_k else k - 8
                        smm = S["ckm"] if is_k else S["cqm"]
                        sev = S["ckev"] if is_k else S["cqev"]
                        t.wait_ge(S["cwb"], 4 * (k + 1))
                        for ncc in range(4):
                            g_id = 4 * kk + ncc
                            if g_id >= 2:
                                t.wait_ge(sev, g_id - 1)
                            for ct in range(32):
                                mm = t.matmul(ps_qk[g_id % 2][:],
                                              wqk_b[k % 2][:, ct * 128:(ct + 1) * 128],
                                              xnT[:, ct * 2048 + ncc * 512:
                                                  ct * 2048 + (ncc + 1) * 512],
                                              start=(ct == 0), stop=(ct == 31))
                                if ct == 31:
                                    mm.then_inc(smm, 1)

                    for k in range(8):
                        qk_slab(k)
                    for vs in range(2):
                        t.wait_ge(S["cvb"], 8 * (vs + 1))
                        for nt in range(16):
                            gv = 16 * vs + nt
                            if gv >= 2:
                                t.wait_ge(S["cvev"], gv - 1)
                            for ct in range(32):
                                mm = t.matmul(ps_v[gv % 2][:],
                                              xnT[:, ct * 2048 + nt * 128:
                                                  ct * 2048 + nt * 128 + 128],
                                              v_bf[:, ct * 512:(ct + 1) * 512],
                                              start=(ct == 0), stop=(ct == 31))
                                if ct == 31:
                                    mm.then_inc(S["cvm"], 1)
                    for k in range(8, 16):
                        qk_slab(k)

                @blk.vector
                def _(v):
                    def wcast(c):
                        k, qt = divmod(c, 4)
                        v.wait_ge(S[f"cwf{c % 2}"], 16 * (c // 2 + 1))
                        if k >= 2 and qt == 0:
                            if k - 2 < 8:
                                v.wait_ge(S["ckm"], 4 * (k - 1))
                            else:
                                v.wait_ge(S["cqm"], 4 * (k - 9))
                        v.tensor_copy(wqk_b[k % 2][:, qt * 1024:(qt + 1) * 1024],
                                      wqk_f[c % 2][:]).then_inc(S["cwb"], 1)

                    for c in range(32):
                        wcast(c)
                    for c2 in range(8):
                        v.wait_ge(S[f"cvf{c2 % 2}"], 16 * (c2 // 2 + 1))
                        v.tensor_copy(v_bf[:, c2 * 2048:(c2 + 1) * 2048],
                                      vq_f[c2 % 2][:]).then_inc(S["cvb"], 1)
                    for c in range(32, 40):
                        wcast(c)
                    for gv in range(16):
                        v.wait_ge(S["cvm"], gv + 1)
                        if gv >= 2:
                            v.wait_ge(S[f"cvst{gv % 2}"], 16 * (gv // 2))
                        v.scalar_tensor_tensor(stg_v[gv % 2][:], ps_v[gv % 2][:], 1.0,
                                               bv_b[:, 0:512],
                                               ALU.mult, ALU.add).then_inc(S["cvev"], 1)
                    for c2 in range(8, 16):
                        v.wait_ge(S[f"cvf{c2 % 2}"], 16 * (c2 // 2 + 1))
                        if c2 == 8:
                            v.wait_ge(S["cvm"], 16)
                        v.tensor_copy(v_bf[:, (c2 - 8) * 2048:(c2 - 7) * 2048],
                                      vq_f[c2 % 2][:]).then_inc(S["cvb"], 1)
                    for gv in range(16, 32):
                        v.wait_ge(S["cvm"], gv + 1)
                        v.wait_ge(S[f"cvst{gv % 2}"], 16 * (gv // 2))
                        v.scalar_tensor_tensor(stg_v[gv % 2][:], ps_v[gv % 2][:], 1.0,
                                               bv_b[:, 512:1024],
                                               ALU.mult, ALU.add).then_inc(S["cvev"], 1)
                    for c in range(40, 64):
                        wcast(c)

                @blk.scalar
                def _(s):
                    def qk_out(k):
                        is_k = k < 8
                        kk = k if is_k else k - 8
                        smm = S["ckm"] if is_k else S["cqm"]
                        sev = S["ckev"] if is_k else S["cqev"]
                        sstp = [S["ckst0"], S["ckst1"]] if is_k else [S["cqst0"], S["cqst1"]]
                        stg = stg_k if is_k else stg_q
                        bcol = 8 + kk if is_k else kk
                        dst = k_loc if is_k else q_dr
                        for ncc in range(4):
                            g_id = 4 * kk + ncc
                            s.wait_ge(smm, g_id + 1)
                            if g_id >= 2:
                                s.wait_ge(sstp[g_id % 2], 16 * (g_id // 2))
                            s.activation(stg[g_id % 2][:], ps_qk[g_id % 2][:],
                                         AF.Identity, bias=bqk[:, bcol:bcol + 1],
                                         scale=1.0).then_inc(sev, 1)
                            s.wait_ge(sev, 4 * kk + ncc + 1)
                            dstride = KT if dst is k_loc else NTOK
                            s.dma_start(out=ap2(dst, kk * 128 * dstride + ncc * 512,
                                                [[dstride, 128], [1, 512]]),
                                        in_=stg[g_id % 2][:]).then_inc(sstp[g_id % 2], 16)

                    for k in range(8):
                        qk_out(k)
                    for vs in range(2):
                        for nt in range(16):
                            gv = 16 * vs + nt
                            s.wait_ge(S["cvev"], gv + 1)
                            s.dma_start(out=ap2(v_loc, nt * 128 * DH + vs * 512,
                                                [[DH, 128], [1, 512]]),
                                        in_=stg_v[gv % 2][:]).then_inc(S[f"cvst{gv % 2}"], 16)
                    for k in range(8, 16):
                        qk_out(k)

    if tap == "kvq":
        with nc.Block() as blk:
            @blk.gpsimd
            def _(g):
                with g.register("rn_d") as rn, g.register("rf0_d") as rf0, \
                        g.register("rf1_d") as rf1, g.register("rt0_d") as rt0, \
                        g.register("rt1_d") as rt1, g.register("rc_d") as rc:
                    g.wait_ge(S["cflw"], 48)
                    g.reg_load(rn, snon[0:1, 0:1])
                    g.reg_mov(rc, 1)
                    with g.While(rc):
                        g.load(rf0, fl_sh[0:1, 0:1])
                        g.load(rf1, fl_sh[1:2, 0:1])
                        g.reg_alu(rt0, rf0, rn, ALU.not_equal)
                        g.reg_alu(rt1, rf1, rn, ALU.not_equal)
                        g.reg_alu(rc, rt0, rt1, ALU.bitwise_or)
                g.sem_inc(S["dkv"], 1)

            @blk.sync
            def _(sy):
                sy.wait_ge(S["dkv"], 1)
                sy.wait_ge(S["cqst0"], 16 * 16)
                sy.wait_ge(S["cqst1"], 16 * 16)
                sy.dma_start(out=taps["k"][:, :],
                             in_=ap2(k_sh, 0, [[NTOK, 2048], [1, NTOK]])).then_inc(S["tapd"], 16)
                sy.dma_start(out=taps["v"][:, :],
                             in_=ap2(v_sh, 0, [[DH, 4096], [1, DH]])).then_inc(S["tapd"], 16)
                sy.dma_start(out=taps["q"][:, :],
                             in_=q_dr[:, :]).then_inc(S["tapd"], 16)
                sy.wait_ge(S["tapd"], 48)

    if upto >= 4:
        with ExitStack() as st:
            wr_f = st.enter_context(nc.sbuf_tensor("wr_f", [128, 2048], F32))
            wr_b = st.enter_context(nc.sbuf_tensor("wr_b", [128, 2048], BF16))
            s_rT = st.enter_context(nc.sbuf_tensor("s_rT", [64, 2048], F32))
            s_r = st.enter_context(nc.sbuf_tensor("s_r", [128, 1024], F32))
            r1 = st.enter_context(nc.sbuf_tensor("r1", [128, 8], F32))
            r2 = st.enter_context(nc.sbuf_tensor("r2", [128, 16], F32))
            t2b = st.enter_context(nc.sbuf_tensor("t2b", [128, 64], F32))
            nm1 = st.enter_context(nc.sbuf_tensor("nm1", [128, 2], F32))
            e_t = st.enter_context(nc.sbuf_tensor("e_t", [128, 128], F32))
            mask = st.enter_context(nc.sbuf_tensor("maskb", [128, 64], F32))
            em_t = st.enter_context(nc.sbuf_tensor("em_t", [128, 1024], F32))
            zc = st.enter_context(nc.sbuf_tensor("zc", [128, 16], F32))
            zic = st.enter_context(nc.sbuf_tensor("zic", [128, 16], F32))
            ps_r = [st.enter_context(nc.psum_tensor(f"ps_r{i}", [64, 512], F32)) for i in range(2)]
            ps_rt = [st.enter_context(nc.psum_tensor(f"ps_rt{i}", [128, 64], F32)) for i in range(2)]

            with nc.named_scope("phB1a"), nc.Block() as blk:
                @blk.sync
                def _(sy):
                    sy.wait_ge(S["ckst0"], 16 * 16)
                    sy.wait_ge(S["ckst1"], 16 * 16)
                    sy.wait_ge(S["cqst0"], 16 * 16)
                    sy.wait_ge(S["cqst1"], 16 * 16)
                    sy.wait_ge(S["cvst0"], 16 * 16)
                    sy.wait_ge(S["cvst1"], 16 * 16)
                    sy.dma_start(out=wr_f[:], in_=wr[:, :]).then_inc(S["brf"], 16)

                @blk.gpsimd
                def _(g):
                    g.wait_ge(S["brf"], 16)
                    g.tensor_copy(wr_b[:], wr_f[:]).then_inc(S["brb"], 1)
                    for nt in range(16):
                        d = nt % 2
                        sl = s_r[:, nt * 64:(nt + 1) * 64]
                        g.wait_ge(S["bdv"], 4 * nt + 3)
                        if nt >= 1:
                            g.wait_ge(S["bpl"], 2 * nt)
                        g.tensor_scalar(mask[:], sl, r2[:, 8 * d + 7:8 * d + 8], None,
                                        ALU.is_ge, ALU.bypass).then_inc(S["bpl"], 1)
                        g.wait_ge(S["bac"], nt + 1)
                        g.wait_ge(S["bpl"], 2 * nt + 1)
                        g.tensor_tensor(em_t[:, 64 * nt:64 * nt + 64],
                                        e_t[:, 64 * d:64 * d + 64], mask[:],
                                        ALU.mult).then_inc(S["bpl"], 1)
                    for nt in range(16):
                        g.wait_ge(S["bdv"], 64 + 2 * nt + 2)
                        g.tensor_scalar(g_tm[:, nt * 64:(nt + 1) * 64],
                                        em_t[:, 64 * nt:64 * nt + 64], zic[:, nt:nt + 1],
                                        2.0, ALU.mult, ALU.mult).then_inc(S["bpl"], 1)

                @blk.tensor
                def _(t):
                    t.wait_ge(S["brb"], 1)
                    for ncc in range(4):
                        if ncc >= 2:
                            t.wait_ge(S["brev"], ncc - 1)
                        for ct in range(32):
                            mm = t.matmul(ps_r[ncc % 2][:],
                                          wr_b[:, ct * 64:(ct + 1) * 64],
                                          xnT[:, ct * 2048 + ncc * 512:
                                              ct * 2048 + (ncc + 1) * 512],
                                          start=(ct == 0), stop=(ct == 31))
                            if ct == 31:
                                mm.then_inc(S["brm"], 1)
                    t.wait_ge(S["brev"], 4)
                    for nt in range(16):
                        if nt >= 2:
                            t.wait_ge(S["brtev"], nt - 1)
                        t.matmul(ps_rt[nt % 2][:], s_rT[0:64, nt * 128:(nt + 1) * 128],
                                 idf[0:64, 0:64], start=True, stop=True).then_inc(S["brtm"], 1)

                @blk.vector
                def _(v):
                    for ncc in range(4):
                        v.wait_ge(S["brm"], ncc + 1)
                        v.tensor_copy(s_rT[0:64, ncc * 512:(ncc + 1) * 512],
                                      ps_r[ncc % 2][0:64, :]).then_inc(S["brev"], 1)
                    for nt in range(16):
                        v.wait_ge(S["brtm"], nt + 1)
                        v.tensor_copy(s_r[:, nt * 64:(nt + 1) * 64],
                                      ps_rt[nt % 2][:, 0:64]).then_inc(S["brtev"], 1)
                    for nt in range(16):
                        b = 4 * nt
                        d = nt % 2
                        sl = s_r[:, nt * 64:(nt + 1) * 64]
                        v.wait_ge(S["brtev"], nt + 1)
                        if nt >= 1:
                            v.wait_ge(S["bdv"], 4 * nt)
                        v.max(r1[:], sl).then_inc(S["bdv"], 1)
                        v.wait_ge(S["bdv"], b + 1)
                        v.match_replace(t2b[:], r1[:], sl, -1e30).then_inc(S["bdv"], 1)
                        v.wait_ge(S["bdv"], b + 2)
                        if nt >= 2:
                            v.wait_ge(S["bpl"], 2 * nt - 3)
                        v.max(r2[:, 8 * d:8 * d + 8], t2b[:]).then_inc(S["bdv"], 1)
                        if nt >= 2:
                            v.wait_ge(S["bac"], nt - 1)
                        v.tensor_scalar(nm1[:, d:d + 1], r1[:, 0:1], -1.0, None,
                                        ALU.mult, ALU.bypass).then_inc(S["bdv"], 1)
                    for nt in range(16):
                        v.wait_ge(S["bpl"], 2 * nt + 2)
                        v.tensor_reduce(zc[:, nt:nt + 1], em_t[:, 64 * nt:64 * nt + 64],
                                        AX.X, ALU.add).then_inc(S["bdv"], 1)
                        v.wait_ge(S["bdv"], 64 + 2 * nt + 1)
                        v.reciprocal(zic[:, nt:nt + 1],
                                     zc[:, nt:nt + 1]).then_inc(S["bdv"], 1)

                @blk.scalar
                def _(s):
                    for nt in range(16):
                        d = nt % 2
                        s.wait_ge(S["bdv"], 4 * nt + 4)
                        if nt >= 2:
                            s.wait_ge(S["bpl"], 2 * nt - 2)
                        s.activation(e_t[:, 64 * d:64 * d + 64],
                                     s_r[:, nt * 64:(nt + 1) * 64], AF.Exp,
                                     bias=nm1[:, d:d + 1], scale=1.0).then_inc(S["bac"], 1)

    if tap == "rg":
        with nc.Block() as blk:
            @blk.sync
            def _(sy):
                sy.wait_ge(S["bdv"], 96)
                sy.wait_ge(S["bpl"], 48)
                sy.dma_start(out=taps["sr"][:, :], in_=s_r[:]).then_inc(S["tapd"], 16)
                sy.dma_start(out=taps["g"][:, :], in_=g_tm[:]).then_inc(S["tapd"], 16)
                sy.wait_ge(S["tapd"], 32)

    if upto >= 5:
        with ExitStack() as st:
            g_exp = [st.enter_context(nc.sbuf_tensor(f"g_exp{i}", [128, 2048], BF16)) for i in range(2)]
            at_f = [st.enter_context(nc.sbuf_tensor(f"at_f{i}", [128, 1024], F32)) for i in range(2)]
            at_b = [st.enter_context(nc.sbuf_tensor(f"at_b{i}", [128, 4096], BF16)) for i in range(2)]
            stg_h = [st.enter_context(nc.sbuf_tensor(f"stg_h{i}", [128, 512], BF16)) for i in range(2)]
            ps_g = [st.enter_context(nc.psum_tensor(f"ps_g{i}", [64, 128], F32)) for i in range(2)]
            ps_ge = [st.enter_context(nc.psum_tensor(f"ps_ge{i}", [128, 512], F32)) for i in range(2)]
            ps_h = [st.enter_context(nc.psum_tensor(f"ps_h{i}", [128, 512], F32)) for i in range(4)]

            with nc.named_scope("phB1b"), nc.Block() as blk:
                @blk.sync
                def _(sy):
                    sy.wait_ge(S["bdv"], 96)
                    sy.wait_ge(S["bpl"], 48)
                    sy.wait_ge(S["bac"], 16)
                    sy.wait_ge(S["brm"], 4)
                    sy.wait_ge(S["brtm"], 16)
                    for c in range(32):
                        ert, qt = divmod(c, 4)
                        if c >= 2:
                            sy.wait_ge(S["bab"], c - 1)
                        sy.dma_start(out=at_f[c % 2][:],
                                     in_=at_w[ert, :, qt * 1024:(qt + 1) * 1024]).then_inc(S[f"baf{c % 2}"], 16)

                @blk.gpsimd
                def _(g):
                    for c in range(32):
                        ert, qt = divmod(c, 4)
                        g.wait_ge(S[f"baf{c % 2}"], 16 * (c // 2 + 1))
                        if ert >= 2 and qt == 0:
                            g.wait_ge(S["bhm"], 4 * (ert - 1))
                        g.tensor_copy(at_b[ert % 2][:, qt * 1024:(qt + 1) * 1024],
                                      at_f[c % 2][:]).then_inc(S["bab"], 1)

                @blk.tensor
                def _(t):
                    def h_grp(gh):
                        ert, ncc = divmod(gh, 4)
                        if ncc == 0:
                            t.wait_ge(S["bab"], 4 * (ert + 1))
                        if gh >= 4:
                            t.wait_ge(S["bhev"], gh - 3)
                        for ct in range(32):
                            mm = t.matmul(ps_h[gh % 4][:],
                                          at_b[ert % 2][:, ct * 128:(ct + 1) * 128],
                                          xnT[:, ct * 2048 + ncc * 512:
                                              ct * 2048 + (ncc + 1) * 512],
                                          start=(ct == 0), stop=(ct == 31))
                            if ct == 31:
                                mm.then_inc(S["bhm"], 1)

                    for gh in range(4):
                        h_grp(gh)
                    for nt in range(16):
                        t.wait_ge(S["bpl"], 33 + nt)
                        if nt >= 2:
                            t.wait_ge(S["bgev"], nt - 1)
                        t.matmul(ps_g[nt % 2][:], g_tm[:, nt * 64:(nt + 1) * 64],
                                 idb[:], start=True, stop=True).then_inc(S["bgm"], 1)
                    t.wait_ge(S["bgev"], 16)
                    for ert in range(8):
                        for ncc in range(4):
                            ge = 4 * ert + ncc
                            if ge >= 2:
                                t.wait_ge(S["bgeev"], ge - 1)
                            t.matmul(ps_ge[ge % 2][:], emat[0:64, ert * 128:(ert + 1) * 128],
                                     g_Ts[0:64, ncc * 512:(ncc + 1) * 512],
                                     start=True, stop=True).then_inc(S["bgem"], 1)
                        if ert < 7:
                            for gh in range(4 * ert + 4, 4 * ert + 8):
                                h_grp(gh)

                @blk.vector
                def _(v):
                    for nt in range(16):
                        v.wait_ge(S["bgm"], nt + 1)
                        v.tensor_copy(g_Ts[0:64, nt * 128:(nt + 1) * 128],
                                      ps_g[nt % 2][0:64, :]).then_inc(S["bgev"], 1)
                    for ert in range(8):
                        for ncc in range(4):
                            ge = 4 * ert + ncc
                            v.wait_ge(S["bgem"], ge + 1)
                            if ert >= 2 and ncc == 0:
                                v.wait_ge(S["bhev"], 4 * (ert - 1))
                            v.tensor_copy(g_exp[ert % 2][:, ncc * 512:(ncc + 1) * 512],
                                          ps_ge[ge % 2][:]).then_inc(S["bgeev"], 1)
                        for ncc in range(4):
                            gh = 4 * ert + ncc
                            v.wait_ge(S["bhm"], gh + 1)
                            if gh >= 2:
                                v.wait_ge(S[f"bhst{gh % 2}"], 16 * (gh // 2))
                            v.scalar_tensor_tensor(stg_h[gh % 2][:], ps_h[gh % 4][:], 1.0,
                                                   g_exp[ert % 2][:, ncc * 512:(ncc + 1) * 512],
                                                   ALU.mult, ALU.mult).then_inc(S["bhev"], 1)

                @blk.scalar
                def _(s):
                    for gh in range(32):
                        ert, ncc = divmod(gh, 4)
                        s.wait_ge(S["bhev"], gh + 1)
                        s.dma_start(out=ap2(hg_dr, ert * 128 * NTOK + ncc * 512,
                                            [[NTOK, 128], [1, 512]]),
                                    in_=stg_h[gh % 2][:]).then_inc(S[f"bhst{gh % 2}"], 16)

    if tap == "hg":
        with nc.Block() as blk:
            @blk.sync
            def _(sy):
                sy.wait_ge(S["bhst0"], 16 * 16)
                sy.wait_ge(S["bhst1"], 16 * 16)
                sy.dma_start(out=taps["hg"][:, :], in_=hg_dr[:, :]).then_inc(S["tapd"], 16)
                sy.wait_ge(S["tapd"], 16)

    if upto >= 6 and not exchange:
        with ExitStack() as st:
            x_sb = [st.enter_context(nc.sbuf_tensor(f"px_sb{i}", [128, 4096], F32)) for i in range(3)]
            xn_sb = [st.enter_context(nc.sbuf_tensor(f"pxn_sb{i}", [128, 4096], BF16)) for i in range(2)]
            stats = [st.enter_context(nc.sbuf_tensor(f"pstats{i}", [128, 48], F32)) for i in range(2)]
            mv = [st.enter_context(nc.sbuf_tensor(f"pmv{i}", [128, 2], F32)) for i in range(4)]
            rsd = [st.enter_context(nc.sbuf_tensor(f"prsd{i}", [128, 1], F32)) for i in range(4)]
            nmr = [st.enter_context(nc.sbuf_tensor(f"pnmr{i}", [128, 1], F32)) for i in range(4)]
            lnv = [st.enter_context(nc.sbuf_tensor(f"plnv{i}", [128, 1], F32)) for i in range(4)]
            ps_t = [st.enter_context(nc.psum_tensor(f"pps_t{i}", [128, 512], F32)) for i in range(4)]

            def ev_dst(i, j4):
                return ap2(xnT, (j4 * 4) * 2048 + i * 128,
                           [[65536, 128], [2048, 4], [1, 128]])

            with nc.named_scope("phAp"), nc.Block() as blk:
                @blk.sync
                def _(sy):
                    sy.wait_ge(S["bhst0"], 16 * 16)
                    sy.wait_ge(S["bhst1"], 16 * 16)
                    sy.wait_ge(S["bhm"], 32)
                    sy.wait_ge(S["bgem"], 32)
                    sy.wait_ge(S["bhev"], 32)
                    for i in range(16):
                        if i >= 3:
                            sy.wait_ge(S["aap"], 16 + i - 2)
                        sy.dma_start(out=x_sb[i % 3][:],
                                     in_=xp_in[i * 128:(i + 1) * 128, :]
                                     ).then_inc(S[f"ax{i % 3}"], 16)

                @blk.vector
                def _(v):
                    for p in range(8):
                        a, bb = 2 * p, 2 * p + 1
                        for i in (a, bb):
                            v.wait_ge(S[f"ax{i % 3}"], 16 * ((6, 5, 5)[i % 3] + i // 3 + 1))
                            if p >= 1:
                                v.wait_ge(S["avs"], 144 + 18 * (p - 1) + 17 + (i - 2 * p))
                            if p >= 2:
                                v.wait_ge(S["aac"], 32 + 4 * p - 6)
                            for j in range(8):
                                v.bn_stats(stats[i % 2][:, 6 * j:6 * j + 6],
                                           x_sb[i % 3][:, 512 * j:512 * (j + 1)]
                                           ).then_inc(S["avs"], 1)
                        v.wait_ge(S["avs"], 144 + 18 * p + 16)
                        for i in (a, bb):
                            v.bn_aggr(mv[i % 4][:], stats[i % 2][:, 0:48]).then_inc(S["avs"], 1)
                        for i in (a, bb):
                            v.wait_ge(S["aac"], 32 + 4 * p + (3 if i == a else 4))
                            v.tensor_scalar(nmr[i % 4][:], mv[i % 4][:, 0:1],
                                            rsd[i % 4][:, 0:1], -1.0,
                                            ALU.mult, ALU.mult).then_inc(S["avr"], 1)
                        for i in (a, bb):
                            for j4 in range(1, 8, 2):
                                g_id = 8 * i + j4
                                v.wait_ge(S["atm"], 128 + g_id + 1)
                                v.tensor_copy(ev_dst(i, j4),
                                              ps_t[g_id % 4][:]).then_inc(S["aevv"], 1)

                @blk.scalar
                def _(s):
                    for p in range(8):
                        a, bb = 2 * p, 2 * p + 1
                        for i in (a, bb):
                            s.wait_ge(S["avs"], 144 + 18 * p + 17 + (i - a))
                            s.activation(lnv[i % 4][:], mv[i % 4][:, 1:2], AF.Ln,
                                         bias=EPS, scale=1.0).then_inc(S["aac"], 1)
                        for i in (a, bb):
                            s.wait_ge(S["aac"], 32 + 4 * p + 2)
                            s.activation(rsd[i % 4][:], lnv[i % 4][:], AF.Exp,
                                         bias=0.0, scale=-0.5).then_inc(S["aac"], 1)
                        for i in (a, bb):
                            s.wait_ge(S["avr"], 16 + 2 * p + 1 + (i - a))
                            if i >= 2:
                                s.wait_ge(S["atm"], 128 + 8 * (i - 1))
                            s.activation(xn_sb[i % 2][:], x_sb[i % 3][:], AF.Identity,
                                         bias=nmr[i % 4][:, 0:1],
                                         scale=rsd[i % 4][:, 0:1]).then_inc(S["aap"], 1)
                        for i in (a, bb):
                            for j4 in range(0, 8, 2):
                                g_id = 8 * i + j4
                                s.wait_ge(S["atm"], 128 + g_id + 1)
                                s.activation(ev_dst(i, j4), ps_t[g_id % 4][:], AF.Identity,
                                             bias=0.0, scale=1.0).then_inc(S["aev"], 1)

                @blk.tensor
                def _(t):
                    t.wait_ge(S["bhev"], 32)
                    for i in range(16):
                        t.wait_ge(S["aap"], 16 + i + 1)
                        for j4 in range(8):
                            g_id = 8 * i + j4
                            if g_id >= 4:
                                if g_id % 2 == 0:
                                    t.wait_ge(S["aev"], 64 + g_id // 2 - 1)
                                else:
                                    t.wait_ge(S["aevv"], 64 + (g_id - 1) // 2 - 1)
                            for jj in range(4):
                                j = j4 * 4 + jj
                                mm = t.matmul(ps_t[g_id % 4][:, jj * 128:(jj + 1) * 128],
                                              xn_sb[i % 2][:, j * 128:(j + 1) * 128],
                                              idb[:], start=True, stop=True)
                                if jj == 3:
                                    mm.then_inc(S["atm"], 1)

    if upto >= 6 and not exchange:
        with ExitStack() as st:
            wqk_f = [st.enter_context(nc.sbuf_tensor(f"pwqk_f{i}", [128, 1024], F32)) for i in range(2)]
            wqk_b = [st.enter_context(nc.sbuf_tensor(f"pwqk_b{i}", [128, 4096], BF16)) for i in range(2)]
            stg_k = [st.enter_context(nc.sbuf_tensor(f"pstg_k{i}", [128, 512], BF16)) for i in range(2)]
            vq_f = [st.enter_context(nc.sbuf_tensor(f"pvq_f{i}", [128, 2048], F32)) for i in range(2)]
            v_bf = st.enter_context(nc.sbuf_tensor("pv_bf", [128, 16384], BF16))
            stg_v = [st.enter_context(nc.sbuf_tensor(f"pstg_v{i}", [128, 512], BF16)) for i in range(2)]
            ps_qk = [st.enter_context(nc.psum_tensor(f"pps_qk{i}", [128, 512], F32)) for i in range(2)]
            ps_v = [st.enter_context(nc.psum_tensor(f"pps_v{i}", [128, 512], F32)) for i in range(2)]

            with nc.named_scope("phCp"), nc.Block() as blk:
                @blk.sync
                def _(sy):
                    sy.wait_ge(S["aap"], 32)
                    sy.wait_ge(S["atm"], 256)
                    sy.wait_ge(S["avs"], 288)
                    for c in range(32):
                        k, qt = divmod(c, 4)
                        if c >= 2:
                            sy.wait_ge(S["cwb"], 64 + c - 1)
                        sy.dma_start(out=wqk_f[c % 2][:],
                                     in_=wqk[8 + k, :, qt * 1024:(qt + 1) * 1024]
                                     ).then_inc(S[f"cwf{c % 2}"], 16)
                    for c2 in range(16):
                        sy.wait_ge(S["cvb"], 16 + max(0, c2 - 1))
                        sy.dma_start(out=vq_f[c2 % 2][:],
                                     in_=wv[c2 // 8, :, (c2 % 8) * 2048:(c2 % 8 + 1) * 2048]
                                     ).then_inc(S[f"cvf{c2 % 2}"], 16)

                @blk.tensor
                def _(t):
                    t.wait_ge(S["aev"], 128)
                    t.wait_ge(S["aevv"], 128)
                    for k in range(8):
                        t.wait_ge(S["cwb"], 64 + 4 * (k + 1))
                        for ncc in range(4):
                            g_id = 4 * k + ncc
                            if g_id >= 2:
                                t.wait_ge(S["ckev"], 32 + g_id - 1)
                            for ct in range(32):
                                mm = t.matmul(ps_qk[g_id % 2][:],
                                              wqk_b[k % 2][:, ct * 128:(ct + 1) * 128],
                                              xnT[:, ct * 2048 + ncc * 512:
                                                  ct * 2048 + (ncc + 1) * 512],
                                              start=(ct == 0), stop=(ct == 31))
                                if ct == 31:
                                    mm.then_inc(S["ckm"], 1)
                    for vs in range(2):
                        t.wait_ge(S["cvb"], 16 + 8 * (vs + 1))
                        for nt in range(16):
                            gv = 16 * vs + nt
                            if gv >= 2:
                                t.wait_ge(S["cvev"], 32 + gv - 1)
                            for ct in range(32):
                                mm = t.matmul(ps_v[gv % 2][:],
                                              xnT[:, ct * 2048 + nt * 128:
                                                  ct * 2048 + nt * 128 + 128],
                                              v_bf[:, ct * 512:(ct + 1) * 512],
                                              start=(ct == 0), stop=(ct == 31))
                                if ct == 31:
                                    mm.then_inc(S["cvm"], 1)

                @blk.vector
                def _(v):
                    for c in range(32):
                        k, qt = divmod(c, 4)
                        v.wait_ge(S[f"cwf{c % 2}"], 16 * (32 + c // 2 + 1))
                        if k >= 2 and qt == 0:
                            v.wait_ge(S["ckm"], 32 + 4 * (k - 1))
                        v.tensor_copy(wqk_b[k % 2][:, qt * 1024:(qt + 1) * 1024],
                                      wqk_f[c % 2][:]).then_inc(S["cwb"], 1)
                    for c2 in range(8):
                        v.wait_ge(S[f"cvf{c2 % 2}"], 16 * (8 + c2 // 2 + 1))
                        v.tensor_copy(v_bf[:, c2 * 2048:(c2 + 1) * 2048],
                                      vq_f[c2 % 2][:]).then_inc(S["cvb"], 1)
                    for gv in range(16):
                        v.wait_ge(S["cvm"], 32 + gv + 1)
                        if gv >= 2:
                            v.wait_ge(S[f"cvst{gv % 2}"], 16 * (16 + gv // 2))
                        v.scalar_tensor_tensor(stg_v[gv % 2][:], ps_v[gv % 2][:], 1.0,
                                               bv_b[:, 0:512],
                                               ALU.mult, ALU.add).then_inc(S["cvev"], 1)
                    for c2 in range(8, 16):
                        v.wait_ge(S[f"cvf{c2 % 2}"], 16 * (8 + c2 // 2 + 1))
                        if c2 == 8:
                            v.wait_ge(S["cvm"], 48)
                        v.tensor_copy(v_bf[:, (c2 - 8) * 2048:(c2 - 7) * 2048],
                                      vq_f[c2 % 2][:]).then_inc(S["cvb"], 1)
                    for gv in range(16, 32):
                        v.wait_ge(S["cvm"], 32 + gv + 1)
                        v.wait_ge(S[f"cvst{gv % 2}"], 16 * (16 + gv // 2))
                        v.scalar_tensor_tensor(stg_v[gv % 2][:], ps_v[gv % 2][:], 1.0,
                                               bv_b[:, 512:1024],
                                               ALU.mult, ALU.add).then_inc(S["cvev"], 1)

                @blk.scalar
                def _(s):
                    for k in range(8):
                        for ncc in range(4):
                            g_id = 4 * k + ncc
                            s.wait_ge(S["ckm"], 32 + g_id + 1)
                            if g_id >= 2:
                                s.wait_ge(S[f"ckst{g_id % 2}"], 16 * (16 + g_id // 2))
                            s.activation(stg_k[g_id % 2][:], ps_qk[g_id % 2][:],
                                         AF.Identity, bias=bqk[:, 8 + k:8 + k + 1],
                                         scale=1.0).then_inc(S["ckev"], 1)
                            s.wait_ge(S["ckev"], 32 + g_id + 1)
                            s.dma_start(out=ap2(k_loc, k * 128 * KT + NTOK + ncc * 512,
                                                [[KT, 128], [1, 512]]),
                                        in_=stg_k[g_id % 2][:]).then_inc(S[f"ckst{g_id % 2}"], 16)
                    for vs in range(2):
                        for nt in range(16):
                            gv = 16 * vs + nt
                            s.wait_ge(S["cvev"], 32 + gv + 1)
                            s.dma_start(out=ap2(v_loc, (NTOK + nt * 128) * DH + vs * 512,
                                                [[DH, 128], [1, 512]]),
                                        in_=stg_v[gv % 2][:]).then_inc(S[f"cvst{gv % 2}"], 16)

    s1.close()

    s2 = ExitStack()
    if upto >= 6:
        qsb = s2.enter_context(nc.sbuf_tensor("qsb", [128, 16384], BF16))
        if not exchange:
            ksb = s2.enter_context(nc.sbuf_tensor("ksb", [128, 32768], BF16))
        with ExitStack() as st:
            hgs = st.enter_context(nc.sbuf_tensor("hgs", [128, 16384], BF16))
            bo_b = st.enter_context(nc.sbuf_tensor("bo_b", [128, 4096], F32))
            bor = st.enter_context(nc.sbuf_tensor("bor_s", [1, 4096], F32))
            bf_f = [st.enter_context(nc.sbuf_tensor(f"bf_f{i}", [128, 1024], F32)) for i in range(2)]
            bf_b = [st.enter_context(nc.sbuf_tensor(f"bf_b{i}", [128, 4096], BF16)) for i in range(2)]
            stg_f = [st.enter_context(nc.sbuf_tensor(f"stg_f{i}", [128, 512], BF16)) for i in range(2)]
            ps_f = [st.enter_context(nc.psum_tensor(f"ps_f{i}", [128, 512], F32)) for i in range(2)]
            ps_bo = st.enter_context(nc.psum_tensor("ps_bo", [128, 512], F32))

            with nc.named_scope("phB2"), nc.Block() as blk:
                @blk.sync
                def _(sy):
                    sy.wait_ge(S["bhst0"], 16 * 16)
                    sy.wait_ge(S["bhst1"], 16 * 16)
                    sy.wait_ge(S["bhm"], 32)
                    sy.wait_ge(S["bgem"], 32)
                    if not exchange:
                        sy.wait_ge(S["aap"], 32)
                        sy.wait_ge(S["atm"], 256)
                        sy.wait_ge(S["avs"], 288)
                        sy.wait_ge(S["ckst0"], 16 * 32)
                        sy.wait_ge(S["ckst1"], 16 * 32)
                        sy.wait_ge(S["cvst0"], 16 * 32)
                        sy.wait_ge(S["cvst1"], 16 * 32)
                    sy.dma_start(out=bor[:], in_=bo_in[:, :]).then_inc(S["bbor"], 16)
                    sy.dma_start(out=ap2(hgs, 0, [[16384, 128], [2048, 8], [1, 2048]]),
                                 in_=ap2(hg_dr, 0, [[NTOK, 128], [128 * NTOK, 8], [1, NTOK]])
                                 ).then_inc(S["bhg"], 16)
                    for c in range(32):
                        cs, qt = divmod(c, 4)
                        if c == 8:
                            sy.wait_ge(S["cqst0"], 16 * 16)
                            sy.wait_ge(S["cqst1"], 16 * 16)
                            sy.dma_start(out=ap2(qsb, 0, [[16384, 128], [2048, 8], [1, 2048]]),
                                         in_=ap2(q_dr, 0,
                                                 [[NTOK, 128], [128 * NTOK, 8], [1, 2048]])
                                         ).then_inc(S["dq"], 16)
                            if not exchange:
                                sy.wait_ge(S["ckst0"], 16 * 32)
                                sy.wait_ge(S["ckst1"], 16 * 32)
                                for dt in range(8):
                                    sy.dma_start(
                                        out=ap2(ksb, dt * 4096, [[32768, 128], [1, 4096]]),
                                        in_=ap2(k_loc, dt * 128 * KT,
                                                [[KT, 128], [1, 4096]])).then_inc(S["dk"], 16)
                        if c >= 2:
                            sy.wait_ge(S["bfb"], c - 1)
                        sy.dma_start(out=bf_f[c % 2][:],
                                     in_=bf_w[cs, :, qt * 1024:(qt + 1) * 1024]).then_inc(S[f"bff{c % 2}"], 16)


                @blk.tensor
                def _(t):
                    t.wait_ge(S["bbor"], 16)
                    for cc in range(8):
                        if cc >= 1:
                            t.wait_ge(S["bboev"], cc)
                        t.matmul(ps_bo[:], onr[0:1, :], bor[0:1, cc * 512:(cc + 1) * 512],
                                 start=True, stop=True).then_inc(S["bbom"], 1)
                    t.wait_ge(S["bhg"], 16)
                    for cs in range(8):
                        t.wait_ge(S["bfb"], 4 * (cs + 1))
                        for nt in range(16):
                            gf = 16 * cs + nt
                            if gf >= 2:
                                t.wait_ge(S["bfev"], gf - 1)
                            for ert in range(8):
                                mm = t.matmul(ps_f[gf % 2][:],
                                              hgs[:, ert * 2048 + nt * 128:
                                                  ert * 2048 + nt * 128 + 128],
                                              bf_b[cs % 2][:, ert * 512:(ert + 1) * 512],
                                              start=(ert == 0), stop=(ert == 7))
                                if ert == 7:
                                    mm.then_inc(S["bfm"], 1)

                @blk.gpsimd
                def _(g):
                    for c in range(32):
                        cs, qt = divmod(c, 4)
                        g.wait_ge(S[f"bff{c % 2}"], 16 * (c // 2 + 1))
                        if cs >= 2 and qt == 0:
                            g.wait_ge(S["bfm"], 16 * (cs - 1))
                        g.tensor_copy(bf_b[cs % 2][:, qt * 1024:(qt + 1) * 1024],
                                      bf_f[c % 2][:]).then_inc(S["bfb"], 1)

                @blk.vector
                def _(v):
                    for cc in range(8):
                        v.wait_ge(S["bbom"], cc + 1)
                        v.tensor_copy(bo_b[:, cc * 512:(cc + 1) * 512],
                                      ps_bo[:]).then_inc(S["bboev"], 1)
                    for gf in range(128):
                        cs, nt = divmod(gf, 16)
                        v.wait_ge(S["bboev"], 8)
                        v.wait_ge(S["bfm"], gf + 1)
                        if gf >= 2:
                            v.wait_ge(S[f"bfst{gf % 2}"], 16 * (gf // 2))
                        v.scalar_tensor_tensor(stg_f[gf % 2][:], ps_f[gf % 2][:], 1.0,
                                               bo_b[:, cs * 512:(cs + 1) * 512],
                                               ALU.mult, ALU.add).then_inc(S["bfev"], 1)

                @blk.scalar
                def _(s):
                    for gf in range(128):
                        cs, nt = divmod(gf, 16)
                        s.wait_ge(S["bfev"], gf + 1)
                        s.dma_start(out=ap2(ffn_dr, nt * 128 * DM + cs * 512,
                                            [[DM, 128], [1, 512]]),
                                    in_=stg_f[gf % 2][:]).then_inc(S[f"bfst{gf % 2}"], 16)

    if tap == "ffn":
        with nc.Block() as blk:
            @blk.sync
            def _(sy):
                sy.wait_ge(S["bfst0"], 16 * 64)
                sy.wait_ge(S["bfst1"], 16 * 64)
                sy.dma_start(out=taps["ffn"][:, :], in_=ffn_dr[:, :]).then_inc(S["tapd"], 16)
                sy.wait_ge(S["tapd"], 16)

    if upto >= 7:
        with ExitStack() as st:
            if exchange:
                ksb = st.enter_context(nc.sbuf_tensor("ksb", [128, 32768], BF16))
            vsb = st.enter_context(nc.sbuf_tensor("vsb", [128, 32768], BF16))
            pts = st.enter_context(nc.sbuf_tensor("pts", [128, 16384], BF16))
            zrow = st.enter_context(nc.sbuf_tensor("zrow", [1, 512], F32))
            zb_sb = st.enter_context(nc.sbuf_tensor("zb_sb", [128, 512], F32))
            stg_pv = [st.enter_context(nc.sbuf_tensor(f"stg_pv{i}", [128, 512], BF16)) for i in range(2)]
            ps_s = [st.enter_context(nc.psum_tensor(f"ps_s{i}", [128, 512], F32)) for i in range(2)]
            ps_z = st.enter_context(nc.psum_tensor("ps_z", [1, 512], F32))
            ps_zb = st.enter_context(nc.psum_tensor("ps_zb", [128, 512], F32))
            ps_pv = [st.enter_context(nc.psum_tensor(f"ps_pv{i}", [128, 512], F32)) for i in range(2)]

            with nc.named_scope("phD"), nc.Block() as blk:
                if exchange:
                    @blk.gpsimd
                    def _(g):
                        with g.register("rn_d") as rn, g.register("rf0_d") as rf0, \
                                g.register("rf1_d") as rf1, g.register("rt0_d") as rt0, \
                                g.register("rt1_d") as rt1, g.register("rc_d") as rc:
                            g.wait_ge(S["cflw"], 48)
                            g.reg_load(rn, snon[0:1, 0:1])
                            g.reg_mov(rc, 1)
                            with g.While(rc):
                                g.load(rf0, fl_sh[0:1, 0:1])
                                g.load(rf1, fl_sh[1:2, 0:1])
                                g.reg_alu(rt0, rf0, rn, ALU.not_equal)
                                g.reg_alu(rt1, rf1, rn, ALU.not_equal)
                                g.reg_alu(rc, rt0, rt1, ALU.bitwise_or)
                        g.sem_inc(S["dkv"], 1)

                @blk.sync
                def _(sy):
                    sy.wait_ge(S["bfst0"], 16 * 64)
                    sy.wait_ge(S["bfst1"], 16 * 64)
                    sy.wait_ge(S["bfm"], 128)
                    sy.wait_ge(S["bfev"], 128)
                    if exchange:
                        sy.wait_ge(S["dkv"], 1)
                        for blkh in range(2):
                            for dt in range(8):
                                sy.dma_start(
                                    out=ap2(ksb, dt * 4096 + blkh * 2048,
                                            [[32768, 128], [1, 2048]]),
                                    in_=ap2(k_sh, blkh * DH * NTOK + dt * 128 * NTOK,
                                            [[NTOK, 128], [1, 2048]])).then_inc(S["dk"], 16)
                        for blkh in range(2):
                            sy.dma_start(
                                out=ap2(vsb, blkh * 16 * 1024,
                                        [[32768, 128], [1024, 16], [1, 1024]]),
                                in_=ap2(v_sh, blkh * NTOK * DH,
                                        [[DH, 128], [128 * DH, 16], [1, 1024]])).then_inc(S["dv"], 16)
                    else:
                        sy.wait_ge(S["cvst0"], 16 * 32)
                        sy.wait_ge(S["cvst1"], 16 * 32)
                        sy.dma_start(
                            out=ap2(vsb, 0, [[32768, 128], [1024, 32], [1, 1024]]),
                            in_=ap2(v_loc, 0,
                                    [[DH, 128], [128 * DH, 32], [1, 1024]])).then_inc(S["dv"], 16)

                @blk.tensor
                def _(t):
                    t.wait_ge(S["dk"], 16 * (16 if exchange else 8))
                    t.wait_ge(S["dq"], 16)
                    for qc in range(4):
                        for kt in range(32):
                            gs = 32 * qc + kt
                            if gs >= 2:
                                t.wait_ge(S["dexp"], gs - 1)
                            for dt in range(8):
                                mm = t.matmul(ps_s[gs % 2][:],
                                              ksb[:, dt * 4096 + kt * 128:
                                                  dt * 4096 + kt * 128 + 128],
                                              qsb[:, dt * 2048 + qc * 512:
                                                  dt * 2048 + (qc + 1) * 512],
                                              start=(dt == 0), stop=(dt == 7))
                                if dt == 7:
                                    mm.then_inc(S["dsm"], 1)
                        if qc >= 1:
                            t.wait_ge(S["dzr"], qc)
                        for kt in range(32):
                            gs = 32 * qc + kt
                            t.wait_ge(S["dexp"], gs + 1)
                            mm = t.matmul(ps_z[:], onc[:, 0:1],
                                          pts[:, kt * 512:(kt + 1) * 512],
                                          start=(kt == 0), stop=(kt == 31))
                            if kt == 31:
                                mm.then_inc(S["dzm"], 1)
                        t.wait_ge(S["dzr"], qc + 1)
                        if qc >= 1:
                            t.wait_ge(S["dzbe"], qc)
                        t.matmul(ps_zb[:], onr[0:1, :], zrow[0:1, :],
                                 start=True, stop=True).then_inc(S["dzbm"], 1)
                        if qc == 0:
                            t.wait_ge(S["dv"], 16 * (2 if exchange else 1))
                        for dt in range(8):
                            gp = 8 * qc + dt
                            if gp >= 2:
                                t.wait_ge(S["dpve"], gp - 1)
                            for kt in range(32):
                                mm = t.matmul(ps_pv[gp % 2][:],
                                              vsb[:, kt * 1024 + dt * 128:
                                                  kt * 1024 + dt * 128 + 128],
                                              pts[:, kt * 512:(kt + 1) * 512],
                                              start=(kt == 0), stop=(kt == 31))
                                if kt == 31:
                                    mm.then_inc(S["dpvm"], 1)

                @blk.scalar
                def _(s):
                    s.wait_ge(S["bfst0"], 16 * 64)
                    s.wait_ge(S["bfst1"], 16 * 64)
                    for qc in range(4):
                        for kt in range(32):
                            gs = 32 * qc + kt
                            s.wait_ge(S["dsm"], gs + 1)
                            s.wait_ge(S["dpvm"], 8 * qc)
                            s.activation(pts[:, kt * 512:(kt + 1) * 512], ps_s[gs % 2][:],
                                         AF.Exp, bias=0.0, scale=0.03125).then_inc(S["dexp"], 1)
                        for dt in range(8):
                            gp = 8 * qc + dt
                            s.wait_ge(S["dpve"], gp + 1)
                            s.dma_start(out=ap2(pv_dr, dt * 128 * NTOK + qc * 512,
                                                [[NTOK, 128], [1, 512]]),
                                        in_=stg_pv[gp % 2][:]).then_inc(S[f"dpvst{gp % 2}"], 16)

                @blk.vector
                def _(v):
                    for qc in range(4):
                        v.wait_ge(S["dzm"], qc + 1)
                        v.reciprocal(zrow[0:1, :], ps_z[0:1, :]).then_inc(S["dzr"], 1)
                        v.wait_ge(S["dzbm"], qc + 1)
                        if qc >= 1:
                            v.wait_ge(S["dpve"], 8 * qc)
                        v.tensor_copy(zb_sb[:], ps_zb[:]).then_inc(S["dzbe"], 1)
                        for dt in range(8):
                            gp = 8 * qc + dt
                            v.wait_ge(S["dpvm"], gp + 1)
                            v.wait_ge(S["dzbe"], qc + 1)
                            if gp >= 2:
                                v.wait_ge(S[f"dpvst{gp % 2}"], 16 * (gp // 2))
                            v.scalar_tensor_tensor(stg_pv[gp % 2][:], ps_pv[gp % 2][:], 1.0,
                                                   zb_sb[:], ALU.mult, ALU.mult
                                                   ).then_inc(S["dpve"], 1)

    if tap == "pv":
        with nc.Block() as blk:
            @blk.sync
            def _(sy):
                sy.wait_ge(S["dpvst0"], 16 * 16)
                sy.wait_ge(S["dpvst1"], 16 * 16)
                sy.dma_start(out=taps["pv"][:, :], in_=pv_dr[:, :]).then_inc(S["tapd"], 16)
                sy.wait_ge(S["tapd"], 16)

    s2.close()

    if upto >= 8:
        with ExitStack() as st:
            pvs = st.enter_context(nc.sbuf_tensor("pvs", [128, 16384], BF16))
            wo_f = [st.enter_context(nc.sbuf_tensor(f"wo_f{i}", [128, 1024], F32)) for i in range(2)]
            wo_b = st.enter_context(nc.sbuf_tensor("wo_b", [128, 32768], BF16))
            xt = [st.enter_context(nc.sbuf_tensor(f"xt{i}", [128, 4096], F32)) for i in range(2)]
            ft = [st.enter_context(nc.sbuf_tensor(f"ft{i}", [128, 4096], BF16)) for i in range(2)]
            xf = [st.enter_context(nc.sbuf_tensor(f"xf{i}", [128, 4096], F32)) for i in range(2)]
            ystg = [st.enter_context(nc.sbuf_tensor(f"ystg{i}", [128, 1024], F32)) for i in range(2)]
            ps_p4 = st.enter_context(nc.psum_tensor("ps_p4", [128, 2048], F32))

            with nc.named_scope("phE"), nc.Block() as blk:
                @blk.gpsimd
                def _(g):
                    g.wait_ge(S["dpvm"], 32)
                    g.wait_ge(S["dpve"], 32)

                    def ld_nt(nt):
                        if nt >= 2:
                            g.wait_ge(S["exf"], nt - 1)
                            g.wait_ge(S["exfp"], nt - 1)
                        g.dma_start(out=xt[nt % 2][:],
                                    in_=x_in[nt * 128:(nt + 1) * 128, :]
                                    ).then_inc(S[f"exc{nt % 2}"], 16)
                        g.dma_start(out=ft[nt % 2][:],
                                    in_=ffn_dr[nt * 128:(nt + 1) * 128, :]
                                    ).then_inc(S[f"efc{nt % 2}"], 16)
                    ld_nt(0)
                    ld_nt(1)
                    for nt in range(16):
                        g.wait_ge(S[f"exc{nt % 2}"], 16 * (nt // 2 + 1))
                        g.wait_ge(S[f"efc{nt % 2}"], 16 * (nt // 2 + 1))
                        if nt >= 2:
                            g.wait_ge(S["et1"], 4 * (nt - 1))
                        g.tensor_tensor(xf[nt % 2][:, 0:3072], xt[nt % 2][:, 0:3072],
                                        ft[nt % 2][:, 0:3072],
                                        ALU.add).then_inc(S["exfp"], 1)
                        if nt + 2 < 16:
                            ld_nt(nt + 2)

                @blk.sync
                def _(sy):
                    sy.wait_ge(S["dpvm"], 32)
                    sy.wait_ge(S["dpve"], 32)
                    for c in range(2):
                        cs, qt = divmod(c, 4)
                        sy.dma_start(out=wo_f[c % 2][:],
                                     in_=wo_w[cs, :, qt * 1024:(qt + 1) * 1024]
                                     ).then_inc(S[f"ewf{c % 2}"], 16)
                    for c in range(2, 32):
                        cs, qt = divmod(c, 4)
                        sy.wait_ge(S["ewb"], c - 1)
                        sy.dma_start(out=wo_f[c % 2][:],
                                     in_=wo_w[cs, :, qt * 1024:(qt + 1) * 1024]
                                     ).then_inc(S[f"ewf{c % 2}"], 16)


                @blk.tensor
                def _(t):
                    for nt in range(16):
                        if nt % 4 == 0:
                            ch = nt // 4
                            t.wait_ge(S["ep" if ch == 0 else f"ep{ch}"], 16)
                        for cs in range(8):
                            gy = 8 * nt + cs
                            if nt > 0:
                                t.wait_ge(S["ewb"], 4 * (cs + 1))
                            if gy >= 4:
                                t.wait_ge(S["et1"], gy // 2 - 1)
                            for dt in range(8):
                                if nt == 0 and dt % 2 == 0:
                                    t.wait_ge(S["ewb"], 4 * cs + dt // 2 + 1)
                                mm = t.matmul(ps_p4[:, (gy % 4) * 512:(gy % 4 + 1) * 512],
                                              pvs[:, dt * 2048 + nt * 128:
                                                  dt * 2048 + nt * 128 + 128],
                                              wo_b[:, cs * 4096 + dt * 512:
                                                   cs * 4096 + (dt + 1) * 512],
                                              start=(dt == 0), stop=(dt == 7))
                                if dt == 7:
                                    mm.then_inc(S["epm"], 1)

                @blk.vector
                def _(v):
                    def wocast(c):
                        v.wait_ge(S[f"ewf{c % 2}"], 16 * (c // 2 + 1))
                        v.tensor_copy(wo_b[:, c * 1024:(c + 1) * 1024],
                                      wo_f[c % 2][:]).then_inc(S["ewb"], 1)

                    def mkxf(nt):
                        v.wait_ge(S[f"exc{nt % 2}"], 16 * (nt // 2 + 1))
                        v.wait_ge(S[f"efc{nt % 2}"], 16 * (nt // 2 + 1))
                        if nt >= 2:
                            v.wait_ge(S["et1"], 4 * (nt - 1))
                        v.tensor_tensor(xf[nt % 2][:, 3072:4096], xt[nt % 2][:, 3072:4096],
                                        ft[nt % 2][:, 3072:4096],
                                        ALU.add).then_inc(S["exf"], 1)

                    for c in range(8):
                        wocast(c)
                    mkxf(0)
                    mkxf(1)
                    for c in range(8, 12):
                        wocast(c)
                    inj = {0: range(12, 20), 1: range(20, 28), 2: range(28, 32)}
                    for e in range(64):
                        for c in inj.get(e, ()):
                            wocast(c)
                        gy = 2 * e
                        nt, cs = divmod(gy, 8)
                        v.wait_ge(S["epm"], gy + 2)
                        v.wait_ge(S["exf"], nt + 1)
                        v.wait_ge(S["exfp"], nt + 1)
                        if e >= 2:
                            v.wait_ge(S[f"eyst{e % 2}"], 16 * (e // 2))
                        v.scalar_tensor_tensor(ystg[e % 2][:],
                                               ps_p4[:, (gy % 4) * 512:(gy % 4) * 512 + 1024],
                                               1.0,
                                               xf[nt % 2][:, cs * 512:(cs + 2) * 512],
                                               ALU.mult, ALU.add).then_inc(S["et1"], 1)
                        if cs == 6 and nt + 2 < 16:
                            mkxf(nt + 2)

                @blk.scalar
                def _(s):
                    s.wait_ge(S["dpvm"], 32)
                    s.wait_ge(S["dpve"], 32)
                    for c in range(4):
                        s.wait_ge(S["dpvst0"], 16 * 4 * (c + 1))
                        s.wait_ge(S["dpvst1"], 16 * 4 * (c + 1))
                        s.dma_start(out=ap2(pvs, c * 512,
                                            [[16384, 128], [2048, 8], [1, 512]]),
                                    in_=ap2(pv_dr, c * 512,
                                            [[NTOK, 128], [128 * NTOK, 8], [1, 512]])
                                    ).then_inc(S["ep" if c == 0 else f"ep{c}"], 16)
                    for e in range(64):
                        gy = 2 * e
                        nt, cs = divmod(gy, 8)
                        s.wait_ge(S["et1"], e + 1)
                        s.dma_start(out=ap2(y_out, nt * 128 * DM + cs * 512,
                                            [[DM, 128], [1, 1024]]),
                                    in_=ystg[e % 2][:]).then_inc(S[f"eyst{e % 2}"], 16)
                    for q in range(2):
                        s.wait_ge(S[f"eyst{q}"], 16 * 32)

    main.close()
    nc.finalize()
    return nc, sorted(taps.keys())


def _host_layout(inputs):
    Wqkv = np.ascontiguousarray(np.asarray(inputs["L0_attn_Wqkv"], dtype=np.float32))
    bqkv = np.asarray(inputs["L0_attn_bqkv"], dtype=np.float32)
    Wo = np.ascontiguousarray(np.asarray(inputs["L0_attn_Wo"], dtype=np.float32))
    bo = np.asarray(inputs["L0_attn_bo"], dtype=np.float32)
    A = np.asarray(inputs["L0_ffn_A"], dtype=np.float32)
    B_ = np.asarray(inputs["L0_ffn_B"], dtype=np.float32)
    Wr = np.asarray(inputs["L0_ffn_Wr"], dtype=np.float32)

    wqk = np.ascontiguousarray(
        Wqkv[:, :2048].reshape(32, 128, 16, 128).transpose(2, 1, 0, 3).reshape(16, 128, 4096))
    wv = np.ascontiguousarray(
        Wqkv[:, 2048:].reshape(32, 128, 2, 512).transpose(2, 1, 0, 3).reshape(2, 128, 16384))
    wrl = np.ascontiguousarray(
        Wr.T.reshape(32, 128, 64).transpose(1, 0, 2).reshape(128, 2048))
    at = np.ascontiguousarray(
        A.reshape(1024, 4096).T.reshape(32, 128, 8, 128).transpose(2, 1, 0, 3).reshape(8, 128, 4096))
    bfl = np.ascontiguousarray(
        B_.transpose(0, 2, 1).reshape(1024, 4096).reshape(8, 128, 8, 512)
        .transpose(2, 1, 0, 3).reshape(8, 128, 4096))
    wo = np.ascontiguousarray(
        Wo.reshape(8, 128, 8, 512).transpose(2, 1, 0, 3).reshape(8, 128, 4096))
    bqk = np.ascontiguousarray(bqkv[:2048].reshape(16, 128).T)
    bvr = np.ascontiguousarray(bqkv[2048:].reshape(1, 1024))
    bor = np.ascontiguousarray(bo.reshape(1, 4096))

    idb = np.eye(128, dtype=BF16NP)
    idf = np.eye(128, dtype=np.float32)
    onc = np.ones((128, 1), dtype=BF16NP)
    onr = np.ones((1, 128), dtype=np.float32)
    emat = np.zeros((64, 1024), dtype=BF16NP)
    for e in range(64):
        emat[e, e * 16:(e + 1) * 16] = 1
    return dict(wqk=wqk, wv=wv, wr=wrl, at=at, bfl=bfl, wo=wo, bqk=bqk,
                bvr=bvr, bor=bor, idb=idb, idf=idf, onc=onc, onr=onr, emat=emat)


EXCHANGE = False


def kernel(**inputs):
    x = np.asarray(inputs["L0_attn_x"], dtype=np.float32)
    common = _host_layout(inputs)
    if EXCHANGE:
        common["non"] = np.full((1, 1), np.random.randint(1, 2**31 - 1), dtype=np.int32)
    in_maps = []
    for c in range(8):
        b, h = c // 2, c % 2
        m = dict(common)
        m["x"] = np.ascontiguousarray(x[b, h * NTOK:(h + 1) * NTOK, :])
        if not EXCHANGE:
            m["xp"] = np.ascontiguousarray(x[b, (1 - h) * NTOK:(2 - h) * NTOK, :])
        in_maps.append(m)
    nc, _ = build_nc(exchange=EXCHANGE)
    res = run_bass_kernel_spmd(nc, in_maps, core_ids=list(range(8)))
    y = np.empty((4, 4096, 4096), dtype=np.float32)
    for c in range(8):
        b, h = c // 2, c % 2
        y[b, h * NTOK:(h + 1) * NTOK, :] = res.results[c]["y"]
    return y


# version 17
# speedup vs baseline: 1.2751x; 1.2751x over previous
import numpy as np
import ml_dtypes
from contextlib import ExitStack

import concourse.bass as bass
import concourse.mybir as mybir
from concourse.bass_utils import run_bass_kernel_spmd

F32 = mybir.dt.float32
BF16 = mybir.dt.bfloat16
I32 = mybir.dt.int32
AF = mybir.ActivationFunctionType
ALU = mybir.AluOpType
AX = mybir.AxisListType
BF16NP = ml_dtypes.bfloat16

NTOK = 2048
DM = 4096
DH = 1024
EPS = 1e-5


def ap2(t, off, dims):
    return bass.AP(t, off, dims)


def build_nc(tap=None, detect_races=False, exchange=False):
    upto = {"xnt": 1, "kvq": 3, "rg": 4, "hg": 5, "ffn": 6, "pv": 7}.get(tap, 8)
    KT = NTOK if exchange else 2 * NTOK
    nc = bass.Bass("TRN2", target_bir_lowering=False, debug=False,
                   detect_race_conditions=detect_races,
                   dynamic_dma_scratch_size=4096)

    x_in = nc.dram_tensor("x", [NTOK, DM], F32, kind="ExternalInput")
    if not exchange:
        xp_in = nc.dram_tensor("xp", [NTOK, DM], F32, kind="ExternalInput")
    wqk = nc.dram_tensor("wqk", [16, 128, 4096], F32, kind="ExternalInput")
    wv = nc.dram_tensor("wv", [2, 128, 16384], F32, kind="ExternalInput")
    wr = nc.dram_tensor("wr", [128, 2048], F32, kind="ExternalInput")
    at_w = nc.dram_tensor("at", [8, 128, 4096], F32, kind="ExternalInput")
    bf_w = nc.dram_tensor("bfl", [8, 128, 4096], F32, kind="ExternalInput")
    wo_w = nc.dram_tensor("wo", [8, 128, 4096], F32, kind="ExternalInput")
    bqk_in = nc.dram_tensor("bqk", [128, 16], F32, kind="ExternalInput")
    bv_in = nc.dram_tensor("bvr", [1, 1024], F32, kind="ExternalInput")
    bo_in = nc.dram_tensor("bor", [1, 4096], F32, kind="ExternalInput")
    idb_in = nc.dram_tensor("idb", [128, 128], BF16, kind="ExternalInput")
    idf_in = nc.dram_tensor("idf", [128, 128], F32, kind="ExternalInput")
    onc_in = nc.dram_tensor("onc", [128, 1], BF16, kind="ExternalInput")
    onr_in = nc.dram_tensor("onr", [1, 128], F32, kind="ExternalInput")
    em_in = nc.dram_tensor("emat", [64, 1024], BF16, kind="ExternalInput")
    non_in = (nc.dram_tensor("non", [1, 1], I32, kind="ExternalInput")
              if exchange else None)
    y_out = nc.dram_tensor("y", [NTOK, DM], F32, kind="ExternalOutput")

    q_dr = nc.dram_tensor("q_dr", [DH, NTOK], BF16)
    k_loc = nc.dram_tensor("k_loc", [DH, KT], BF16)
    v_loc = nc.dram_tensor("v_loc", [KT, DH], BF16)
    pv_dr = nc.dram_tensor("pv_dr", [DH, NTOK], BF16)
    hg_dr = nc.dram_tensor("hg_dr", [1024, NTOK], BF16)
    ffn_dr = nc.dram_tensor("ffn_dr", [NTOK, DM], BF16)
    if exchange:
        k_sh = nc.dram_tensor("k_sh", [2, DH, NTOK], BF16, addr_space="Shared")
        v_sh = nc.dram_tensor("v_sh", [2, NTOK, DH], BF16, addr_space="Shared")
        fl_sh = nc.dram_tensor("fl_sh", [2, 128], I32, addr_space="Shared")

    taps = {}

    def tap_out(name, shape, dt):
        taps[name] = nc.dram_tensor("t_" + name, shape, dt, kind="ExternalOutput")

    if tap == "xnt":
        tap_out("xnt", [128, 65536], BF16)
    elif tap == "kvq":
        tap_out("k", [2 * DH, NTOK], BF16)
        tap_out("v", [2 * NTOK, DH], BF16)
        tap_out("q", [DH, NTOK], BF16)
    elif tap == "rg":
        tap_out("sr", [128, 1024], F32)
        tap_out("g", [128, 1024], BF16)
    elif tap == "hg":
        tap_out("hg", [1024, NTOK], BF16)
    elif tap == "ffn":
        tap_out("ffn", [NTOK, DM], BF16)
    elif tap == "pv":
        tap_out("pv", [DH, NTOK], BF16)

    main = ExitStack()

    S = {}
    base_sems = ['cst', 'ini', 'ax0', 'ax1', 'ax2', 'avs', 'avr', 'aac', 'aap', 'atm', 'aev', 'aevv', 'abvm', 'abvev', 'cwf0', 'cwf1', 'cwb', 'ckm', 'ckev', 'ckst0', 'ckst1', 'cqm', 'cqev', 'cqst0', 'cqst1', 'cvf0', 'cvf1', 'cvb', 'cvm', 'cvev', 'cvst0', 'cvst1', 'brf', 'brb', 'brm', 'brev', 'brtm', 'brtev', 'bdv', 'bpl', 'bac', 'bgm', 'bgev', 'bgem', 'bgeev', 'baf0', 'baf1', 'bab', 'bhm', 'bhev', 'bhst0', 'bhst1', 'bhg', 'bbor', 'bbom', 'bboev', 'bff0', 'bff1', 'bfb', 'bfm', 'bfev', 'bfst0', 'bfst1', 'dk', 'dv', 'dq', 'dsm', 'dexp', 'dzm', 'dzr', 'dzbm', 'dzbe', 'dpvm', 'dpve', 'dpvst0', 'dpvst1', 'ep', 'ep1', 'ep2', 'ep3', 'ewf0', 'ewf1', 'ewb', 'epm', 'et1', 'exc0', 'exc1', 'efc0', 'efc1', 'exf', 'exfp', 'eyst0', 'eyst1', 'eyst2', 'eyst3', 'tapd']
    x_sems = ['cflw', 'csv', 'dkv']
    r_sems = []
    for name in base_sems + (x_sems if exchange else r_sems):
        S[name] = main.enter_context(nc.semaphore(name))

    idb = main.enter_context(nc.sbuf_tensor("idb_s", [128, 128], BF16))
    idf = main.enter_context(nc.sbuf_tensor("idf_s", [128, 128], F32))
    onc = main.enter_context(nc.sbuf_tensor("onc_s", [128, 1], BF16))
    onr = main.enter_context(nc.sbuf_tensor("onr_s", [1, 128], F32))
    emat = main.enter_context(nc.sbuf_tensor("emat_s", [64, 1024], BF16))
    bqk = main.enter_context(nc.sbuf_tensor("bqk_s", [128, 16], F32))
    snon = main.enter_context(nc.sbuf_tensor("snon_s", [1, 2], I32))
    czero = main.enter_context(nc.sbuf_tensor("czero_s", [128, 2], F32))
    NCONST = 8 if exchange else 7
    nc.const_aps.register(F32, 0.0, czero[:, 0:1])
    nc.const_aps.register(F32, EPS, czero[:, 1:2])

    s1 = ExitStack()
    xnT = s1.enter_context(nc.sbuf_tensor("xnT", [128, 65536], BF16))
    bv_b = s1.enter_context(nc.sbuf_tensor("bv_b", [128, 1024], F32))
    g_tm = s1.enter_context(nc.sbuf_tensor("g_tm", [128, 1024], BF16))
    g_Ts = s1.enter_context(nc.sbuf_tensor("g_Ts", [64, 2048], BF16))

    if upto >= 1:
        with ExitStack() as st:
            x_sb = [st.enter_context(nc.sbuf_tensor(f"x_sb{i}", [128, 4096], F32)) for i in range(3)]
            xn_sb = [st.enter_context(nc.sbuf_tensor(f"xn_sb{i}", [128, 4096], BF16)) for i in range(2)]
            stats = [st.enter_context(nc.sbuf_tensor(f"stats{i}", [128, 48], F32)) for i in range(2)]
            mv = [st.enter_context(nc.sbuf_tensor(f"mv{i}", [128, 2], F32)) for i in range(4)]
            rsd = [st.enter_context(nc.sbuf_tensor(f"rsd{i}", [128, 1], F32)) for i in range(4)]
            nmr = [st.enter_context(nc.sbuf_tensor(f"nmr{i}", [128, 1], F32)) for i in range(4)]
            lnv = [st.enter_context(nc.sbuf_tensor(f"lnv{i}", [128, 1], F32)) for i in range(4)]
            ps_t = [st.enter_context(nc.psum_tensor(f"ps_t{i}", [128, 512], F32)) for i in range(4)]
            ps_bv = st.enter_context(nc.psum_tensor("ps_bv", [128, 512], F32))
            bvr = st.enter_context(nc.sbuf_tensor("bvr_s", [1, 1024], F32))

            def ev_dst(i, j4):
                return ap2(xnT, (j4 * 4) * 2048 + i * 128,
                           [[65536, 128], [2048, 4], [1, 128]])

            with nc.named_scope("phA"), nc.Block() as blk:
                @blk.sync
                def _(sy):
                    sy.dma_start(out=idb[:], in_=idb_in[:, :]).then_inc(S["cst"], 16)
                    sy.dma_start(out=idf[:], in_=idf_in[:, :]).then_inc(S["cst"], 16)
                    sy.dma_start(out=onc[:], in_=onc_in[:, :]).then_inc(S["cst"], 16)
                    sy.dma_start(out=onr[:], in_=onr_in[:, :]).then_inc(S["cst"], 16)
                    sy.dma_start(out=emat[:], in_=em_in[:, :]).then_inc(S["cst"], 16)
                    sy.dma_start(out=bqk[:], in_=bqk_in[:, :]).then_inc(S["cst"], 16)
                    sy.dma_start(out=bvr[:], in_=bv_in[:, :]).then_inc(S["cst"], 16)
                    if exchange:
                        sy.dma_start(out=snon[0:1, 0:1], in_=non_in[:, :]).then_inc(S["cst"], 16)
                    for i in range(16):
                        if i >= 3:
                            sy.wait_ge(S["aap"], i - 2)
                        sy.dma_start(out=x_sb[i % 3][:],
                                     in_=x_in[i * 128:(i + 1) * 128, :]
                                     ).then_inc(S[f"ax{i % 3}"], 16)

                @blk.gpsimd
                def _(g):
                    if exchange:
                        g.wait_ge(S["cst"], 16 * NCONST)
                        with g.register("rn") as rn:
                            g.reg_load(rn, snon[0:1, 0:1])
                            g.reg_save(snon[0:1, 1:2], rn).then_inc(S["csv"], 1)

                @blk.vector
                def _(v):
                    v.memset(czero[:, 0:1], 0.0).then_inc(S["ini"], 1)
                    v.memset(czero[:, 1:2], EPS).then_inc(S["ini"], 1)
                    for cc in range(2):
                        v.wait_ge(S["abvm"], cc + 1)
                        v.tensor_copy(bv_b[:, cc * 512:(cc + 1) * 512],
                                      ps_bv[:]).then_inc(S["abvev"], 1)
                    for p in range(8):
                        a, bb = 2 * p, 2 * p + 1
                        for i in (a, bb):
                            v.wait_ge(S[f"ax{i % 3}"], 16 * (i // 3 + 1))
                            if p >= 1:
                                v.wait_ge(S["avs"], 18 * (p - 1) + 17 + (i - 2 * p))
                            if p >= 2:
                                v.wait_ge(S["aac"], 4 * p - 6)
                            for j in range(8):
                                v.bn_stats(stats[i % 2][:, 6 * j:6 * j + 6],
                                           x_sb[i % 3][:, 512 * j:512 * (j + 1)]
                                           ).then_inc(S["avs"], 1)
                        v.wait_ge(S["avs"], 18 * p + 16)
                        for i in (a, bb):
                            v.bn_aggr(mv[i % 4][:], stats[i % 2][:, 0:48]).then_inc(S["avs"], 1)
                        for i in (a, bb):
                            v.wait_ge(S["aac"], 4 * p + (3 if i == a else 4))
                            v.tensor_scalar(nmr[i % 4][:], mv[i % 4][:, 0:1],
                                            rsd[i % 4][:, 0:1], -1.0,
                                            ALU.mult, ALU.mult).then_inc(S["avr"], 1)
                        for i in (a, bb):
                            for j4 in range(1, 8, 2):
                                g_id = 8 * i + j4
                                v.wait_ge(S["atm"], g_id + 1)
                                v.tensor_copy(ev_dst(i, j4),
                                              ps_t[g_id % 4][:]).then_inc(S["aevv"], 1)

                @blk.scalar
                def _(s):
                    s.wait_ge(S["ini"], 2)
                    for p in range(8):
                        a, bb = 2 * p, 2 * p + 1
                        for i in (a, bb):
                            s.wait_ge(S["avs"], 18 * p + 17 + (i - a))
                            s.activation(lnv[i % 4][:], mv[i % 4][:, 1:2], AF.Ln,
                                         bias=EPS, scale=1.0).then_inc(S["aac"], 1)
                        for i in (a, bb):
                            s.wait_ge(S["aac"], 4 * p + 2)
                            s.activation(rsd[i % 4][:], lnv[i % 4][:], AF.Exp,
                                         bias=0.0, scale=-0.5).then_inc(S["aac"], 1)
                        for i in (a, bb):
                            s.wait_ge(S["avr"], 2 * p + 1 + (i - a))
                            if i >= 2:
                                s.wait_ge(S["atm"], 8 * (i - 1))
                            s.activation(xn_sb[i % 2][:], x_sb[i % 3][:], AF.Identity,
                                         bias=nmr[i % 4][:, 0:1],
                                         scale=rsd[i % 4][:, 0:1]).then_inc(S["aap"], 1)
                        for i in (a, bb):
                            for j4 in range(0, 8, 2):
                                g_id = 8 * i + j4
                                s.wait_ge(S["atm"], g_id + 1)
                                s.activation(ev_dst(i, j4), ps_t[g_id % 4][:], AF.Identity,
                                             bias=0.0, scale=1.0).then_inc(S["aev"], 1)

                @blk.tensor
                def _(t):
                    t.wait_ge(S["cst"], 16 * NCONST)
                    for cc in range(2):
                        if cc >= 1:
                            t.wait_ge(S["abvev"], cc)
                        t.matmul(ps_bv[:], onr[0:1, :], bvr[0:1, cc * 512:(cc + 1) * 512],
                                 start=True, stop=True).then_inc(S["abvm"], 1)
                    for i in range(16):
                        t.wait_ge(S["aap"], i + 1)
                        for j4 in range(8):
                            g_id = 8 * i + j4
                            if g_id >= 4:
                                if g_id % 2 == 0:
                                    t.wait_ge(S["aev"], g_id // 2 - 1)
                                else:
                                    t.wait_ge(S["aevv"], (g_id - 1) // 2 - 1)
                            for jj in range(4):
                                j = j4 * 4 + jj
                                mm = t.matmul(ps_t[g_id % 4][:, jj * 128:(jj + 1) * 128],
                                              xn_sb[i % 2][:, j * 128:(j + 1) * 128],
                                              idb[:], start=True, stop=True)
                                if jj == 3:
                                    mm.then_inc(S["atm"], 1)

    if tap == "xnt":
        with nc.Block() as blk:
            @blk.sync
            def _(sy):
                sy.wait_ge(S["aev"], 64)
                sy.wait_ge(S["aevv"], 64)
                sy.dma_start(out=taps["xnt"][:, :], in_=xnT[:]).then_inc(S["tapd"], 16)
                sy.wait_ge(S["tapd"], 16)

    if upto >= 2:
        with ExitStack() as st:
            wqk_f = [st.enter_context(nc.sbuf_tensor(f"wqk_f{i}", [128, 1024], F32)) for i in range(2)]
            wqk_b = [st.enter_context(nc.sbuf_tensor(f"wqk_b{i}", [128, 4096], BF16)) for i in range(2)]
            stg_k = [st.enter_context(nc.sbuf_tensor(f"stg_k{i}", [128, 512], BF16)) for i in range(2)]
            stg_q = [st.enter_context(nc.sbuf_tensor(f"stg_q{i}", [128, 512], BF16)) for i in range(2)]
            vq_f = [st.enter_context(nc.sbuf_tensor(f"vq_f{i}", [128, 2048], F32)) for i in range(2)]
            v_bf = st.enter_context(nc.sbuf_tensor("v_bf", [128, 16384], BF16))
            stg_v = [st.enter_context(nc.sbuf_tensor(f"stg_v{i}", [128, 512], BF16)) for i in range(2)]
            ps_qk = [st.enter_context(nc.psum_tensor(f"ps_qk{i}", [128, 512], F32)) for i in range(2)]
            ps_v = [st.enter_context(nc.psum_tensor(f"ps_v{i}", [128, 512], F32)) for i in range(2)]

            with nc.named_scope("phC"), nc.Block() as blk:
                @blk.sync
                def _(sy):
                    sy.wait_ge(S["aap"], 16)
                    sy.wait_ge(S["atm"], 128)
                    sy.wait_ge(S["avs"], 144)
                    sy.wait_ge(S["abvev"], 2)
                    for c in range(32):
                        k, qt = divmod(c, 4)
                        if c >= 2:
                            sy.wait_ge(S["cwb"], c - 1)
                        sy.dma_start(out=wqk_f[c % 2][:],
                                     in_=wqk[8 + k, :, qt * 1024:(qt + 1) * 1024]
                                     ).then_inc(S[f"cwf{c % 2}"], 16)
                    for c2 in range(8):
                        if c2 >= 2:
                            sy.wait_ge(S["cvb"], c2 - 1)
                        sy.dma_start(out=vq_f[c2 % 2][:],
                                     in_=wv[0, :, c2 * 2048:(c2 + 1) * 2048]
                                     ).then_inc(S[f"cvf{c2 % 2}"], 16)
                    for c in range(32, 42):
                        k, qt = divmod(c, 4)
                        sy.wait_ge(S["cwb"], c - 1)
                        sy.dma_start(out=wqk_f[c % 2][:],
                                     in_=wqk[k - 8, :, qt * 1024:(qt + 1) * 1024]
                                     ).then_inc(S[f"cwf{c % 2}"], 16)
                    for c2 in range(8, 16):
                        sy.wait_ge(S["cvb"], c2 - 1)
                        sy.dma_start(out=vq_f[c2 % 2][:],
                                     in_=wv[1, :, (c2 % 8) * 2048:(c2 % 8 + 1) * 2048]
                                     ).then_inc(S[f"cvf{c2 % 2}"], 16)
                    for c in range(42, 64):
                        k, qt = divmod(c, 4)
                        sy.wait_ge(S["cwb"], c - 1)
                        sy.dma_start(out=wqk_f[c % 2][:],
                                     in_=wqk[k - 8, :, qt * 1024:(qt + 1) * 1024]
                                     ).then_inc(S[f"cwf{c % 2}"], 16)

                @blk.gpsimd
                def _(g):
                    if exchange:
                        g.wait_ge(S["ckst0"], 16 * 16)
                        g.wait_ge(S["ckst1"], 16 * 16)
                        g.wait_ge(S["cvst0"], 16 * 16)
                        g.wait_ge(S["cvst1"], 16 * 16)
                        g.wait_ge(S["csv"], 1)
                        with g.register("rh") as rh, g.register("rko") as rko, \
                                g.register("rvo") as rvo, g.register("rfo") as rfo:
                            g.reg_alu(rh, g.to_reg(g.partition_id()), 1, ALU.bitwise_and)
                            g.reg_alu(rko, rh, DH * NTOK, ALU.mult)
                            g.reg_alu(rvo, rh, NTOK * DH, ALU.mult)
                            g.reg_alu(rfo, rh, 128, ALU.mult)
                            g.dma_start(out=ap2(k_sh, rko, [[NTOK, DH], [1, NTOK]]),
                                        in_=k_loc[:, :]).then_inc(S["cflw"], 16)
                            g.dma_start(out=ap2(v_sh, rvo, [[DH, NTOK], [1, DH]]),
                                        in_=v_loc[:, :]).then_inc(S["cflw"], 16)
                            g.wait_ge(S["cflw"], 32)
                            g.dma_start(out=ap2(fl_sh, rfo, [[1, 1], [1, 1]]),
                                        in_=snon[0:1, 1:2]).then_inc(S["cflw"], 16)
                            g.wait_ge(S["cflw"], 48)

                @blk.tensor
                def _(t):
                    t.wait_ge(S["aev"], 64)
                    t.wait_ge(S["aevv"], 64)

                    def qk_slab(k):
                        is_k = k < 8
                        kk = k if is_k else k - 8
                        smm = S["ckm"] if is_k else S["cqm"]
                        sev = S["ckev"] if is_k else S["cqev"]
                        t.wait_ge(S["cwb"], 4 * (k + 1))
                        for ncc in range(4):
                            g_id = 4 * kk + ncc
                            if g_id >= 2:
                                t.wait_ge(sev, g_id - 1)
                            for ct in range(32):
                                mm = t.matmul(ps_qk[g_id % 2][:],
                                              wqk_b[k % 2][:, ct * 128:(ct + 1) * 128],
                                              xnT[:, ct * 2048 + ncc * 512:
                                                  ct * 2048 + (ncc + 1) * 512],
                                              start=(ct == 0), stop=(ct == 31))
                                if ct == 31:
                                    mm.then_inc(smm, 1)

                    for k in range(8):
                        qk_slab(k)
                    for vs in range(2):
                        t.wait_ge(S["cvb"], 8 * (vs + 1))
                        for nt in range(16):
                            gv = 16 * vs + nt
                            if gv >= 2:
                                t.wait_ge(S["cvev"], gv - 1)
                            for ct in range(32):
                                mm = t.matmul(ps_v[gv % 2][:],
                                              xnT[:, ct * 2048 + nt * 128:
                                                  ct * 2048 + nt * 128 + 128],
                                              v_bf[:, ct * 512:(ct + 1) * 512],
                                              start=(ct == 0), stop=(ct == 31))
                                if ct == 31:
                                    mm.then_inc(S["cvm"], 1)
                    for k in range(8, 16):
                        qk_slab(k)

                @blk.vector
                def _(v):
                    def wcast(c):
                        k, qt = divmod(c, 4)
                        v.wait_ge(S[f"cwf{c % 2}"], 16 * (c // 2 + 1))
                        if k >= 2 and qt == 0:
                            if k - 2 < 8:
                                v.wait_ge(S["ckm"], 4 * (k - 1))
                            else:
                                v.wait_ge(S["cqm"], 4 * (k - 9))
                        v.tensor_copy(wqk_b[k % 2][:, qt * 1024:(qt + 1) * 1024],
                                      wqk_f[c % 2][:]).then_inc(S["cwb"], 1)

                    for c in range(32):
                        wcast(c)
                    for c2 in range(8):
                        v.wait_ge(S[f"cvf{c2 % 2}"], 16 * (c2 // 2 + 1))
                        v.tensor_copy(v_bf[:, c2 * 2048:(c2 + 1) * 2048],
                                      vq_f[c2 % 2][:]).then_inc(S["cvb"], 1)
                    for c in range(32, 40):
                        wcast(c)
                    for gv in range(16):
                        v.wait_ge(S["cvm"], gv + 1)
                        if gv >= 2:
                            v.wait_ge(S[f"cvst{gv % 2}"], 16 * (gv // 2))
                        v.scalar_tensor_tensor(stg_v[gv % 2][:], ps_v[gv % 2][:], 1.0,
                                               bv_b[:, 0:512],
                                               ALU.mult, ALU.add).then_inc(S["cvev"], 1)
                    for c2 in range(8, 16):
                        v.wait_ge(S[f"cvf{c2 % 2}"], 16 * (c2 // 2 + 1))
                        if c2 == 8:
                            v.wait_ge(S["cvm"], 16)
                        v.tensor_copy(v_bf[:, (c2 - 8) * 2048:(c2 - 7) * 2048],
                                      vq_f[c2 % 2][:]).then_inc(S["cvb"], 1)
                    for gv in range(16, 32):
                        v.wait_ge(S["cvm"], gv + 1)
                        v.wait_ge(S[f"cvst{gv % 2}"], 16 * (gv // 2))
                        v.scalar_tensor_tensor(stg_v[gv % 2][:], ps_v[gv % 2][:], 1.0,
                                               bv_b[:, 512:1024],
                                               ALU.mult, ALU.add).then_inc(S["cvev"], 1)
                    for c in range(40, 64):
                        wcast(c)

                @blk.scalar
                def _(s):
                    def qk_out(k):
                        is_k = k < 8
                        kk = k if is_k else k - 8
                        smm = S["ckm"] if is_k else S["cqm"]
                        sev = S["ckev"] if is_k else S["cqev"]
                        sstp = [S["ckst0"], S["ckst1"]] if is_k else [S["cqst0"], S["cqst1"]]
                        stg = stg_k if is_k else stg_q
                        bcol = 8 + kk if is_k else kk
                        dst = k_loc if is_k else q_dr
                        for ncc in range(4):
                            g_id = 4 * kk + ncc
                            s.wait_ge(smm, g_id + 1)
                            if g_id >= 2:
                                s.wait_ge(sstp[g_id % 2], 16 * (g_id // 2))
                            s.activation(stg[g_id % 2][:], ps_qk[g_id % 2][:],
                                         AF.Identity, bias=bqk[:, bcol:bcol + 1],
                                         scale=1.0).then_inc(sev, 1)
                            s.wait_ge(sev, 4 * kk + ncc + 1)
                            dstride = KT if dst is k_loc else NTOK
                            s.dma_start(out=ap2(dst, kk * 128 * dstride + ncc * 512,
                                                [[dstride, 128], [1, 512]]),
                                        in_=stg[g_id % 2][:]).then_inc(sstp[g_id % 2], 16)

                    for k in range(8):
                        qk_out(k)
                    for vs in range(2):
                        for nt in range(16):
                            gv = 16 * vs + nt
                            s.wait_ge(S["cvev"], gv + 1)
                            s.dma_start(out=ap2(v_loc, nt * 128 * DH + vs * 512,
                                                [[DH, 128], [1, 512]]),
                                        in_=stg_v[gv % 2][:]).then_inc(S[f"cvst{gv % 2}"], 16)
                    for k in range(8, 16):
                        qk_out(k)

    if tap == "kvq":
        with nc.Block() as blk:
            @blk.gpsimd
            def _(g):
                with g.register("rn_d") as rn, g.register("rf0_d") as rf0, \
                        g.register("rf1_d") as rf1, g.register("rt0_d") as rt0, \
                        g.register("rt1_d") as rt1, g.register("rc_d") as rc:
                    g.wait_ge(S["cflw"], 48)
                    g.reg_load(rn, snon[0:1, 0:1])
                    g.reg_mov(rc, 1)
                    with g.While(rc):
                        g.load(rf0, fl_sh[0:1, 0:1])
                        g.load(rf1, fl_sh[1:2, 0:1])
                        g.reg_alu(rt0, rf0, rn, ALU.not_equal)
                        g.reg_alu(rt1, rf1, rn, ALU.not_equal)
                        g.reg_alu(rc, rt0, rt1, ALU.bitwise_or)
                g.sem_inc(S["dkv"], 1)

            @blk.sync
            def _(sy):
                sy.wait_ge(S["dkv"], 1)
                sy.wait_ge(S["cqst0"], 16 * 16)
                sy.wait_ge(S["cqst1"], 16 * 16)
                sy.dma_start(out=taps["k"][:, :],
                             in_=ap2(k_sh, 0, [[NTOK, 2048], [1, NTOK]])).then_inc(S["tapd"], 16)
                sy.dma_start(out=taps["v"][:, :],
                             in_=ap2(v_sh, 0, [[DH, 4096], [1, DH]])).then_inc(S["tapd"], 16)
                sy.dma_start(out=taps["q"][:, :],
                             in_=q_dr[:, :]).then_inc(S["tapd"], 16)
                sy.wait_ge(S["tapd"], 48)

    if upto >= 4:
        with ExitStack() as st:
            wr_f = st.enter_context(nc.sbuf_tensor("wr_f", [128, 2048], F32))
            wr_b = st.enter_context(nc.sbuf_tensor("wr_b", [128, 2048], BF16))
            s_rT = st.enter_context(nc.sbuf_tensor("s_rT", [64, 2048], F32))
            s_r = st.enter_context(nc.sbuf_tensor("s_r", [128, 1024], F32))
            r1 = st.enter_context(nc.sbuf_tensor("r1", [128, 8], F32))
            r2 = st.enter_context(nc.sbuf_tensor("r2", [128, 16], F32))
            t2b = st.enter_context(nc.sbuf_tensor("t2b", [128, 64], F32))
            nm1 = st.enter_context(nc.sbuf_tensor("nm1", [128, 2], F32))
            e_t = st.enter_context(nc.sbuf_tensor("e_t", [128, 128], F32))
            mask = st.enter_context(nc.sbuf_tensor("maskb", [128, 64], F32))
            em_t = st.enter_context(nc.sbuf_tensor("em_t", [128, 1024], F32))
            zc = st.enter_context(nc.sbuf_tensor("zc", [128, 16], F32))
            zic = st.enter_context(nc.sbuf_tensor("zic", [128, 16], F32))
            ps_r = [st.enter_context(nc.psum_tensor(f"ps_r{i}", [64, 512], F32)) for i in range(2)]
            ps_rt = [st.enter_context(nc.psum_tensor(f"ps_rt{i}", [128, 64], F32)) for i in range(2)]

            with nc.named_scope("phB1a"), nc.Block() as blk:
                @blk.sync
                def _(sy):
                    sy.wait_ge(S["ckst0"], 16 * 16)
                    sy.wait_ge(S["ckst1"], 16 * 16)
                    sy.wait_ge(S["cqst0"], 16 * 16)
                    sy.wait_ge(S["cqst1"], 16 * 16)
                    sy.wait_ge(S["cvst0"], 16 * 16)
                    sy.wait_ge(S["cvst1"], 16 * 16)
                    sy.dma_start(out=wr_f[:], in_=wr[:, :]).then_inc(S["brf"], 16)

                @blk.gpsimd
                def _(g):
                    g.wait_ge(S["brf"], 16)
                    g.tensor_copy(wr_b[:], wr_f[:]).then_inc(S["brb"], 1)
                    for nt in range(16):
                        d = nt % 2
                        sl = s_r[:, nt * 64:(nt + 1) * 64]
                        g.wait_ge(S["bdv"], 4 * nt + 3)
                        if nt >= 1:
                            g.wait_ge(S["bpl"], 2 * nt)
                        g.tensor_scalar(mask[:], sl, r2[:, 8 * d + 7:8 * d + 8], None,
                                        ALU.is_ge, ALU.bypass).then_inc(S["bpl"], 1)
                        g.wait_ge(S["bac"], nt + 1)
                        g.wait_ge(S["bpl"], 2 * nt + 1)
                        g.tensor_tensor(em_t[:, 64 * nt:64 * nt + 64],
                                        e_t[:, 64 * d:64 * d + 64], mask[:],
                                        ALU.mult).then_inc(S["bpl"], 1)
                    for nt in range(16):
                        g.wait_ge(S["bdv"], 64 + 2 * nt + 2)
                        g.tensor_scalar(g_tm[:, nt * 64:(nt + 1) * 64],
                                        em_t[:, 64 * nt:64 * nt + 64], zic[:, nt:nt + 1],
                                        2.0, ALU.mult, ALU.mult).then_inc(S["bpl"], 1)

                @blk.tensor
                def _(t):
                    t.wait_ge(S["brb"], 1)
                    for ncc in range(4):
                        if ncc >= 2:
                            t.wait_ge(S["brev"], ncc - 1)
                        for ct in range(32):
                            mm = t.matmul(ps_r[ncc % 2][:],
                                          wr_b[:, ct * 64:(ct + 1) * 64],
                                          xnT[:, ct * 2048 + ncc * 512:
                                              ct * 2048 + (ncc + 1) * 512],
                                          start=(ct == 0), stop=(ct == 31))
                            if ct == 31:
                                mm.then_inc(S["brm"], 1)
                    t.wait_ge(S["brev"], 4)
                    for nt in range(16):
                        if nt >= 2:
                            t.wait_ge(S["brtev"], nt - 1)
                        t.matmul(ps_rt[nt % 2][:], s_rT[0:64, nt * 128:(nt + 1) * 128],
                                 idf[0:64, 0:64], start=True, stop=True).then_inc(S["brtm"], 1)

                @blk.vector
                def _(v):
                    for ncc in range(4):
                        v.wait_ge(S["brm"], ncc + 1)
                        v.tensor_copy(s_rT[0:64, ncc * 512:(ncc + 1) * 512],
                                      ps_r[ncc % 2][0:64, :]).then_inc(S["brev"], 1)
                    for nt in range(16):
                        v.wait_ge(S["brtm"], nt + 1)
                        v.tensor_copy(s_r[:, nt * 64:(nt + 1) * 64],
                                      ps_rt[nt % 2][:, 0:64]).then_inc(S["brtev"], 1)
                    for nt in range(16):
                        b = 4 * nt
                        d = nt % 2
                        sl = s_r[:, nt * 64:(nt + 1) * 64]
                        v.wait_ge(S["brtev"], nt + 1)
                        if nt >= 1:
                            v.wait_ge(S["bdv"], 4 * nt)
                        v.max(r1[:], sl).then_inc(S["bdv"], 1)
                        v.wait_ge(S["bdv"], b + 1)
                        v.match_replace(t2b[:], r1[:], sl, -1e30).then_inc(S["bdv"], 1)
                        v.wait_ge(S["bdv"], b + 2)
                        if nt >= 2:
                            v.wait_ge(S["bpl"], 2 * nt - 3)
                        v.max(r2[:, 8 * d:8 * d + 8], t2b[:]).then_inc(S["bdv"], 1)
                        if nt >= 2:
                            v.wait_ge(S["bac"], nt - 1)
                        v.tensor_scalar(nm1[:, d:d + 1], r1[:, 0:1], -1.0, None,
                                        ALU.mult, ALU.bypass).then_inc(S["bdv"], 1)
                    for nt in range(16):
                        v.wait_ge(S["bpl"], 2 * nt + 2)
                        v.tensor_reduce(zc[:, nt:nt + 1], em_t[:, 64 * nt:64 * nt + 64],
                                        AX.X, ALU.add).then_inc(S["bdv"], 1)
                        v.wait_ge(S["bdv"], 64 + 2 * nt + 1)
                        v.reciprocal(zic[:, nt:nt + 1],
                                     zc[:, nt:nt + 1]).then_inc(S["bdv"], 1)

                @blk.scalar
                def _(s):
                    for nt in range(16):
                        d = nt % 2
                        s.wait_ge(S["bdv"], 4 * nt + 4)
                        if nt >= 2:
                            s.wait_ge(S["bpl"], 2 * nt - 2)
                        s.activation(e_t[:, 64 * d:64 * d + 64],
                                     s_r[:, nt * 64:(nt + 1) * 64], AF.Exp,
                                     bias=nm1[:, d:d + 1], scale=1.0).then_inc(S["bac"], 1)

    if tap == "rg":
        with nc.Block() as blk:
            @blk.sync
            def _(sy):
                sy.wait_ge(S["bdv"], 96)
                sy.wait_ge(S["bpl"], 48)
                sy.dma_start(out=taps["sr"][:, :], in_=s_r[:]).then_inc(S["tapd"], 16)
                sy.dma_start(out=taps["g"][:, :], in_=g_tm[:]).then_inc(S["tapd"], 16)
                sy.wait_ge(S["tapd"], 32)

    if upto >= 5:
        with ExitStack() as st:
            g_exp = [st.enter_context(nc.sbuf_tensor(f"g_exp{i}", [128, 2048], BF16)) for i in range(2)]
            at_f = [st.enter_context(nc.sbuf_tensor(f"at_f{i}", [128, 1024], F32)) for i in range(2)]
            at_b = [st.enter_context(nc.sbuf_tensor(f"at_b{i}", [128, 4096], BF16)) for i in range(2)]
            stg_h = [st.enter_context(nc.sbuf_tensor(f"stg_h{i}", [128, 512], BF16)) for i in range(2)]
            ps_g = [st.enter_context(nc.psum_tensor(f"ps_g{i}", [64, 128], F32)) for i in range(2)]
            ps_ge = [st.enter_context(nc.psum_tensor(f"ps_ge{i}", [128, 512], F32)) for i in range(2)]
            ps_h = [st.enter_context(nc.psum_tensor(f"ps_h{i}", [128, 512], F32)) for i in range(4)]

            with nc.named_scope("phB1b"), nc.Block() as blk:
                @blk.sync
                def _(sy):
                    sy.wait_ge(S["bdv"], 96)
                    sy.wait_ge(S["bpl"], 48)
                    sy.wait_ge(S["bac"], 16)
                    sy.wait_ge(S["brm"], 4)
                    sy.wait_ge(S["brtm"], 16)
                    for c in range(32):
                        ert, qt = divmod(c, 4)
                        if c >= 2:
                            sy.wait_ge(S["bab"], c - 1)
                        sy.dma_start(out=at_f[c % 2][:],
                                     in_=at_w[ert, :, qt * 1024:(qt + 1) * 1024]).then_inc(S[f"baf{c % 2}"], 16)

                @blk.gpsimd
                def _(g):
                    for c in range(32):
                        ert, qt = divmod(c, 4)
                        g.wait_ge(S[f"baf{c % 2}"], 16 * (c // 2 + 1))
                        if ert >= 2 and qt == 0:
                            g.wait_ge(S["bhm"], 4 * (ert - 1))
                        g.tensor_copy(at_b[ert % 2][:, qt * 1024:(qt + 1) * 1024],
                                      at_f[c % 2][:]).then_inc(S["bab"], 1)

                @blk.tensor
                def _(t):
                    def h_grp(gh):
                        ert, ncc = divmod(gh, 4)
                        if ncc == 0:
                            t.wait_ge(S["bab"], 4 * (ert + 1))
                        if gh >= 4:
                            t.wait_ge(S["bhev"], gh - 3)
                        for ct in range(32):
                            mm = t.matmul(ps_h[gh % 4][:],
                                          at_b[ert % 2][:, ct * 128:(ct + 1) * 128],
                                          xnT[:, ct * 2048 + ncc * 512:
                                              ct * 2048 + (ncc + 1) * 512],
                                          start=(ct == 0), stop=(ct == 31))
                            if ct == 31:
                                mm.then_inc(S["bhm"], 1)

                    for gh in range(4):
                        h_grp(gh)
                    for nt in range(16):
                        t.wait_ge(S["bpl"], 33 + nt)
                        if nt >= 2:
                            t.wait_ge(S["bgev"], nt - 1)
                        t.matmul(ps_g[nt % 2][:], g_tm[:, nt * 64:(nt + 1) * 64],
                                 idb[:], start=True, stop=True).then_inc(S["bgm"], 1)
                    t.wait_ge(S["bgev"], 16)
                    for ert in range(8):
                        for ncc in range(4):
                            ge = 4 * ert + ncc
                            if ge >= 2:
                                t.wait_ge(S["bgeev"], ge - 1)
                            t.matmul(ps_ge[ge % 2][:], emat[0:64, ert * 128:(ert + 1) * 128],
                                     g_Ts[0:64, ncc * 512:(ncc + 1) * 512],
                                     start=True, stop=True).then_inc(S["bgem"], 1)
                        if ert < 7:
                            for gh in range(4 * ert + 4, 4 * ert + 8):
                                h_grp(gh)

                @blk.vector
                def _(v):
                    for nt in range(16):
                        v.wait_ge(S["bgm"], nt + 1)
                        v.tensor_copy(g_Ts[0:64, nt * 128:(nt + 1) * 128],
                                      ps_g[nt % 2][0:64, :]).then_inc(S["bgev"], 1)
                    for ert in range(8):
                        for ncc in range(4):
                            ge = 4 * ert + ncc
                            v.wait_ge(S["bgem"], ge + 1)
                            if ert >= 2 and ncc == 0:
                                v.wait_ge(S["bhev"], 4 * (ert - 1))
                            v.tensor_copy(g_exp[ert % 2][:, ncc * 512:(ncc + 1) * 512],
                                          ps_ge[ge % 2][:]).then_inc(S["bgeev"], 1)
                        for ncc in range(4):
                            gh = 4 * ert + ncc
                            v.wait_ge(S["bhm"], gh + 1)
                            if gh >= 2:
                                v.wait_ge(S[f"bhst{gh % 2}"], 16 * (gh // 2))
                            v.scalar_tensor_tensor(stg_h[gh % 2][:], ps_h[gh % 4][:], 1.0,
                                                   g_exp[ert % 2][:, ncc * 512:(ncc + 1) * 512],
                                                   ALU.mult, ALU.mult).then_inc(S["bhev"], 1)

                @blk.scalar
                def _(s):
                    for gh in range(32):
                        ert, ncc = divmod(gh, 4)
                        s.wait_ge(S["bhev"], gh + 1)
                        s.dma_start(out=ap2(hg_dr, ert * 128 * NTOK + ncc * 512,
                                            [[NTOK, 128], [1, 512]]),
                                    in_=stg_h[gh % 2][:]).then_inc(S[f"bhst{gh % 2}"], 16)

    if tap == "hg":
        with nc.Block() as blk:
            @blk.sync
            def _(sy):
                sy.wait_ge(S["bhst0"], 16 * 16)
                sy.wait_ge(S["bhst1"], 16 * 16)
                sy.dma_start(out=taps["hg"][:, :], in_=hg_dr[:, :]).then_inc(S["tapd"], 16)
                sy.wait_ge(S["tapd"], 16)

    if upto >= 6 and not exchange:
        with ExitStack() as st:
            x_sb = [st.enter_context(nc.sbuf_tensor(f"px_sb{i}", [128, 4096], F32)) for i in range(3)]
            xn_sb = [st.enter_context(nc.sbuf_tensor(f"pxn_sb{i}", [128, 4096], BF16)) for i in range(2)]
            stats = [st.enter_context(nc.sbuf_tensor(f"pstats{i}", [128, 48], F32)) for i in range(2)]
            mv = [st.enter_context(nc.sbuf_tensor(f"pmv{i}", [128, 2], F32)) for i in range(4)]
            rsd = [st.enter_context(nc.sbuf_tensor(f"prsd{i}", [128, 1], F32)) for i in range(4)]
            nmr = [st.enter_context(nc.sbuf_tensor(f"pnmr{i}", [128, 1], F32)) for i in range(4)]
            lnv = [st.enter_context(nc.sbuf_tensor(f"plnv{i}", [128, 1], F32)) for i in range(4)]
            ps_t = [st.enter_context(nc.psum_tensor(f"pps_t{i}", [128, 512], F32)) for i in range(4)]

            def ev_dst(i, j4):
                return ap2(xnT, (j4 * 4) * 2048 + i * 128,
                           [[65536, 128], [2048, 4], [1, 128]])

            with nc.named_scope("phAp"), nc.Block() as blk:
                @blk.sync
                def _(sy):
                    sy.wait_ge(S["bhst0"], 16 * 16)
                    sy.wait_ge(S["bhst1"], 16 * 16)
                    sy.wait_ge(S["bhm"], 32)
                    sy.wait_ge(S["bgem"], 32)
                    sy.wait_ge(S["bhev"], 32)
                    for i in range(16):
                        if i >= 3:
                            sy.wait_ge(S["aap"], 16 + i - 2)
                        sy.dma_start(out=x_sb[i % 3][:],
                                     in_=xp_in[i * 128:(i + 1) * 128, :]
                                     ).then_inc(S[f"ax{i % 3}"], 16)

                @blk.vector
                def _(v):
                    for p in range(8):
                        a, bb = 2 * p, 2 * p + 1
                        for i in (a, bb):
                            v.wait_ge(S[f"ax{i % 3}"], 16 * ((6, 5, 5)[i % 3] + i // 3 + 1))
                            if p >= 1:
                                v.wait_ge(S["avs"], 144 + 18 * (p - 1) + 17 + (i - 2 * p))
                            if p >= 2:
                                v.wait_ge(S["aac"], 32 + 4 * p - 6)
                            for j in range(8):
                                v.bn_stats(stats[i % 2][:, 6 * j:6 * j + 6],
                                           x_sb[i % 3][:, 512 * j:512 * (j + 1)]
                                           ).then_inc(S["avs"], 1)
                        v.wait_ge(S["avs"], 144 + 18 * p + 16)
                        for i in (a, bb):
                            v.bn_aggr(mv[i % 4][:], stats[i % 2][:, 0:48]).then_inc(S["avs"], 1)
                        for i in (a, bb):
                            v.wait_ge(S["aac"], 32 + 4 * p + (3 if i == a else 4))
                            v.tensor_scalar(nmr[i % 4][:], mv[i % 4][:, 0:1],
                                            rsd[i % 4][:, 0:1], -1.0,
                                            ALU.mult, ALU.mult).then_inc(S["avr"], 1)
                        for i in (a, bb):
                            for j4 in range(1, 8, 2):
                                g_id = 8 * i + j4
                                v.wait_ge(S["atm"], 128 + g_id + 1)
                                v.tensor_copy(ev_dst(i, j4),
                                              ps_t[g_id % 4][:]).then_inc(S["aevv"], 1)

                @blk.scalar
                def _(s):
                    for p in range(8):
                        a, bb = 2 * p, 2 * p + 1
                        for i in (a, bb):
                            s.wait_ge(S["avs"], 144 + 18 * p + 17 + (i - a))
                            s.activation(lnv[i % 4][:], mv[i % 4][:, 1:2], AF.Ln,
                                         bias=EPS, scale=1.0).then_inc(S["aac"], 1)
                        for i in (a, bb):
                            s.wait_ge(S["aac"], 32 + 4 * p + 2)
                            s.activation(rsd[i % 4][:], lnv[i % 4][:], AF.Exp,
                                         bias=0.0, scale=-0.5).then_inc(S["aac"], 1)
                        for i in (a, bb):
                            s.wait_ge(S["avr"], 16 + 2 * p + 1 + (i - a))
                            if i >= 2:
                                s.wait_ge(S["atm"], 128 + 8 * (i - 1))
                            s.activation(xn_sb[i % 2][:], x_sb[i % 3][:], AF.Identity,
                                         bias=nmr[i % 4][:, 0:1],
                                         scale=rsd[i % 4][:, 0:1]).then_inc(S["aap"], 1)
                        for i in (a, bb):
                            for j4 in range(0, 8, 2):
                                g_id = 8 * i + j4
                                s.wait_ge(S["atm"], 128 + g_id + 1)
                                s.activation(ev_dst(i, j4), ps_t[g_id % 4][:], AF.Identity,
                                             bias=0.0, scale=1.0).then_inc(S["aev"], 1)

                @blk.tensor
                def _(t):
                    t.wait_ge(S["bhev"], 32)
                    for i in range(16):
                        t.wait_ge(S["aap"], 16 + i + 1)
                        for j4 in range(8):
                            g_id = 8 * i + j4
                            if g_id >= 4:
                                if g_id % 2 == 0:
                                    t.wait_ge(S["aev"], 64 + g_id // 2 - 1)
                                else:
                                    t.wait_ge(S["aevv"], 64 + (g_id - 1) // 2 - 1)
                            for jj in range(4):
                                j = j4 * 4 + jj
                                mm = t.matmul(ps_t[g_id % 4][:, jj * 128:(jj + 1) * 128],
                                              xn_sb[i % 2][:, j * 128:(j + 1) * 128],
                                              idb[:], start=True, stop=True)
                                if jj == 3:
                                    mm.then_inc(S["atm"], 1)

    if upto >= 6 and not exchange:
        with ExitStack() as st:
            wqk_f = [st.enter_context(nc.sbuf_tensor(f"pwqk_f{i}", [128, 1024], F32)) for i in range(2)]
            wqk_b = [st.enter_context(nc.sbuf_tensor(f"pwqk_b{i}", [128, 4096], BF16)) for i in range(2)]
            stg_k = [st.enter_context(nc.sbuf_tensor(f"pstg_k{i}", [128, 512], BF16)) for i in range(2)]
            vq_f = [st.enter_context(nc.sbuf_tensor(f"pvq_f{i}", [128, 2048], F32)) for i in range(2)]
            v_bf = st.enter_context(nc.sbuf_tensor("pv_bf", [128, 16384], BF16))
            stg_v = [st.enter_context(nc.sbuf_tensor(f"pstg_v{i}", [128, 512], BF16)) for i in range(2)]
            ps_qk = [st.enter_context(nc.psum_tensor(f"pps_qk{i}", [128, 512], F32)) for i in range(2)]
            ps_v = [st.enter_context(nc.psum_tensor(f"pps_v{i}", [128, 512], F32)) for i in range(2)]

            with nc.named_scope("phCp"), nc.Block() as blk:
                @blk.sync
                def _(sy):
                    sy.wait_ge(S["aap"], 32)
                    sy.wait_ge(S["atm"], 256)
                    sy.wait_ge(S["avs"], 288)
                    for c in range(32):
                        k, qt = divmod(c, 4)
                        if c >= 2:
                            sy.wait_ge(S["cwb"], 64 + c - 1)
                        sy.dma_start(out=wqk_f[c % 2][:],
                                     in_=wqk[8 + k, :, qt * 1024:(qt + 1) * 1024]
                                     ).then_inc(S[f"cwf{c % 2}"], 16)
                    for c2 in range(16):
                        sy.wait_ge(S["cvb"], 16 + max(0, c2 - 1))
                        sy.dma_start(out=vq_f[c2 % 2][:],
                                     in_=wv[c2 // 8, :, (c2 % 8) * 2048:(c2 % 8 + 1) * 2048]
                                     ).then_inc(S[f"cvf{c2 % 2}"], 16)

                @blk.tensor
                def _(t):
                    t.wait_ge(S["aev"], 128)
                    t.wait_ge(S["aevv"], 128)
                    for k in range(8):
                        t.wait_ge(S["cwb"], 64 + 4 * (k + 1))
                        for ncc in range(4):
                            g_id = 4 * k + ncc
                            if g_id >= 2:
                                t.wait_ge(S["ckev"], 32 + g_id - 1)
                            for ct in range(32):
                                mm = t.matmul(ps_qk[g_id % 2][:],
                                              wqk_b[k % 2][:, ct * 128:(ct + 1) * 128],
                                              xnT[:, ct * 2048 + ncc * 512:
                                                  ct * 2048 + (ncc + 1) * 512],
                                              start=(ct == 0), stop=(ct == 31))
                                if ct == 31:
                                    mm.then_inc(S["ckm"], 1)
                    for vs in range(2):
                        t.wait_ge(S["cvb"], 16 + 8 * (vs + 1))
                        for nt in range(16):
                            gv = 16 * vs + nt
                            if gv >= 2:
                                t.wait_ge(S["cvev"], 32 + gv - 1)
                            for ct in range(32):
                                mm = t.matmul(ps_v[gv % 2][:],
                                              xnT[:, ct * 2048 + nt * 128:
                                                  ct * 2048 + nt * 128 + 128],
                                              v_bf[:, ct * 512:(ct + 1) * 512],
                                              start=(ct == 0), stop=(ct == 31))
                                if ct == 31:
                                    mm.then_inc(S["cvm"], 1)

                @blk.vector
                def _(v):
                    for c in range(32):
                        k, qt = divmod(c, 4)
                        v.wait_ge(S[f"cwf{c % 2}"], 16 * (32 + c // 2 + 1))
                        if k >= 2 and qt == 0:
                            v.wait_ge(S["ckm"], 32 + 4 * (k - 1))
                        v.tensor_copy(wqk_b[k % 2][:, qt * 1024:(qt + 1) * 1024],
                                      wqk_f[c % 2][:]).then_inc(S["cwb"], 1)
                    for c2 in range(8):
                        v.wait_ge(S[f"cvf{c2 % 2}"], 16 * (8 + c2 // 2 + 1))
                        v.tensor_copy(v_bf[:, c2 * 2048:(c2 + 1) * 2048],
                                      vq_f[c2 % 2][:]).then_inc(S["cvb"], 1)
                    for gv in range(16):
                        v.wait_ge(S["cvm"], 32 + gv + 1)
                        if gv >= 2:
                            v.wait_ge(S[f"cvst{gv % 2}"], 16 * (16 + gv // 2))
                        v.scalar_tensor_tensor(stg_v[gv % 2][:], ps_v[gv % 2][:], 1.0,
                                               bv_b[:, 0:512],
                                               ALU.mult, ALU.add).then_inc(S["cvev"], 1)
                    for c2 in range(8, 16):
                        v.wait_ge(S[f"cvf{c2 % 2}"], 16 * (8 + c2 // 2 + 1))
                        if c2 == 8:
                            v.wait_ge(S["cvm"], 48)
                        v.tensor_copy(v_bf[:, (c2 - 8) * 2048:(c2 - 7) * 2048],
                                      vq_f[c2 % 2][:]).then_inc(S["cvb"], 1)
                    for gv in range(16, 32):
                        v.wait_ge(S["cvm"], 32 + gv + 1)
                        v.wait_ge(S[f"cvst{gv % 2}"], 16 * (16 + gv // 2))
                        v.scalar_tensor_tensor(stg_v[gv % 2][:], ps_v[gv % 2][:], 1.0,
                                               bv_b[:, 512:1024],
                                               ALU.mult, ALU.add).then_inc(S["cvev"], 1)

                @blk.scalar
                def _(s):
                    for k in range(8):
                        for ncc in range(4):
                            g_id = 4 * k + ncc
                            s.wait_ge(S["ckm"], 32 + g_id + 1)
                            if g_id >= 2:
                                s.wait_ge(S[f"ckst{g_id % 2}"], 16 * (16 + g_id // 2))
                            s.activation(stg_k[g_id % 2][:], ps_qk[g_id % 2][:],
                                         AF.Identity, bias=bqk[:, 8 + k:8 + k + 1],
                                         scale=1.0).then_inc(S["ckev"], 1)
                            s.wait_ge(S["ckev"], 32 + g_id + 1)
                            s.dma_start(out=ap2(k_loc, k * 128 * KT + NTOK + ncc * 512,
                                                [[KT, 128], [1, 512]]),
                                        in_=stg_k[g_id % 2][:]).then_inc(S[f"ckst{g_id % 2}"], 16)
                    for vs in range(2):
                        for nt in range(16):
                            gv = 16 * vs + nt
                            s.wait_ge(S["cvev"], 32 + gv + 1)
                            s.dma_start(out=ap2(v_loc, (NTOK + nt * 128) * DH + vs * 512,
                                                [[DH, 128], [1, 512]]),
                                        in_=stg_v[gv % 2][:]).then_inc(S[f"cvst{gv % 2}"], 16)

    s1.close()

    s2 = ExitStack()
    if upto >= 6:
        qsb = s2.enter_context(nc.sbuf_tensor("qsb", [128, 16384], BF16))
        if not exchange:
            ksb = s2.enter_context(nc.sbuf_tensor("ksb", [128, 32768], BF16))
        with ExitStack() as st:
            hgs = st.enter_context(nc.sbuf_tensor("hgs", [128, 16384], BF16))
            bo_b = st.enter_context(nc.sbuf_tensor("bo_b", [128, 4096], F32))
            bor = st.enter_context(nc.sbuf_tensor("bor_s", [1, 4096], F32))
            bf_f = [st.enter_context(nc.sbuf_tensor(f"bf_f{i}", [128, 1024], F32)) for i in range(2)]
            bf_b = [st.enter_context(nc.sbuf_tensor(f"bf_b{i}", [128, 4096], BF16)) for i in range(2)]
            stg_f = [st.enter_context(nc.sbuf_tensor(f"stg_f{i}", [128, 512], BF16)) for i in range(2)]
            ps_f = [st.enter_context(nc.psum_tensor(f"ps_f{i}", [128, 512], F32)) for i in range(2)]
            ps_bo = st.enter_context(nc.psum_tensor("ps_bo", [128, 512], F32))

            with nc.named_scope("phB2"), nc.Block() as blk:
                @blk.sync
                def _(sy):
                    sy.wait_ge(S["bhst0"], 16 * 16)
                    sy.wait_ge(S["bhst1"], 16 * 16)
                    sy.wait_ge(S["bhm"], 32)
                    sy.wait_ge(S["bgem"], 32)
                    if not exchange:
                        sy.wait_ge(S["aap"], 32)
                        sy.wait_ge(S["atm"], 256)
                        sy.wait_ge(S["avs"], 288)
                        sy.wait_ge(S["ckst0"], 16 * 32)
                        sy.wait_ge(S["ckst1"], 16 * 32)
                        sy.wait_ge(S["cvst0"], 16 * 32)
                        sy.wait_ge(S["cvst1"], 16 * 32)
                    sy.dma_start(out=bor[:], in_=bo_in[:, :]).then_inc(S["bbor"], 16)
                    sy.dma_start(out=ap2(hgs, 0, [[16384, 128], [2048, 8], [1, 2048]]),
                                 in_=ap2(hg_dr, 0, [[NTOK, 128], [128 * NTOK, 8], [1, NTOK]])
                                 ).then_inc(S["bhg"], 16)
                    for c in range(32):
                        cs, qt = divmod(c, 4)
                        if c == 8:
                            sy.wait_ge(S["cqst0"], 16 * 16)
                            sy.wait_ge(S["cqst1"], 16 * 16)
                            sy.dma_start(out=ap2(qsb, 0, [[16384, 128], [2048, 8], [1, 2048]]),
                                         in_=ap2(q_dr, 0,
                                                 [[NTOK, 128], [128 * NTOK, 8], [1, 2048]])
                                         ).then_inc(S["dq"], 16)
                            if not exchange:
                                sy.wait_ge(S["ckst0"], 16 * 32)
                                sy.wait_ge(S["ckst1"], 16 * 32)
                                for dt in range(8):
                                    sy.dma_start(
                                        out=ap2(ksb, dt * 4096, [[32768, 128], [1, 4096]]),
                                        in_=ap2(k_loc, dt * 128 * KT,
                                                [[KT, 128], [1, 4096]])).then_inc(S["dk"], 16)
                        if c >= 2:
                            sy.wait_ge(S["bfb"], c - 1)
                        sy.dma_start(out=bf_f[c % 2][:],
                                     in_=bf_w[cs, :, qt * 1024:(qt + 1) * 1024]).then_inc(S[f"bff{c % 2}"], 16)


                @blk.tensor
                def _(t):
                    t.wait_ge(S["bbor"], 16)
                    for cc in range(8):
                        if cc >= 1:
                            t.wait_ge(S["bboev"], cc)
                        t.matmul(ps_bo[:], onr[0:1, :], bor[0:1, cc * 512:(cc + 1) * 512],
                                 start=True, stop=True).then_inc(S["bbom"], 1)
                    t.wait_ge(S["bhg"], 16)
                    for cs in range(8):
                        t.wait_ge(S["bfb"], 4 * (cs + 1))
                        for nt in range(16):
                            gf = 16 * cs + nt
                            if gf >= 2:
                                t.wait_ge(S["bfev"], gf - 1)
                            for ert in range(8):
                                mm = t.matmul(ps_f[gf % 2][:],
                                              hgs[:, ert * 2048 + nt * 128:
                                                  ert * 2048 + nt * 128 + 128],
                                              bf_b[cs % 2][:, ert * 512:(ert + 1) * 512],
                                              start=(ert == 0), stop=(ert == 7))
                                if ert == 7:
                                    mm.then_inc(S["bfm"], 1)

                @blk.gpsimd
                def _(g):
                    for c in range(32):
                        cs, qt = divmod(c, 4)
                        g.wait_ge(S[f"bff{c % 2}"], 16 * (c // 2 + 1))
                        if cs >= 2 and qt == 0:
                            g.wait_ge(S["bfm"], 16 * (cs - 1))
                        g.tensor_copy(bf_b[cs % 2][:, qt * 1024:(qt + 1) * 1024],
                                      bf_f[c % 2][:]).then_inc(S["bfb"], 1)

                @blk.vector
                def _(v):
                    for cc in range(8):
                        v.wait_ge(S["bbom"], cc + 1)
                        v.tensor_copy(bo_b[:, cc * 512:(cc + 1) * 512],
                                      ps_bo[:]).then_inc(S["bboev"], 1)
                    for gf in range(128):
                        cs, nt = divmod(gf, 16)
                        v.wait_ge(S["bboev"], 8)
                        v.wait_ge(S["bfm"], gf + 1)
                        if gf >= 2:
                            v.wait_ge(S[f"bfst{gf % 2}"], 16 * (gf // 2))
                        v.scalar_tensor_tensor(stg_f[gf % 2][:], ps_f[gf % 2][:], 1.0,
                                               bo_b[:, cs * 512:(cs + 1) * 512],
                                               ALU.mult, ALU.add).then_inc(S["bfev"], 1)

                @blk.scalar
                def _(s):
                    for gf in range(128):
                        cs, nt = divmod(gf, 16)
                        s.wait_ge(S["bfev"], gf + 1)
                        s.dma_start(out=ap2(ffn_dr, nt * 128 * DM + cs * 512,
                                            [[DM, 128], [1, 512]]),
                                    in_=stg_f[gf % 2][:]).then_inc(S[f"bfst{gf % 2}"], 16)

    if tap == "ffn":
        with nc.Block() as blk:
            @blk.sync
            def _(sy):
                sy.wait_ge(S["bfst0"], 16 * 64)
                sy.wait_ge(S["bfst1"], 16 * 64)
                sy.dma_start(out=taps["ffn"][:, :], in_=ffn_dr[:, :]).then_inc(S["tapd"], 16)
                sy.wait_ge(S["tapd"], 16)

    if upto >= 7:
        with ExitStack() as st:
            if exchange:
                ksb = st.enter_context(nc.sbuf_tensor("ksb", [128, 32768], BF16))
            vsb = st.enter_context(nc.sbuf_tensor("vsb", [128, 32768], BF16))
            pts = st.enter_context(nc.sbuf_tensor("pts", [128, 16384], BF16))
            zrow = st.enter_context(nc.sbuf_tensor("zrow", [1, 512], F32))
            zb_sb = st.enter_context(nc.sbuf_tensor("zb_sb", [128, 512], F32))
            stg_pv = [st.enter_context(nc.sbuf_tensor(f"stg_pv{i}", [128, 512], BF16)) for i in range(2)]
            ps_s = [st.enter_context(nc.psum_tensor(f"ps_s{i}", [128, 512], F32)) for i in range(2)]
            ps_z = st.enter_context(nc.psum_tensor("ps_z", [1, 512], F32))
            ps_zb = st.enter_context(nc.psum_tensor("ps_zb", [128, 512], F32))
            ps_pv = [st.enter_context(nc.psum_tensor(f"ps_pv{i}", [128, 512], F32)) for i in range(2)]

            with nc.named_scope("phD"), nc.Block() as blk:
                if exchange:
                    @blk.gpsimd
                    def _(g):
                        with g.register("rn_d") as rn, g.register("rf0_d") as rf0, \
                                g.register("rf1_d") as rf1, g.register("rt0_d") as rt0, \
                                g.register("rt1_d") as rt1, g.register("rc_d") as rc:
                            g.wait_ge(S["cflw"], 48)
                            g.reg_load(rn, snon[0:1, 0:1])
                            g.reg_mov(rc, 1)
                            with g.While(rc):
                                g.load(rf0, fl_sh[0:1, 0:1])
                                g.load(rf1, fl_sh[1:2, 0:1])
                                g.reg_alu(rt0, rf0, rn, ALU.not_equal)
                                g.reg_alu(rt1, rf1, rn, ALU.not_equal)
                                g.reg_alu(rc, rt0, rt1, ALU.bitwise_or)
                        g.sem_inc(S["dkv"], 1)

                @blk.sync
                def _(sy):
                    sy.wait_ge(S["bfst0"], 16 * 64)
                    sy.wait_ge(S["bfst1"], 16 * 64)
                    sy.wait_ge(S["bfm"], 128)
                    sy.wait_ge(S["bfev"], 128)
                    if exchange:
                        sy.wait_ge(S["dkv"], 1)
                        for blkh in range(2):
                            for dt in range(8):
                                sy.dma_start(
                                    out=ap2(ksb, dt * 4096 + blkh * 2048,
                                            [[32768, 128], [1, 2048]]),
                                    in_=ap2(k_sh, blkh * DH * NTOK + dt * 128 * NTOK,
                                            [[NTOK, 128], [1, 2048]])).then_inc(S["dk"], 16)
                        for blkh in range(2):
                            sy.dma_start(
                                out=ap2(vsb, blkh * 16 * 1024,
                                        [[32768, 128], [1024, 16], [1, 1024]]),
                                in_=ap2(v_sh, blkh * NTOK * DH,
                                        [[DH, 128], [128 * DH, 16], [1, 1024]])).then_inc(S["dv"], 16)
                    else:
                        sy.wait_ge(S["cvst0"], 16 * 32)
                        sy.wait_ge(S["cvst1"], 16 * 32)
                        sy.dma_start(
                            out=ap2(vsb, 0, [[32768, 128], [1024, 32], [1, 1024]]),
                            in_=ap2(v_loc, 0,
                                    [[DH, 128], [128 * DH, 32], [1, 1024]])).then_inc(S["dv"], 16)

                @blk.tensor
                def _(t):
                    t.wait_ge(S["dk"], 16 * (16 if exchange else 8))
                    t.wait_ge(S["dq"], 16)
                    for qc in range(4):
                        for kt in range(32):
                            gs = 32 * qc + kt
                            if gs >= 2:
                                t.wait_ge(S["dexp"], gs - 1)
                            for dt in range(8):
                                mm = t.matmul(ps_s[gs % 2][:],
                                              ksb[:, dt * 4096 + kt * 128:
                                                  dt * 4096 + kt * 128 + 128],
                                              qsb[:, dt * 2048 + qc * 512:
                                                  dt * 2048 + (qc + 1) * 512],
                                              start=(dt == 0), stop=(dt == 7))
                                if dt == 7:
                                    mm.then_inc(S["dsm"], 1)
                        if qc >= 1:
                            t.wait_ge(S["dzr"], qc)
                        for kt in range(32):
                            gs = 32 * qc + kt
                            t.wait_ge(S["dexp"], gs + 1)
                            mm = t.matmul(ps_z[:], onc[:, 0:1],
                                          pts[:, kt * 512:(kt + 1) * 512],
                                          start=(kt == 0), stop=(kt == 31))
                            if kt == 31:
                                mm.then_inc(S["dzm"], 1)
                        t.wait_ge(S["dzr"], qc + 1)
                        if qc >= 1:
                            t.wait_ge(S["dzbe"], qc)
                        t.matmul(ps_zb[:], onr[0:1, :], zrow[0:1, :],
                                 start=True, stop=True).then_inc(S["dzbm"], 1)
                        if qc == 0:
                            t.wait_ge(S["dv"], 16 * (2 if exchange else 1))
                        for dt in range(8):
                            gp = 8 * qc + dt
                            if gp >= 2:
                                t.wait_ge(S["dpve"], gp - 1)
                            for kt in range(32):
                                mm = t.matmul(ps_pv[gp % 2][:],
                                              vsb[:, kt * 1024 + dt * 128:
                                                  kt * 1024 + dt * 128 + 128],
                                              pts[:, kt * 512:(kt + 1) * 512],
                                              start=(kt == 0), stop=(kt == 31))
                                if kt == 31:
                                    mm.then_inc(S["dpvm"], 1)

                @blk.scalar
                def _(s):
                    s.wait_ge(S["bfst0"], 16 * 64)
                    s.wait_ge(S["bfst1"], 16 * 64)
                    for qc in range(4):
                        for kt in range(32):
                            gs = 32 * qc + kt
                            s.wait_ge(S["dsm"], gs + 1)
                            s.wait_ge(S["dpvm"], 8 * qc)
                            s.activation(pts[:, kt * 512:(kt + 1) * 512], ps_s[gs % 2][:],
                                         AF.Exp, bias=0.0, scale=0.03125).then_inc(S["dexp"], 1)
                        for dt in range(8):
                            gp = 8 * qc + dt
                            s.wait_ge(S["dpve"], gp + 1)
                            s.dma_start(out=ap2(pv_dr, dt * 128 * NTOK + qc * 512,
                                                [[NTOK, 128], [1, 512]]),
                                        in_=stg_pv[gp % 2][:]).then_inc(S[f"dpvst{gp % 2}"], 16)

                @blk.vector
                def _(v):
                    for qc in range(4):
                        v.wait_ge(S["dzm"], qc + 1)
                        v.reciprocal(zrow[0:1, :], ps_z[0:1, :]).then_inc(S["dzr"], 1)
                        v.wait_ge(S["dzbm"], qc + 1)
                        if qc >= 1:
                            v.wait_ge(S["dpve"], 8 * qc)
                        v.tensor_copy(zb_sb[:], ps_zb[:]).then_inc(S["dzbe"], 1)
                        for dt in range(8):
                            gp = 8 * qc + dt
                            v.wait_ge(S["dpvm"], gp + 1)
                            v.wait_ge(S["dzbe"], qc + 1)
                            if gp >= 2:
                                v.wait_ge(S[f"dpvst{gp % 2}"], 16 * (gp // 2))
                            v.scalar_tensor_tensor(stg_pv[gp % 2][:], ps_pv[gp % 2][:], 1.0,
                                                   zb_sb[:], ALU.mult, ALU.mult
                                                   ).then_inc(S["dpve"], 1)

    if tap == "pv":
        with nc.Block() as blk:
            @blk.sync
            def _(sy):
                sy.wait_ge(S["dpvst0"], 16 * 16)
                sy.wait_ge(S["dpvst1"], 16 * 16)
                sy.dma_start(out=taps["pv"][:, :], in_=pv_dr[:, :]).then_inc(S["tapd"], 16)
                sy.wait_ge(S["tapd"], 16)

    s2.close()

    if upto >= 8:
        with ExitStack() as st:
            pvs = st.enter_context(nc.sbuf_tensor("pvs", [128, 16384], BF16))
            wo_f = [st.enter_context(nc.sbuf_tensor(f"wo_f{i}", [128, 1024], F32)) for i in range(2)]
            wo_b = st.enter_context(nc.sbuf_tensor("wo_b", [128, 32768], BF16))
            xt = [st.enter_context(nc.sbuf_tensor(f"xt{i}", [128, 4096], F32)) for i in range(2)]
            ft = [st.enter_context(nc.sbuf_tensor(f"ft{i}", [128, 4096], BF16)) for i in range(2)]
            xf = [st.enter_context(nc.sbuf_tensor(f"xf{i}", [128, 4096], F32)) for i in range(2)]
            ystg = [st.enter_context(nc.sbuf_tensor(f"ystg{i}", [128, 1024], F32)) for i in range(2)]
            ps_p4 = st.enter_context(nc.psum_tensor("ps_p4", [128, 2048], F32))

            with nc.named_scope("phE"), nc.Block() as blk:
                @blk.gpsimd
                def _(g):
                    g.wait_ge(S["dpvm"], 32)
                    g.wait_ge(S["dpve"], 32)

                    def ld_nt(nt):
                        if nt >= 2:
                            g.wait_ge(S["exf"], nt - 1)
                            g.wait_ge(S["exfp"], nt - 1)
                        g.dma_start(out=xt[nt % 2][:],
                                    in_=x_in[nt * 128:(nt + 1) * 128, :]
                                    ).then_inc(S[f"exc{nt % 2}"], 16)
                        g.dma_start(out=ft[nt % 2][:],
                                    in_=ffn_dr[nt * 128:(nt + 1) * 128, :]
                                    ).then_inc(S[f"efc{nt % 2}"], 16)
                    ld_nt(0)
                    ld_nt(1)
                    for nt in range(16):
                        g.wait_ge(S[f"exc{nt % 2}"], 16 * (nt // 2 + 1))
                        g.wait_ge(S[f"efc{nt % 2}"], 16 * (nt // 2 + 1))
                        if nt >= 2:
                            g.wait_ge(S["et1"], 4 * (nt - 1))
                        g.tensor_tensor(xf[nt % 2][:, 0:3072], xt[nt % 2][:, 0:3072],
                                        ft[nt % 2][:, 0:3072],
                                        ALU.add).then_inc(S["exfp"], 1)
                        if nt + 2 < 16:
                            ld_nt(nt + 2)

                @blk.sync
                def _(sy):
                    sy.wait_ge(S["dpvm"], 32)
                    sy.wait_ge(S["dpve"], 32)
                    for c in range(2):
                        cs, qt = divmod(c, 4)
                        sy.dma_start(out=wo_f[c % 2][:],
                                     in_=wo_w[cs, :, qt * 1024:(qt + 1) * 1024]
                                     ).then_inc(S[f"ewf{c % 2}"], 16)
                    for c in range(2, 32):
                        cs, qt = divmod(c, 4)
                        sy.wait_ge(S["ewb"], c - 1)
                        sy.dma_start(out=wo_f[c % 2][:],
                                     in_=wo_w[cs, :, qt * 1024:(qt + 1) * 1024]
                                     ).then_inc(S[f"ewf{c % 2}"], 16)


                @blk.tensor
                def _(t):
                    for nt in range(16):
                        if nt % 4 == 0:
                            ch = nt // 4
                            t.wait_ge(S["ep" if ch == 0 else f"ep{ch}"], 16)
                        for cs in range(8):
                            gy = 8 * nt + cs
                            if nt > 0:
                                t.wait_ge(S["ewb"], 4 * (cs + 1))
                            if gy >= 4:
                                t.wait_ge(S["et1"], gy // 2 - 1)
                            for dt in range(8):
                                if nt == 0 and dt % 2 == 0:
                                    t.wait_ge(S["ewb"], 4 * cs + dt // 2 + 1)
                                mm = t.matmul(ps_p4[:, (gy % 4) * 512:(gy % 4 + 1) * 512],
                                              pvs[:, dt * 2048 + nt * 128:
                                                  dt * 2048 + nt * 128 + 128],
                                              wo_b[:, cs * 4096 + dt * 512:
                                                   cs * 4096 + (dt + 1) * 512],
                                              start=(dt == 0), stop=(dt == 7))
                                if dt == 7:
                                    mm.then_inc(S["epm"], 1)

                @blk.vector
                def _(v):
                    def wocast(c):
                        v.wait_ge(S[f"ewf{c % 2}"], 16 * (c // 2 + 1))
                        v.tensor_copy(wo_b[:, c * 1024:(c + 1) * 1024],
                                      wo_f[c % 2][:]).then_inc(S["ewb"], 1)

                    def mkxf(nt):
                        v.wait_ge(S[f"exc{nt % 2}"], 16 * (nt // 2 + 1))
                        v.wait_ge(S[f"efc{nt % 2}"], 16 * (nt // 2 + 1))
                        if nt >= 2:
                            v.wait_ge(S["et1"], 4 * (nt - 1))
                        v.tensor_tensor(xf[nt % 2][:, 3072:4096], xt[nt % 2][:, 3072:4096],
                                        ft[nt % 2][:, 3072:4096],
                                        ALU.add).then_inc(S["exf"], 1)

                    for c in range(8):
                        wocast(c)
                    mkxf(0)
                    mkxf(1)
                    for c in range(8, 12):
                        wocast(c)
                    inj = {0: range(12, 20), 1: range(20, 28), 2: range(28, 32)}
                    for e in range(64):
                        for c in inj.get(e, ()):
                            wocast(c)
                        gy = 2 * e
                        nt, cs = divmod(gy, 8)
                        v.wait_ge(S["epm"], gy + 2)
                        v.wait_ge(S["exf"], nt + 1)
                        v.wait_ge(S["exfp"], nt + 1)
                        if e >= 2:
                            v.wait_ge(S[f"eyst{e % 2}"], 16 * (e // 2))
                        v.scalar_tensor_tensor(ystg[e % 2][:],
                                               ps_p4[:, (gy % 4) * 512:(gy % 4) * 512 + 1024],
                                               1.0,
                                               xf[nt % 2][:, cs * 512:(cs + 2) * 512],
                                               ALU.mult, ALU.add).then_inc(S["et1"], 1)
                        if cs == 6 and nt + 2 < 16:
                            mkxf(nt + 2)

                @blk.scalar
                def _(s):
                    s.wait_ge(S["dpvm"], 32)
                    s.wait_ge(S["dpve"], 32)
                    for c in range(4):
                        s.wait_ge(S["dpvst0"], 16 * 4 * (c + 1))
                        s.wait_ge(S["dpvst1"], 16 * 4 * (c + 1))
                        s.dma_start(out=ap2(pvs, c * 512,
                                            [[16384, 128], [2048, 8], [1, 512]]),
                                    in_=ap2(pv_dr, c * 512,
                                            [[NTOK, 128], [128 * NTOK, 8], [1, 512]])
                                    ).then_inc(S["ep" if c == 0 else f"ep{c}"], 16)
                    for e in range(64):
                        gy = 2 * e
                        nt, cs = divmod(gy, 8)
                        s.wait_ge(S["et1"], e + 1)
                        s.dma_start(out=ap2(y_out, nt * 128 * DM + cs * 512,
                                            [[DM, 128], [1, 1024]]),
                                    in_=ystg[e % 2][:]).then_inc(S[f"eyst{e % 2}"], 16)
                    for q in range(2):
                        s.wait_ge(S[f"eyst{q}"], 16 * 32)

    main.close()
    nc.finalize()
    return nc, sorted(taps.keys())


def _host_layout(inputs):
    Wqkv = np.ascontiguousarray(np.asarray(inputs["L0_attn_Wqkv"], dtype=np.float32))
    bqkv = np.asarray(inputs["L0_attn_bqkv"], dtype=np.float32)
    Wo = np.ascontiguousarray(np.asarray(inputs["L0_attn_Wo"], dtype=np.float32))
    bo = np.asarray(inputs["L0_attn_bo"], dtype=np.float32)
    A = np.asarray(inputs["L0_ffn_A"], dtype=np.float32)
    B_ = np.asarray(inputs["L0_ffn_B"], dtype=np.float32)
    Wr = np.asarray(inputs["L0_ffn_Wr"], dtype=np.float32)

    wqk = np.ascontiguousarray(
        Wqkv[:, :2048].reshape(32, 128, 16, 128).transpose(2, 1, 0, 3).reshape(16, 128, 4096))
    wv = np.ascontiguousarray(
        Wqkv[:, 2048:].reshape(32, 128, 2, 512).transpose(2, 1, 0, 3).reshape(2, 128, 16384))
    wrl = np.ascontiguousarray(
        Wr.T.reshape(32, 128, 64).transpose(1, 0, 2).reshape(128, 2048))
    at = np.ascontiguousarray(
        A.reshape(1024, 4096).T.reshape(32, 128, 8, 128).transpose(2, 1, 0, 3).reshape(8, 128, 4096))
    bfl = np.ascontiguousarray(
        B_.transpose(0, 2, 1).reshape(1024, 4096).reshape(8, 128, 8, 512)
        .transpose(2, 1, 0, 3).reshape(8, 128, 4096))
    wo = np.ascontiguousarray(
        Wo.reshape(8, 128, 8, 512).transpose(2, 1, 0, 3).reshape(8, 128, 4096))
    bqk = np.ascontiguousarray(bqkv[:2048].reshape(16, 128).T)
    bvr = np.ascontiguousarray(bqkv[2048:].reshape(1, 1024))
    bor = np.ascontiguousarray(bo.reshape(1, 4096))

    idb = np.eye(128, dtype=BF16NP)
    idf = np.eye(128, dtype=np.float32)
    onc = np.ones((128, 1), dtype=BF16NP)
    onr = np.ones((1, 128), dtype=np.float32)
    emat = np.zeros((64, 1024), dtype=BF16NP)
    for e in range(64):
        emat[e, e * 16:(e + 1) * 16] = 1
    return dict(wqk=wqk, wv=wv, wr=wrl, at=at, bfl=bfl, wo=wo, bqk=bqk,
                bvr=bvr, bor=bor, idb=idb, idf=idf, onc=onc, onr=onr, emat=emat)


EXCHANGE = True


def kernel(**inputs):
    x = np.asarray(inputs["L0_attn_x"], dtype=np.float32)
    common = _host_layout(inputs)
    if EXCHANGE:
        common["non"] = np.full((1, 1), np.random.randint(1, 2**31 - 1), dtype=np.int32)
    in_maps = []
    for c in range(8):
        b, h = c // 2, c % 2
        m = dict(common)
        m["x"] = np.ascontiguousarray(x[b, h * NTOK:(h + 1) * NTOK, :])
        if not EXCHANGE:
            m["xp"] = np.ascontiguousarray(x[b, (1 - h) * NTOK:(2 - h) * NTOK, :])
        in_maps.append(m)
    nc, _ = build_nc(exchange=EXCHANGE)
    res = run_bass_kernel_spmd(nc, in_maps, core_ids=list(range(8)))
    y = np.empty((4, 4096, 4096), dtype=np.float32)
    for c in range(8):
        b, h = c // 2, c % 2
        y[b, h * NTOK:(h + 1) * NTOK, :] = res.results[c]["y"]
    return y


# version 18
# speedup vs baseline: 1.2862x; 1.0087x over previous
import numpy as np
import ml_dtypes
from contextlib import ExitStack

import concourse.bass as bass
import concourse.mybir as mybir
from concourse.bass_utils import run_bass_kernel_spmd

F32 = mybir.dt.float32
BF16 = mybir.dt.bfloat16
I32 = mybir.dt.int32
AF = mybir.ActivationFunctionType
ALU = mybir.AluOpType
AX = mybir.AxisListType
BF16NP = ml_dtypes.bfloat16

NTOK = 2048
DM = 4096
DH = 1024
EPS = 1e-5


def ap2(t, off, dims):
    return bass.AP(t, off, dims)


def build_nc(tap=None, detect_races=False, exchange=False):
    upto = {"xnt": 1, "kvq": 3, "rg": 4, "hg": 5, "ffn": 6, "pv": 7}.get(tap, 8)
    KT = NTOK if exchange else 2 * NTOK
    nc = bass.Bass("TRN2", target_bir_lowering=False, debug=False,
                   detect_race_conditions=detect_races,
                   dynamic_dma_scratch_size=4096)

    x_in = nc.dram_tensor("x", [NTOK, DM], F32, kind="ExternalInput")
    if not exchange:
        xp_in = nc.dram_tensor("xp", [NTOK, DM], F32, kind="ExternalInput")
    wqk = nc.dram_tensor("wqk", [16, 128, 4096], F32, kind="ExternalInput")
    wv = nc.dram_tensor("wv", [2, 128, 16384], F32, kind="ExternalInput")
    wr = nc.dram_tensor("wr", [128, 2048], F32, kind="ExternalInput")
    at_w = nc.dram_tensor("at", [8, 128, 4096], F32, kind="ExternalInput")
    bf_w = nc.dram_tensor("bfl", [8, 128, 4096], F32, kind="ExternalInput")
    wo_w = nc.dram_tensor("wo", [8, 128, 4096], F32, kind="ExternalInput")
    bqk_in = nc.dram_tensor("bqk", [128, 16], F32, kind="ExternalInput")
    bv_in = nc.dram_tensor("bvr", [1, 1024], F32, kind="ExternalInput")
    bo_in = nc.dram_tensor("bor", [1, 4096], F32, kind="ExternalInput")
    idb_in = nc.dram_tensor("idb", [128, 128], BF16, kind="ExternalInput")
    idf_in = nc.dram_tensor("idf", [128, 128], F32, kind="ExternalInput")
    onc_in = nc.dram_tensor("onc", [128, 1], BF16, kind="ExternalInput")
    onr_in = nc.dram_tensor("onr", [1, 128], F32, kind="ExternalInput")
    em_in = nc.dram_tensor("emat", [64, 1024], BF16, kind="ExternalInput")
    non_in = (nc.dram_tensor("non", [1, 1], I32, kind="ExternalInput")
              if exchange else None)
    y_out = nc.dram_tensor("y", [NTOK, DM], F32, kind="ExternalOutput")

    q_dr = nc.dram_tensor("q_dr", [DH, NTOK], BF16)
    k_loc = nc.dram_tensor("k_loc", [DH, KT], BF16)
    v_loc = nc.dram_tensor("v_loc", [KT, DH], BF16)
    pv_dr = nc.dram_tensor("pv_dr", [DH, NTOK], BF16)
    hg_dr = nc.dram_tensor("hg_dr", [1024, NTOK], BF16)
    ffn_dr = nc.dram_tensor("ffn_dr", [NTOK, DM], BF16)
    wo_bfd = nc.dram_tensor("wo_bfd", [8, 128, 4096], BF16)
    if exchange:
        k_sh = nc.dram_tensor("k_sh", [2, DH, NTOK], BF16, addr_space="Shared")
        v_sh = nc.dram_tensor("v_sh", [2, NTOK, DH], BF16, addr_space="Shared")
        fl_sh = nc.dram_tensor("fl_sh", [2, 128], I32, addr_space="Shared")

    taps = {}

    def tap_out(name, shape, dt):
        taps[name] = nc.dram_tensor("t_" + name, shape, dt, kind="ExternalOutput")

    if tap == "xnt":
        tap_out("xnt", [128, 65536], BF16)
    elif tap == "kvq":
        tap_out("k", [2 * DH, NTOK], BF16)
        tap_out("v", [2 * NTOK, DH], BF16)
        tap_out("q", [DH, NTOK], BF16)
    elif tap == "rg":
        tap_out("sr", [128, 1024], F32)
        tap_out("g", [128, 1024], BF16)
    elif tap == "hg":
        tap_out("hg", [1024, NTOK], BF16)
    elif tap == "ffn":
        tap_out("ffn", [NTOK, DM], BF16)
    elif tap == "pv":
        tap_out("pv", [DH, NTOK], BF16)

    main = ExitStack()

    S = {}
    base_sems = ['cst', 'ini', 'ax0', 'ax1', 'ax2', 'avs', 'avr', 'aac', 'aap', 'atm', 'aev', 'aevv', 'abvm', 'abvev', 'cwf0', 'cwf1', 'cwb', 'ckm', 'ckev', 'ckst0', 'ckst1', 'cqm', 'cqev', 'cqst0', 'cqst1', 'cvf0', 'cvf1', 'cvb', 'cvm', 'cvev', 'cvst0', 'cvst1', 'brf', 'brb', 'brm', 'brev', 'brtm', 'brtev', 'bdv', 'bpl', 'bac', 'bgm', 'bgev', 'bgem', 'bgeev', 'baf0', 'baf1', 'bab', 'bhm', 'bhev', 'bhst0', 'bhst1', 'bhg', 'bbor', 'bbom', 'bboev', 'bff0', 'bff1', 'bfb', 'bfm', 'bfev', 'bfst0', 'bfst1', 'dk', 'dv', 'dq', 'dsm', 'dexp', 'dzm', 'dzr', 'dzbm', 'dzbe', 'dpvm', 'dpve', 'dpvst0', 'dpvst1', 'ep', 'ep1', 'ep2', 'ep3', 'ewf0', 'ewf1', 'ewb', 'epm', 'et1', 'exc0', 'exc1', 'efc0', 'efc1', 'exf', 'exfp', 'eyst0', 'eyst1', 'eyst2', 'eyst3', 'tapd']
    x_sems = ['cflw', 'csv', 'dkv']
    r_sems = []
    for name in base_sems + (x_sems if exchange else r_sems):
        S[name] = main.enter_context(nc.semaphore(name))
    PW = {"pwf0": ("brf", 16), "pwf1": ("brb", 1), "pwc": ("brm", 4),
          "pws0": ("brtm", 16), "pws1": ("brev", 4)}

    def pw(name):
        s, off = PW[name]
        return S[s], off

    idb = main.enter_context(nc.sbuf_tensor("idb_s", [128, 128], BF16))
    idf = main.enter_context(nc.sbuf_tensor("idf_s", [128, 128], F32))
    onc = main.enter_context(nc.sbuf_tensor("onc_s", [128, 1], BF16))
    onr = main.enter_context(nc.sbuf_tensor("onr_s", [1, 128], F32))
    emat = main.enter_context(nc.sbuf_tensor("emat_s", [64, 1024], BF16))
    bqk = main.enter_context(nc.sbuf_tensor("bqk_s", [128, 16], F32))
    snon = main.enter_context(nc.sbuf_tensor("snon_s", [1, 2], I32))
    czero = main.enter_context(nc.sbuf_tensor("czero_s", [128, 2], F32))
    NCONST = 8 if exchange else 7
    nc.const_aps.register(F32, 0.0, czero[:, 0:1])
    nc.const_aps.register(F32, EPS, czero[:, 1:2])

    s1 = ExitStack()
    xnT = s1.enter_context(nc.sbuf_tensor("xnT", [128, 65536], BF16))
    bv_b = s1.enter_context(nc.sbuf_tensor("bv_b", [128, 1024], F32))
    g_tm = s1.enter_context(nc.sbuf_tensor("g_tm", [128, 1024], BF16))
    g_Ts = s1.enter_context(nc.sbuf_tensor("g_Ts", [64, 2048], BF16))

    if upto >= 1:
        with ExitStack() as st:
            x_sb = [st.enter_context(nc.sbuf_tensor(f"x_sb{i}", [128, 4096], F32)) for i in range(3)]
            xn_sb = [st.enter_context(nc.sbuf_tensor(f"xn_sb{i}", [128, 4096], BF16)) for i in range(2)]
            stats = [st.enter_context(nc.sbuf_tensor(f"stats{i}", [128, 48], F32)) for i in range(2)]
            mv = [st.enter_context(nc.sbuf_tensor(f"mv{i}", [128, 2], F32)) for i in range(4)]
            rsd = [st.enter_context(nc.sbuf_tensor(f"rsd{i}", [128, 1], F32)) for i in range(4)]
            nmr = [st.enter_context(nc.sbuf_tensor(f"nmr{i}", [128, 1], F32)) for i in range(4)]
            lnv = [st.enter_context(nc.sbuf_tensor(f"lnv{i}", [128, 1], F32)) for i in range(4)]
            ps_t = [st.enter_context(nc.psum_tensor(f"ps_t{i}", [128, 512], F32)) for i in range(4)]
            ps_bv = st.enter_context(nc.psum_tensor("ps_bv", [128, 512], F32))
            bvr = st.enter_context(nc.sbuf_tensor("bvr_s", [1, 1024], F32))

            def ev_dst(i, j4):
                return ap2(xnT, (j4 * 4) * 2048 + i * 128,
                           [[65536, 128], [2048, 4], [1, 128]])

            with nc.named_scope("phA"), nc.Block() as blk:
                @blk.sync
                def _(sy):
                    sy.dma_start(out=idb[:], in_=idb_in[:, :]).then_inc(S["cst"], 16)
                    sy.dma_start(out=idf[:], in_=idf_in[:, :]).then_inc(S["cst"], 16)
                    sy.dma_start(out=onc[:], in_=onc_in[:, :]).then_inc(S["cst"], 16)
                    sy.dma_start(out=onr[:], in_=onr_in[:, :]).then_inc(S["cst"], 16)
                    sy.dma_start(out=emat[:], in_=em_in[:, :]).then_inc(S["cst"], 16)
                    sy.dma_start(out=bqk[:], in_=bqk_in[:, :]).then_inc(S["cst"], 16)
                    sy.dma_start(out=bvr[:], in_=bv_in[:, :]).then_inc(S["cst"], 16)
                    if exchange:
                        sy.dma_start(out=snon[0:1, 0:1], in_=non_in[:, :]).then_inc(S["cst"], 16)
                    for i in range(16):
                        if i >= 3:
                            sy.wait_ge(S["aap"], i - 2)
                        sy.dma_start(out=x_sb[i % 3][:],
                                     in_=x_in[i * 128:(i + 1) * 128, :]
                                     ).then_inc(S[f"ax{i % 3}"], 16)

                @blk.gpsimd
                def _(g):
                    if exchange:
                        g.wait_ge(S["cst"], 16 * NCONST)
                        with g.register("rn") as rn:
                            g.reg_load(rn, snon[0:1, 0:1])
                            g.reg_save(snon[0:1, 1:2], rn).then_inc(S["csv"], 1)

                @blk.vector
                def _(v):
                    v.memset(czero[:, 0:1], 0.0).then_inc(S["ini"], 1)
                    v.memset(czero[:, 1:2], EPS).then_inc(S["ini"], 1)
                    for cc in range(2):
                        v.wait_ge(S["abvm"], cc + 1)
                        v.tensor_copy(bv_b[:, cc * 512:(cc + 1) * 512],
                                      ps_bv[:]).then_inc(S["abvev"], 1)
                    for p in range(8):
                        a, bb = 2 * p, 2 * p + 1
                        for i in (a, bb):
                            v.wait_ge(S[f"ax{i % 3}"], 16 * (i // 3 + 1))
                            if p >= 1:
                                v.wait_ge(S["avs"], 18 * (p - 1) + 17 + (i - 2 * p))
                            if p >= 2:
                                v.wait_ge(S["aac"], 4 * p - 6)
                            for j in range(8):
                                v.bn_stats(stats[i % 2][:, 6 * j:6 * j + 6],
                                           x_sb[i % 3][:, 512 * j:512 * (j + 1)]
                                           ).then_inc(S["avs"], 1)
                        v.wait_ge(S["avs"], 18 * p + 16)
                        for i in (a, bb):
                            v.bn_aggr(mv[i % 4][:], stats[i % 2][:, 0:48]).then_inc(S["avs"], 1)
                        for i in (a, bb):
                            v.wait_ge(S["aac"], 4 * p + (3 if i == a else 4))
                            v.tensor_scalar(nmr[i % 4][:], mv[i % 4][:, 0:1],
                                            rsd[i % 4][:, 0:1], -1.0,
                                            ALU.mult, ALU.mult).then_inc(S["avr"], 1)
                        for i in (a, bb):
                            for j4 in range(1, 8, 2):
                                g_id = 8 * i + j4
                                v.wait_ge(S["atm"], g_id + 1)
                                v.tensor_copy(ev_dst(i, j4),
                                              ps_t[g_id % 4][:]).then_inc(S["aevv"], 1)

                @blk.scalar
                def _(s):
                    s.wait_ge(S["ini"], 2)
                    for p in range(8):
                        a, bb = 2 * p, 2 * p + 1
                        for i in (a, bb):
                            s.wait_ge(S["avs"], 18 * p + 17 + (i - a))
                            s.activation(lnv[i % 4][:], mv[i % 4][:, 1:2], AF.Ln,
                                         bias=EPS, scale=1.0).then_inc(S["aac"], 1)
                        for i in (a, bb):
                            s.wait_ge(S["aac"], 4 * p + 2)
                            s.activation(rsd[i % 4][:], lnv[i % 4][:], AF.Exp,
                                         bias=0.0, scale=-0.5).then_inc(S["aac"], 1)
                        for i in (a, bb):
                            s.wait_ge(S["avr"], 2 * p + 1 + (i - a))
                            if i >= 2:
                                s.wait_ge(S["atm"], 8 * (i - 1))
                            s.activation(xn_sb[i % 2][:], x_sb[i % 3][:], AF.Identity,
                                         bias=nmr[i % 4][:, 0:1],
                                         scale=rsd[i % 4][:, 0:1]).then_inc(S["aap"], 1)
                        for i in (a, bb):
                            for j4 in range(0, 8, 2):
                                g_id = 8 * i + j4
                                s.wait_ge(S["atm"], g_id + 1)
                                s.activation(ev_dst(i, j4), ps_t[g_id % 4][:], AF.Identity,
                                             bias=0.0, scale=1.0).then_inc(S["aev"], 1)

                @blk.tensor
                def _(t):
                    t.wait_ge(S["cst"], 16 * NCONST)
                    for cc in range(2):
                        if cc >= 1:
                            t.wait_ge(S["abvev"], cc)
                        t.matmul(ps_bv[:], onr[0:1, :], bvr[0:1, cc * 512:(cc + 1) * 512],
                                 start=True, stop=True).then_inc(S["abvm"], 1)
                    for i in range(16):
                        t.wait_ge(S["aap"], i + 1)
                        for j4 in range(8):
                            g_id = 8 * i + j4
                            if g_id >= 4:
                                if g_id % 2 == 0:
                                    t.wait_ge(S["aev"], g_id // 2 - 1)
                                else:
                                    t.wait_ge(S["aevv"], (g_id - 1) // 2 - 1)
                            for jj in range(4):
                                j = j4 * 4 + jj
                                mm = t.matmul(ps_t[g_id % 4][:, jj * 128:(jj + 1) * 128],
                                              xn_sb[i % 2][:, j * 128:(j + 1) * 128],
                                              idb[:], start=True, stop=True)
                                if jj == 3:
                                    mm.then_inc(S["atm"], 1)

    if tap == "xnt":
        with nc.Block() as blk:
            @blk.sync
            def _(sy):
                sy.wait_ge(S["aev"], 64)
                sy.wait_ge(S["aevv"], 64)
                sy.dma_start(out=taps["xnt"][:, :], in_=xnT[:]).then_inc(S["tapd"], 16)
                sy.wait_ge(S["tapd"], 16)

    if upto >= 2:
        with ExitStack() as st:
            wqk_f = [st.enter_context(nc.sbuf_tensor(f"wqk_f{i}", [128, 1024], F32)) for i in range(2)]
            wqk_b = [st.enter_context(nc.sbuf_tensor(f"wqk_b{i}", [128, 4096], BF16)) for i in range(2)]
            stg_k = [st.enter_context(nc.sbuf_tensor(f"stg_k{i}", [128, 512], BF16)) for i in range(2)]
            stg_q = [st.enter_context(nc.sbuf_tensor(f"stg_q{i}", [128, 512], BF16)) for i in range(2)]
            vq_f = [st.enter_context(nc.sbuf_tensor(f"vq_f{i}", [128, 2048], F32)) for i in range(2)]
            v_bf = st.enter_context(nc.sbuf_tensor("v_bf", [128, 16384], BF16))
            stg_v = [st.enter_context(nc.sbuf_tensor(f"stg_v{i}", [128, 512], BF16)) for i in range(2)]
            ps_qk = [st.enter_context(nc.psum_tensor(f"ps_qk{i}", [128, 512], F32)) for i in range(2)]
            ps_v = [st.enter_context(nc.psum_tensor(f"ps_v{i}", [128, 512], F32)) for i in range(2)]

            with nc.named_scope("phC"), nc.Block() as blk:
                @blk.sync
                def _(sy):
                    sy.wait_ge(S["aap"], 16)
                    sy.wait_ge(S["atm"], 128)
                    sy.wait_ge(S["avs"], 144)
                    sy.wait_ge(S["abvev"], 2)
                    for c in range(32):
                        k, qt = divmod(c, 4)
                        if c >= 2:
                            sy.wait_ge(S["cwb"], c - 1)
                        sy.dma_start(out=wqk_f[c % 2][:],
                                     in_=wqk[8 + k, :, qt * 1024:(qt + 1) * 1024]
                                     ).then_inc(S[f"cwf{c % 2}"], 16)
                    for c2 in range(8):
                        if c2 >= 2:
                            sy.wait_ge(S["cvb"], c2 - 1)
                        sy.dma_start(out=vq_f[c2 % 2][:],
                                     in_=wv[0, :, c2 * 2048:(c2 + 1) * 2048]
                                     ).then_inc(S[f"cvf{c2 % 2}"], 16)
                    for c in range(32, 42):
                        k, qt = divmod(c, 4)
                        sy.wait_ge(S["cwb"], c - 1)
                        sy.dma_start(out=wqk_f[c % 2][:],
                                     in_=wqk[k - 8, :, qt * 1024:(qt + 1) * 1024]
                                     ).then_inc(S[f"cwf{c % 2}"], 16)
                    for c2 in range(8, 16):
                        sy.wait_ge(S["cvb"], c2 - 1)
                        sy.dma_start(out=vq_f[c2 % 2][:],
                                     in_=wv[1, :, (c2 % 8) * 2048:(c2 % 8 + 1) * 2048]
                                     ).then_inc(S[f"cvf{c2 % 2}"], 16)
                    for c in range(42, 64):
                        k, qt = divmod(c, 4)
                        sy.wait_ge(S["cwb"], c - 1)
                        sy.dma_start(out=wqk_f[c % 2][:],
                                     in_=wqk[k - 8, :, qt * 1024:(qt + 1) * 1024]
                                     ).then_inc(S[f"cwf{c % 2}"], 16)

                @blk.gpsimd
                def _(g):
                    if exchange:
                        g.wait_ge(S["ckst0"], 16 * 16)
                        g.wait_ge(S["ckst1"], 16 * 16)
                        g.wait_ge(S["cvst0"], 16 * 16)
                        g.wait_ge(S["cvst1"], 16 * 16)
                        g.wait_ge(S["csv"], 1)
                        with g.register("rh") as rh, g.register("rko") as rko, \
                                g.register("rvo") as rvo, g.register("rfo") as rfo:
                            g.reg_alu(rh, g.to_reg(g.partition_id()), 1, ALU.bitwise_and)
                            g.reg_alu(rko, rh, DH * NTOK, ALU.mult)
                            g.reg_alu(rvo, rh, NTOK * DH, ALU.mult)
                            g.reg_alu(rfo, rh, 128, ALU.mult)
                            g.dma_start(out=ap2(k_sh, rko, [[NTOK, DH], [1, NTOK]]),
                                        in_=k_loc[:, :]).then_inc(S["cflw"], 16)
                            g.dma_start(out=ap2(v_sh, rvo, [[DH, NTOK], [1, DH]]),
                                        in_=v_loc[:, :]).then_inc(S["cflw"], 16)
                            g.wait_ge(S["cflw"], 32)
                            g.dma_start(out=ap2(fl_sh, rfo, [[1, 1], [1, 1]]),
                                        in_=snon[0:1, 1:2]).then_inc(S["cflw"], 16)
                            g.wait_ge(S["cflw"], 48)

                @blk.tensor
                def _(t):
                    t.wait_ge(S["aev"], 64)
                    t.wait_ge(S["aevv"], 64)

                    def qk_slab(k):
                        is_k = k < 8
                        kk = k if is_k else k - 8
                        smm = S["ckm"] if is_k else S["cqm"]
                        sev = S["ckev"] if is_k else S["cqev"]
                        t.wait_ge(S["cwb"], 4 * (k + 1))
                        for ncc in range(4):
                            g_id = 4 * kk + ncc
                            if g_id >= 2:
                                t.wait_ge(sev, g_id - 1)
                            for ct in range(32):
                                mm = t.matmul(ps_qk[g_id % 2][:],
                                              wqk_b[k % 2][:, ct * 128:(ct + 1) * 128],
                                              xnT[:, ct * 2048 + ncc * 512:
                                                  ct * 2048 + (ncc + 1) * 512],
                                              start=(ct == 0), stop=(ct == 31))
                                if ct == 31:
                                    mm.then_inc(smm, 1)

                    for k in range(8):
                        qk_slab(k)
                    for vs in range(2):
                        t.wait_ge(S["cvb"], 8 * (vs + 1))
                        for nt in range(16):
                            gv = 16 * vs + nt
                            if gv >= 2:
                                t.wait_ge(S["cvev"], gv - 1)
                            for ct in range(32):
                                mm = t.matmul(ps_v[gv % 2][:],
                                              xnT[:, ct * 2048 + nt * 128:
                                                  ct * 2048 + nt * 128 + 128],
                                              v_bf[:, ct * 512:(ct + 1) * 512],
                                              start=(ct == 0), stop=(ct == 31))
                                if ct == 31:
                                    mm.then_inc(S["cvm"], 1)
                    for k in range(8, 16):
                        qk_slab(k)

                @blk.vector
                def _(v):
                    def wcast(c):
                        k, qt = divmod(c, 4)
                        v.wait_ge(S[f"cwf{c % 2}"], 16 * (c // 2 + 1))
                        if k >= 2 and qt == 0:
                            if k - 2 < 8:
                                v.wait_ge(S["ckm"], 4 * (k - 1))
                            else:
                                v.wait_ge(S["cqm"], 4 * (k - 9))
                        v.tensor_copy(wqk_b[k % 2][:, qt * 1024:(qt + 1) * 1024],
                                      wqk_f[c % 2][:]).then_inc(S["cwb"], 1)

                    for c in range(32):
                        wcast(c)
                    for c2 in range(8):
                        v.wait_ge(S[f"cvf{c2 % 2}"], 16 * (c2 // 2 + 1))
                        v.tensor_copy(v_bf[:, c2 * 2048:(c2 + 1) * 2048],
                                      vq_f[c2 % 2][:]).then_inc(S["cvb"], 1)
                    for c in range(32, 40):
                        wcast(c)
                    for gv in range(16):
                        v.wait_ge(S["cvm"], gv + 1)
                        if gv >= 2:
                            v.wait_ge(S[f"cvst{gv % 2}"], 16 * (gv // 2))
                        v.scalar_tensor_tensor(stg_v[gv % 2][:], ps_v[gv % 2][:], 1.0,
                                               bv_b[:, 0:512],
                                               ALU.mult, ALU.add).then_inc(S["cvev"], 1)
                    for c2 in range(8, 16):
                        v.wait_ge(S[f"cvf{c2 % 2}"], 16 * (c2 // 2 + 1))
                        if c2 == 8:
                            v.wait_ge(S["cvm"], 16)
                        v.tensor_copy(v_bf[:, (c2 - 8) * 2048:(c2 - 7) * 2048],
                                      vq_f[c2 % 2][:]).then_inc(S["cvb"], 1)
                    for gv in range(16, 32):
                        v.wait_ge(S["cvm"], gv + 1)
                        v.wait_ge(S[f"cvst{gv % 2}"], 16 * (gv // 2))
                        v.scalar_tensor_tensor(stg_v[gv % 2][:], ps_v[gv % 2][:], 1.0,
                                               bv_b[:, 512:1024],
                                               ALU.mult, ALU.add).then_inc(S["cvev"], 1)
                    for c in range(40, 64):
                        wcast(c)

                @blk.scalar
                def _(s):
                    def qk_out(k):
                        is_k = k < 8
                        kk = k if is_k else k - 8
                        smm = S["ckm"] if is_k else S["cqm"]
                        sev = S["ckev"] if is_k else S["cqev"]
                        sstp = [S["ckst0"], S["ckst1"]] if is_k else [S["cqst0"], S["cqst1"]]
                        stg = stg_k if is_k else stg_q
                        bcol = 8 + kk if is_k else kk
                        dst = k_loc if is_k else q_dr
                        for ncc in range(4):
                            g_id = 4 * kk + ncc
                            s.wait_ge(smm, g_id + 1)
                            if g_id >= 2:
                                s.wait_ge(sstp[g_id % 2], 16 * (g_id // 2))
                            s.activation(stg[g_id % 2][:], ps_qk[g_id % 2][:],
                                         AF.Identity, bias=bqk[:, bcol:bcol + 1],
                                         scale=1.0).then_inc(sev, 1)
                            s.wait_ge(sev, 4 * kk + ncc + 1)
                            dstride = KT if dst is k_loc else NTOK
                            s.dma_start(out=ap2(dst, kk * 128 * dstride + ncc * 512,
                                                [[dstride, 128], [1, 512]]),
                                        in_=stg[g_id % 2][:]).then_inc(sstp[g_id % 2], 16)

                    for k in range(8):
                        qk_out(k)
                    for vs in range(2):
                        for nt in range(16):
                            gv = 16 * vs + nt
                            s.wait_ge(S["cvev"], gv + 1)
                            s.dma_start(out=ap2(v_loc, nt * 128 * DH + vs * 512,
                                                [[DH, 128], [1, 512]]),
                                        in_=stg_v[gv % 2][:]).then_inc(S[f"cvst{gv % 2}"], 16)
                    for k in range(8, 16):
                        qk_out(k)

    if tap == "kvq":
        with nc.Block() as blk:
            @blk.gpsimd
            def _(g):
                with g.register("rn_d") as rn, g.register("rf0_d") as rf0, \
                        g.register("rf1_d") as rf1, g.register("rt0_d") as rt0, \
                        g.register("rt1_d") as rt1, g.register("rc_d") as rc:
                    g.wait_ge(S["cflw"], 48)
                    g.reg_load(rn, snon[0:1, 0:1])
                    g.reg_mov(rc, 1)
                    with g.While(rc):
                        g.load(rf0, fl_sh[0:1, 0:1])
                        g.load(rf1, fl_sh[1:2, 0:1])
                        g.reg_alu(rt0, rf0, rn, ALU.not_equal)
                        g.reg_alu(rt1, rf1, rn, ALU.not_equal)
                        g.reg_alu(rc, rt0, rt1, ALU.bitwise_or)
                g.sem_inc(S["dkv"], 1)

            @blk.sync
            def _(sy):
                sy.wait_ge(S["dkv"], 1)
                sy.wait_ge(S["cqst0"], 16 * 16)
                sy.wait_ge(S["cqst1"], 16 * 16)
                sy.dma_start(out=taps["k"][:, :],
                             in_=ap2(k_sh, 0, [[NTOK, 2048], [1, NTOK]])).then_inc(S["tapd"], 16)
                sy.dma_start(out=taps["v"][:, :],
                             in_=ap2(v_sh, 0, [[DH, 4096], [1, DH]])).then_inc(S["tapd"], 16)
                sy.dma_start(out=taps["q"][:, :],
                             in_=q_dr[:, :]).then_inc(S["tapd"], 16)
                sy.wait_ge(S["tapd"], 48)

    if upto >= 4:
        with ExitStack() as st:
            wr_f = st.enter_context(nc.sbuf_tensor("wr_f", [128, 2048], F32))
            wr_b = st.enter_context(nc.sbuf_tensor("wr_b", [128, 2048], BF16))
            s_rT = st.enter_context(nc.sbuf_tensor("s_rT", [64, 2048], F32))
            s_r = st.enter_context(nc.sbuf_tensor("s_r", [128, 1024], F32))
            r1 = st.enter_context(nc.sbuf_tensor("r1", [128, 8], F32))
            r2 = st.enter_context(nc.sbuf_tensor("r2", [128, 16], F32))
            t2b = st.enter_context(nc.sbuf_tensor("t2b", [128, 64], F32))
            nm1 = st.enter_context(nc.sbuf_tensor("nm1", [128, 2], F32))
            e_t = st.enter_context(nc.sbuf_tensor("e_t", [128, 128], F32))
            mask = st.enter_context(nc.sbuf_tensor("maskb", [128, 64], F32))
            em_t = st.enter_context(nc.sbuf_tensor("em_t", [128, 1024], F32))
            zc = st.enter_context(nc.sbuf_tensor("zc", [128, 16], F32))
            zic = st.enter_context(nc.sbuf_tensor("zic", [128, 16], F32))
            ps_r = [st.enter_context(nc.psum_tensor(f"ps_r{i}", [64, 512], F32)) for i in range(2)]
            ps_rt = [st.enter_context(nc.psum_tensor(f"ps_rt{i}", [128, 64], F32)) for i in range(2)]

            with nc.named_scope("phB1a"), nc.Block() as blk:
                @blk.sync
                def _(sy):
                    sy.wait_ge(S["ckst0"], 16 * 16)
                    sy.wait_ge(S["ckst1"], 16 * 16)
                    sy.wait_ge(S["cqst0"], 16 * 16)
                    sy.wait_ge(S["cqst1"], 16 * 16)
                    sy.wait_ge(S["cvst0"], 16 * 16)
                    sy.wait_ge(S["cvst1"], 16 * 16)
                    sy.dma_start(out=wr_f[:], in_=wr[:, :]).then_inc(S["brf"], 16)

                @blk.gpsimd
                def _(g):
                    g.wait_ge(S["brf"], 16)
                    g.tensor_copy(wr_b[:], wr_f[:]).then_inc(S["brb"], 1)
                    for nt in range(16):
                        d = nt % 2
                        sl = s_r[:, nt * 64:(nt + 1) * 64]
                        g.wait_ge(S["bdv"], 4 * nt + 3)
                        if nt >= 1:
                            g.wait_ge(S["bpl"], 2 * nt)
                        g.tensor_scalar(mask[:], sl, r2[:, 8 * d + 7:8 * d + 8], None,
                                        ALU.is_ge, ALU.bypass).then_inc(S["bpl"], 1)
                        g.wait_ge(S["bac"], nt + 1)
                        g.wait_ge(S["bpl"], 2 * nt + 1)
                        g.tensor_tensor(em_t[:, 64 * nt:64 * nt + 64],
                                        e_t[:, 64 * d:64 * d + 64], mask[:],
                                        ALU.mult).then_inc(S["bpl"], 1)
                    for nt in range(16):
                        g.wait_ge(S["bdv"], 64 + 2 * nt + 2)
                        g.tensor_scalar(g_tm[:, nt * 64:(nt + 1) * 64],
                                        em_t[:, 64 * nt:64 * nt + 64], zic[:, nt:nt + 1],
                                        2.0, ALU.mult, ALU.mult).then_inc(S["bpl"], 1)

                @blk.tensor
                def _(t):
                    t.wait_ge(S["brb"], 1)
                    for ncc in range(4):
                        if ncc >= 2:
                            t.wait_ge(S["brev"], ncc - 1)
                        for ct in range(32):
                            mm = t.matmul(ps_r[ncc % 2][:],
                                          wr_b[:, ct * 64:(ct + 1) * 64],
                                          xnT[:, ct * 2048 + ncc * 512:
                                              ct * 2048 + (ncc + 1) * 512],
                                          start=(ct == 0), stop=(ct == 31))
                            if ct == 31:
                                mm.then_inc(S["brm"], 1)
                    t.wait_ge(S["brev"], 4)
                    for nt in range(16):
                        if nt >= 2:
                            t.wait_ge(S["brtev"], nt - 1)
                        t.matmul(ps_rt[nt % 2][:], s_rT[0:64, nt * 128:(nt + 1) * 128],
                                 idf[0:64, 0:64], start=True, stop=True).then_inc(S["brtm"], 1)

                @blk.vector
                def _(v):
                    for ncc in range(4):
                        v.wait_ge(S["brm"], ncc + 1)
                        v.tensor_copy(s_rT[0:64, ncc * 512:(ncc + 1) * 512],
                                      ps_r[ncc % 2][0:64, :]).then_inc(S["brev"], 1)
                    for nt in range(16):
                        v.wait_ge(S["brtm"], nt + 1)
                        v.tensor_copy(s_r[:, nt * 64:(nt + 1) * 64],
                                      ps_rt[nt % 2][:, 0:64]).then_inc(S["brtev"], 1)
                    for nt in range(16):
                        b = 4 * nt
                        d = nt % 2
                        sl = s_r[:, nt * 64:(nt + 1) * 64]
                        v.wait_ge(S["brtev"], nt + 1)
                        if nt >= 1:
                            v.wait_ge(S["bdv"], 4 * nt)
                        v.max(r1[:], sl).then_inc(S["bdv"], 1)
                        v.wait_ge(S["bdv"], b + 1)
                        v.match_replace(t2b[:], r1[:], sl, -1e30).then_inc(S["bdv"], 1)
                        v.wait_ge(S["bdv"], b + 2)
                        if nt >= 2:
                            v.wait_ge(S["bpl"], 2 * nt - 3)
                        v.max(r2[:, 8 * d:8 * d + 8], t2b[:]).then_inc(S["bdv"], 1)
                        if nt >= 2:
                            v.wait_ge(S["bac"], nt - 1)
                        v.tensor_scalar(nm1[:, d:d + 1], r1[:, 0:1], -1.0, None,
                                        ALU.mult, ALU.bypass).then_inc(S["bdv"], 1)
                    for nt in range(16):
                        v.wait_ge(S["bpl"], 2 * nt + 2)
                        v.tensor_reduce(zc[:, nt:nt + 1], em_t[:, 64 * nt:64 * nt + 64],
                                        AX.X, ALU.add).then_inc(S["bdv"], 1)
                        v.wait_ge(S["bdv"], 64 + 2 * nt + 1)
                        v.reciprocal(zic[:, nt:nt + 1],
                                     zc[:, nt:nt + 1]).then_inc(S["bdv"], 1)

                @blk.scalar
                def _(s):
                    for nt in range(16):
                        d = nt % 2
                        s.wait_ge(S["bdv"], 4 * nt + 4)
                        if nt >= 2:
                            s.wait_ge(S["bpl"], 2 * nt - 2)
                        s.activation(e_t[:, 64 * d:64 * d + 64],
                                     s_r[:, nt * 64:(nt + 1) * 64], AF.Exp,
                                     bias=nm1[:, d:d + 1], scale=1.0).then_inc(S["bac"], 1)

    if tap == "rg":
        with nc.Block() as blk:
            @blk.sync
            def _(sy):
                sy.wait_ge(S["bdv"], 96)
                sy.wait_ge(S["bpl"], 48)
                sy.dma_start(out=taps["sr"][:, :], in_=s_r[:]).then_inc(S["tapd"], 16)
                sy.dma_start(out=taps["g"][:, :], in_=g_tm[:]).then_inc(S["tapd"], 16)
                sy.wait_ge(S["tapd"], 32)

    if upto >= 5:
        with ExitStack() as st:
            g_exp = [st.enter_context(nc.sbuf_tensor(f"g_exp{i}", [128, 2048], BF16)) for i in range(2)]
            at_f = [st.enter_context(nc.sbuf_tensor(f"at_f{i}", [128, 1024], F32)) for i in range(2)]
            at_b = [st.enter_context(nc.sbuf_tensor(f"at_b{i}", [128, 4096], BF16)) for i in range(2)]
            stg_h = [st.enter_context(nc.sbuf_tensor(f"stg_h{i}", [128, 512], BF16)) for i in range(2)]
            ps_g = [st.enter_context(nc.psum_tensor(f"ps_g{i}", [64, 128], F32)) for i in range(2)]
            ps_ge = [st.enter_context(nc.psum_tensor(f"ps_ge{i}", [128, 512], F32)) for i in range(2)]
            ps_h = [st.enter_context(nc.psum_tensor(f"ps_h{i}", [128, 512], F32)) for i in range(4)]

            with nc.named_scope("phB1b"), nc.Block() as blk:
                @blk.sync
                def _(sy):
                    sy.wait_ge(S["bdv"], 96)
                    sy.wait_ge(S["bpl"], 48)
                    sy.wait_ge(S["bac"], 16)
                    sy.wait_ge(S["brm"], 4)
                    sy.wait_ge(S["brtm"], 16)
                    for c in range(32):
                        ert, qt = divmod(c, 4)
                        if c >= 2:
                            sy.wait_ge(S["bab"], c - 1)
                        sy.dma_start(out=at_f[c % 2][:],
                                     in_=at_w[ert, :, qt * 1024:(qt + 1) * 1024]).then_inc(S[f"baf{c % 2}"], 16)

                @blk.gpsimd
                def _(g):
                    for c in range(32):
                        ert, qt = divmod(c, 4)
                        g.wait_ge(S[f"baf{c % 2}"], 16 * (c // 2 + 1))
                        if ert >= 2 and qt == 0:
                            g.wait_ge(S["bhm"], 4 * (ert - 1))
                        g.tensor_copy(at_b[ert % 2][:, qt * 1024:(qt + 1) * 1024],
                                      at_f[c % 2][:]).then_inc(S["bab"], 1)

                @blk.tensor
                def _(t):
                    def h_grp(gh):
                        ert, ncc = divmod(gh, 4)
                        if ncc == 0:
                            t.wait_ge(S["bab"], 4 * (ert + 1))
                        if gh >= 4:
                            t.wait_ge(S["bhev"], gh - 3)
                        for ct in range(32):
                            mm = t.matmul(ps_h[gh % 4][:],
                                          at_b[ert % 2][:, ct * 128:(ct + 1) * 128],
                                          xnT[:, ct * 2048 + ncc * 512:
                                              ct * 2048 + (ncc + 1) * 512],
                                          start=(ct == 0), stop=(ct == 31))
                            if ct == 31:
                                mm.then_inc(S["bhm"], 1)

                    for gh in range(4):
                        h_grp(gh)
                    for nt in range(16):
                        t.wait_ge(S["bpl"], 33 + nt)
                        if nt >= 2:
                            t.wait_ge(S["bgev"], nt - 1)
                        t.matmul(ps_g[nt % 2][:], g_tm[:, nt * 64:(nt + 1) * 64],
                                 idb[:], start=True, stop=True).then_inc(S["bgm"], 1)
                    t.wait_ge(S["bgev"], 16)
                    for ert in range(8):
                        for ncc in range(4):
                            ge = 4 * ert + ncc
                            if ge >= 2:
                                t.wait_ge(S["bgeev"], ge - 1)
                            t.matmul(ps_ge[ge % 2][:], emat[0:64, ert * 128:(ert + 1) * 128],
                                     g_Ts[0:64, ncc * 512:(ncc + 1) * 512],
                                     start=True, stop=True).then_inc(S["bgem"], 1)
                        if ert < 7:
                            for gh in range(4 * ert + 4, 4 * ert + 8):
                                h_grp(gh)

                @blk.vector
                def _(v):
                    for nt in range(16):
                        v.wait_ge(S["bgm"], nt + 1)
                        v.tensor_copy(g_Ts[0:64, nt * 128:(nt + 1) * 128],
                                      ps_g[nt % 2][0:64, :]).then_inc(S["bgev"], 1)
                    for ert in range(8):
                        for ncc in range(4):
                            ge = 4 * ert + ncc
                            v.wait_ge(S["bgem"], ge + 1)
                            if ert >= 2 and ncc == 0:
                                v.wait_ge(S["bhev"], 4 * (ert - 1))
                            v.tensor_copy(g_exp[ert % 2][:, ncc * 512:(ncc + 1) * 512],
                                          ps_ge[ge % 2][:]).then_inc(S["bgeev"], 1)
                        for ncc in range(4):
                            gh = 4 * ert + ncc
                            v.wait_ge(S["bhm"], gh + 1)
                            if gh >= 2:
                                v.wait_ge(S[f"bhst{gh % 2}"], 16 * (gh // 2))
                            v.scalar_tensor_tensor(stg_h[gh % 2][:], ps_h[gh % 4][:], 1.0,
                                                   g_exp[ert % 2][:, ncc * 512:(ncc + 1) * 512],
                                                   ALU.mult, ALU.mult).then_inc(S["bhev"], 1)

                @blk.scalar
                def _(s):
                    for gh in range(32):
                        ert, ncc = divmod(gh, 4)
                        s.wait_ge(S["bhev"], gh + 1)
                        s.dma_start(out=ap2(hg_dr, ert * 128 * NTOK + ncc * 512,
                                            [[NTOK, 128], [1, 512]]),
                                    in_=stg_h[gh % 2][:]).then_inc(S[f"bhst{gh % 2}"], 16)

    if tap == "hg":
        with nc.Block() as blk:
            @blk.sync
            def _(sy):
                sy.wait_ge(S["bhst0"], 16 * 16)
                sy.wait_ge(S["bhst1"], 16 * 16)
                sy.dma_start(out=taps["hg"][:, :], in_=hg_dr[:, :]).then_inc(S["tapd"], 16)
                sy.wait_ge(S["tapd"], 16)

    if upto >= 6 and not exchange:
        with ExitStack() as st:
            x_sb = [st.enter_context(nc.sbuf_tensor(f"px_sb{i}", [128, 4096], F32)) for i in range(3)]
            xn_sb = [st.enter_context(nc.sbuf_tensor(f"pxn_sb{i}", [128, 4096], BF16)) for i in range(2)]
            stats = [st.enter_context(nc.sbuf_tensor(f"pstats{i}", [128, 48], F32)) for i in range(2)]
            mv = [st.enter_context(nc.sbuf_tensor(f"pmv{i}", [128, 2], F32)) for i in range(4)]
            rsd = [st.enter_context(nc.sbuf_tensor(f"prsd{i}", [128, 1], F32)) for i in range(4)]
            nmr = [st.enter_context(nc.sbuf_tensor(f"pnmr{i}", [128, 1], F32)) for i in range(4)]
            lnv = [st.enter_context(nc.sbuf_tensor(f"plnv{i}", [128, 1], F32)) for i in range(4)]
            ps_t = [st.enter_context(nc.psum_tensor(f"pps_t{i}", [128, 512], F32)) for i in range(4)]

            def ev_dst(i, j4):
                return ap2(xnT, (j4 * 4) * 2048 + i * 128,
                           [[65536, 128], [2048, 4], [1, 128]])

            with nc.named_scope("phAp"), nc.Block() as blk:
                @blk.sync
                def _(sy):
                    sy.wait_ge(S["bhst0"], 16 * 16)
                    sy.wait_ge(S["bhst1"], 16 * 16)
                    sy.wait_ge(S["bhm"], 32)
                    sy.wait_ge(S["bgem"], 32)
                    sy.wait_ge(S["bhev"], 32)
                    for i in range(16):
                        if i >= 3:
                            sy.wait_ge(S["aap"], 16 + i - 2)
                        sy.dma_start(out=x_sb[i % 3][:],
                                     in_=xp_in[i * 128:(i + 1) * 128, :]
                                     ).then_inc(S[f"ax{i % 3}"], 16)

                @blk.vector
                def _(v):
                    for p in range(8):
                        a, bb = 2 * p, 2 * p + 1
                        for i in (a, bb):
                            v.wait_ge(S[f"ax{i % 3}"], 16 * ((6, 5, 5)[i % 3] + i // 3 + 1))
                            if p >= 1:
                                v.wait_ge(S["avs"], 144 + 18 * (p - 1) + 17 + (i - 2 * p))
                            if p >= 2:
                                v.wait_ge(S["aac"], 32 + 4 * p - 6)
                            for j in range(8):
                                v.bn_stats(stats[i % 2][:, 6 * j:6 * j + 6],
                                           x_sb[i % 3][:, 512 * j:512 * (j + 1)]
                                           ).then_inc(S["avs"], 1)
                        v.wait_ge(S["avs"], 144 + 18 * p + 16)
                        for i in (a, bb):
                            v.bn_aggr(mv[i % 4][:], stats[i % 2][:, 0:48]).then_inc(S["avs"], 1)
                        for i in (a, bb):
                            v.wait_ge(S["aac"], 32 + 4 * p + (3 if i == a else 4))
                            v.tensor_scalar(nmr[i % 4][:], mv[i % 4][:, 0:1],
                                            rsd[i % 4][:, 0:1], -1.0,
                                            ALU.mult, ALU.mult).then_inc(S["avr"], 1)
                        for i in (a, bb):
                            for j4 in range(1, 8, 2):
                                g_id = 8 * i + j4
                                v.wait_ge(S["atm"], 128 + g_id + 1)
                                v.tensor_copy(ev_dst(i, j4),
                                              ps_t[g_id % 4][:]).then_inc(S["aevv"], 1)

                @blk.scalar
                def _(s):
                    for p in range(8):
                        a, bb = 2 * p, 2 * p + 1
                        for i in (a, bb):
                            s.wait_ge(S["avs"], 144 + 18 * p + 17 + (i - a))
                            s.activation(lnv[i % 4][:], mv[i % 4][:, 1:2], AF.Ln,
                                         bias=EPS, scale=1.0).then_inc(S["aac"], 1)
                        for i in (a, bb):
                            s.wait_ge(S["aac"], 32 + 4 * p + 2)
                            s.activation(rsd[i % 4][:], lnv[i % 4][:], AF.Exp,
                                         bias=0.0, scale=-0.5).then_inc(S["aac"], 1)
                        for i in (a, bb):
                            s.wait_ge(S["avr"], 16 + 2 * p + 1 + (i - a))
                            if i >= 2:
                                s.wait_ge(S["atm"], 128 + 8 * (i - 1))
                            s.activation(xn_sb[i % 2][:], x_sb[i % 3][:], AF.Identity,
                                         bias=nmr[i % 4][:, 0:1],
                                         scale=rsd[i % 4][:, 0:1]).then_inc(S["aap"], 1)
                        for i in (a, bb):
                            for j4 in range(0, 8, 2):
                                g_id = 8 * i + j4
                                s.wait_ge(S["atm"], 128 + g_id + 1)
                                s.activation(ev_dst(i, j4), ps_t[g_id % 4][:], AF.Identity,
                                             bias=0.0, scale=1.0).then_inc(S["aev"], 1)

                @blk.tensor
                def _(t):
                    t.wait_ge(S["bhev"], 32)
                    for i in range(16):
                        t.wait_ge(S["aap"], 16 + i + 1)
                        for j4 in range(8):
                            g_id = 8 * i + j4
                            if g_id >= 4:
                                if g_id % 2 == 0:
                                    t.wait_ge(S["aev"], 64 + g_id // 2 - 1)
                                else:
                                    t.wait_ge(S["aevv"], 64 + (g_id - 1) // 2 - 1)
                            for jj in range(4):
                                j = j4 * 4 + jj
                                mm = t.matmul(ps_t[g_id % 4][:, jj * 128:(jj + 1) * 128],
                                              xn_sb[i % 2][:, j * 128:(j + 1) * 128],
                                              idb[:], start=True, stop=True)
                                if jj == 3:
                                    mm.then_inc(S["atm"], 1)

    if upto >= 6 and not exchange:
        with ExitStack() as st:
            wqk_f = [st.enter_context(nc.sbuf_tensor(f"pwqk_f{i}", [128, 1024], F32)) for i in range(2)]
            wqk_b = [st.enter_context(nc.sbuf_tensor(f"pwqk_b{i}", [128, 4096], BF16)) for i in range(2)]
            stg_k = [st.enter_context(nc.sbuf_tensor(f"pstg_k{i}", [128, 512], BF16)) for i in range(2)]
            vq_f = [st.enter_context(nc.sbuf_tensor(f"pvq_f{i}", [128, 2048], F32)) for i in range(2)]
            v_bf = st.enter_context(nc.sbuf_tensor("pv_bf", [128, 16384], BF16))
            stg_v = [st.enter_context(nc.sbuf_tensor(f"pstg_v{i}", [128, 512], BF16)) for i in range(2)]
            ps_qk = [st.enter_context(nc.psum_tensor(f"pps_qk{i}", [128, 512], F32)) for i in range(2)]
            ps_v = [st.enter_context(nc.psum_tensor(f"pps_v{i}", [128, 512], F32)) for i in range(2)]

            with nc.named_scope("phCp"), nc.Block() as blk:
                @blk.sync
                def _(sy):
                    sy.wait_ge(S["aap"], 32)
                    sy.wait_ge(S["atm"], 256)
                    sy.wait_ge(S["avs"], 288)
                    for c in range(32):
                        k, qt = divmod(c, 4)
                        if c >= 2:
                            sy.wait_ge(S["cwb"], 64 + c - 1)
                        sy.dma_start(out=wqk_f[c % 2][:],
                                     in_=wqk[8 + k, :, qt * 1024:(qt + 1) * 1024]
                                     ).then_inc(S[f"cwf{c % 2}"], 16)
                    for c2 in range(16):
                        sy.wait_ge(S["cvb"], 16 + max(0, c2 - 1))
                        sy.dma_start(out=vq_f[c2 % 2][:],
                                     in_=wv[c2 // 8, :, (c2 % 8) * 2048:(c2 % 8 + 1) * 2048]
                                     ).then_inc(S[f"cvf{c2 % 2}"], 16)

                @blk.tensor
                def _(t):
                    t.wait_ge(S["aev"], 128)
                    t.wait_ge(S["aevv"], 128)
                    for k in range(8):
                        t.wait_ge(S["cwb"], 64 + 4 * (k + 1))
                        for ncc in range(4):
                            g_id = 4 * k + ncc
                            if g_id >= 2:
                                t.wait_ge(S["ckev"], 32 + g_id - 1)
                            for ct in range(32):
                                mm = t.matmul(ps_qk[g_id % 2][:],
                                              wqk_b[k % 2][:, ct * 128:(ct + 1) * 128],
                                              xnT[:, ct * 2048 + ncc * 512:
                                                  ct * 2048 + (ncc + 1) * 512],
                                              start=(ct == 0), stop=(ct == 31))
                                if ct == 31:
                                    mm.then_inc(S["ckm"], 1)
                    for vs in range(2):
                        t.wait_ge(S["cvb"], 16 + 8 * (vs + 1))
                        for nt in range(16):
                            gv = 16 * vs + nt
                            if gv >= 2:
                                t.wait_ge(S["cvev"], 32 + gv - 1)
                            for ct in range(32):
                                mm = t.matmul(ps_v[gv % 2][:],
                                              xnT[:, ct * 2048 + nt * 128:
                                                  ct * 2048 + nt * 128 + 128],
                                              v_bf[:, ct * 512:(ct + 1) * 512],
                                              start=(ct == 0), stop=(ct == 31))
                                if ct == 31:
                                    mm.then_inc(S["cvm"], 1)

                @blk.vector
                def _(v):
                    for c in range(32):
                        k, qt = divmod(c, 4)
                        v.wait_ge(S[f"cwf{c % 2}"], 16 * (32 + c // 2 + 1))
                        if k >= 2 and qt == 0:
                            v.wait_ge(S["ckm"], 32 + 4 * (k - 1))
                        v.tensor_copy(wqk_b[k % 2][:, qt * 1024:(qt + 1) * 1024],
                                      wqk_f[c % 2][:]).then_inc(S["cwb"], 1)
                    for c2 in range(8):
                        v.wait_ge(S[f"cvf{c2 % 2}"], 16 * (8 + c2 // 2 + 1))
                        v.tensor_copy(v_bf[:, c2 * 2048:(c2 + 1) * 2048],
                                      vq_f[c2 % 2][:]).then_inc(S["cvb"], 1)
                    for gv in range(16):
                        v.wait_ge(S["cvm"], 32 + gv + 1)
                        if gv >= 2:
                            v.wait_ge(S[f"cvst{gv % 2}"], 16 * (16 + gv // 2))
                        v.scalar_tensor_tensor(stg_v[gv % 2][:], ps_v[gv % 2][:], 1.0,
                                               bv_b[:, 0:512],
                                               ALU.mult, ALU.add).then_inc(S["cvev"], 1)
                    for c2 in range(8, 16):
                        v.wait_ge(S[f"cvf{c2 % 2}"], 16 * (8 + c2 // 2 + 1))
                        if c2 == 8:
                            v.wait_ge(S["cvm"], 48)
                        v.tensor_copy(v_bf[:, (c2 - 8) * 2048:(c2 - 7) * 2048],
                                      vq_f[c2 % 2][:]).then_inc(S["cvb"], 1)
                    for gv in range(16, 32):
                        v.wait_ge(S["cvm"], 32 + gv + 1)
                        v.wait_ge(S[f"cvst{gv % 2}"], 16 * (16 + gv // 2))
                        v.scalar_tensor_tensor(stg_v[gv % 2][:], ps_v[gv % 2][:], 1.0,
                                               bv_b[:, 512:1024],
                                               ALU.mult, ALU.add).then_inc(S["cvev"], 1)

                @blk.scalar
                def _(s):
                    for k in range(8):
                        for ncc in range(4):
                            g_id = 4 * k + ncc
                            s.wait_ge(S["ckm"], 32 + g_id + 1)
                            if g_id >= 2:
                                s.wait_ge(S[f"ckst{g_id % 2}"], 16 * (16 + g_id // 2))
                            s.activation(stg_k[g_id % 2][:], ps_qk[g_id % 2][:],
                                         AF.Identity, bias=bqk[:, 8 + k:8 + k + 1],
                                         scale=1.0).then_inc(S["ckev"], 1)
                            s.wait_ge(S["ckev"], 32 + g_id + 1)
                            s.dma_start(out=ap2(k_loc, k * 128 * KT + NTOK + ncc * 512,
                                                [[KT, 128], [1, 512]]),
                                        in_=stg_k[g_id % 2][:]).then_inc(S[f"ckst{g_id % 2}"], 16)
                    for vs in range(2):
                        for nt in range(16):
                            gv = 16 * vs + nt
                            s.wait_ge(S["cvev"], 32 + gv + 1)
                            s.dma_start(out=ap2(v_loc, (NTOK + nt * 128) * DH + vs * 512,
                                                [[DH, 128], [1, 512]]),
                                        in_=stg_v[gv % 2][:]).then_inc(S[f"cvst{gv % 2}"], 16)

    s1.close()

    s2 = ExitStack()
    if upto >= 6:
        qsb = s2.enter_context(nc.sbuf_tensor("qsb", [128, 16384], BF16))
        ksb = s2.enter_context(nc.sbuf_tensor("ksb", [128, 32768], BF16))
        with ExitStack() as st:
            hgs = st.enter_context(nc.sbuf_tensor("hgs", [128, 16384], BF16))
            bo_b = st.enter_context(nc.sbuf_tensor("bo_b", [128, 4096], F32))
            bor = st.enter_context(nc.sbuf_tensor("bor_s", [1, 4096], F32))
            bf_f = [st.enter_context(nc.sbuf_tensor(f"bf_f{i}", [128, 1024], F32)) for i in range(2)]
            bf_b = [st.enter_context(nc.sbuf_tensor(f"bf_b{i}", [128, 4096], BF16)) for i in range(2)]
            stg_f = [st.enter_context(nc.sbuf_tensor(f"stg_f{i}", [128, 512], BF16)) for i in range(2)]
            ps_f = [st.enter_context(nc.psum_tensor(f"ps_f{i}", [128, 512], F32)) for i in range(2)]
            ps_bo = st.enter_context(nc.psum_tensor("ps_bo", [128, 512], F32))

            with nc.named_scope("phB2"), nc.Block() as blk:
                @blk.sync
                def _(sy):
                    sy.wait_ge(S["bhst0"], 16 * 16)
                    sy.wait_ge(S["bhst1"], 16 * 16)
                    sy.wait_ge(S["bhm"], 32)
                    sy.wait_ge(S["bgem"], 32)
                    if not exchange:
                        sy.wait_ge(S["aap"], 32)
                        sy.wait_ge(S["atm"], 256)
                        sy.wait_ge(S["avs"], 288)
                        sy.wait_ge(S["ckst0"], 16 * 32)
                        sy.wait_ge(S["ckst1"], 16 * 32)
                        sy.wait_ge(S["cvst0"], 16 * 32)
                        sy.wait_ge(S["cvst1"], 16 * 32)
                    sy.dma_start(out=bor[:], in_=bo_in[:, :]).then_inc(S["bbor"], 16)
                    sy.dma_start(out=ap2(hgs, 0, [[16384, 128], [2048, 8], [1, 2048]]),
                                 in_=ap2(hg_dr, 0, [[NTOK, 128], [128 * NTOK, 8], [1, NTOK]])
                                 ).then_inc(S["bhg"], 16)
                    if exchange:
                        sy.wait_ge(S["dkv"], 1)
                        for blkh in range(2):
                            for dt in range(8):
                                sy.dma_start(
                                    out=ap2(ksb, dt * 4096 + blkh * 2048,
                                            [[32768, 128], [1, 2048]]),
                                    in_=ap2(k_sh, blkh * DH * NTOK + dt * 128 * NTOK,
                                            [[NTOK, 128], [1, 2048]])).then_inc(S["dk"], 16)
                    for c in range(32):
                        cs, qt = divmod(c, 4)
                        if c == 8:
                            sy.wait_ge(S["cqst0"], 16 * 16)
                            sy.wait_ge(S["cqst1"], 16 * 16)
                            sy.dma_start(out=ap2(qsb, 0, [[16384, 128], [2048, 8], [1, 2048]]),
                                         in_=ap2(q_dr, 0,
                                                 [[NTOK, 128], [128 * NTOK, 8], [1, 2048]])
                                         ).then_inc(S["dq"], 16)
                            if not exchange:
                                sy.wait_ge(S["ckst0"], 16 * 32)
                                sy.wait_ge(S["ckst1"], 16 * 32)
                                for dt in range(8):
                                    sy.dma_start(
                                        out=ap2(ksb, dt * 4096, [[32768, 128], [1, 4096]]),
                                        in_=ap2(k_loc, dt * 128 * KT,
                                                [[KT, 128], [1, 4096]])).then_inc(S["dk"], 16)
                        if c >= 2:
                            sy.wait_ge(S["bfb"], c - 1)
                        sy.dma_start(out=bf_f[c % 2][:],
                                     in_=bf_w[cs, :, qt * 1024:(qt + 1) * 1024]).then_inc(S[f"bff{c % 2}"], 16)


                @blk.tensor
                def _(t):
                    t.wait_ge(S["bbor"], 16)
                    for cc in range(8):
                        if cc >= 1:
                            t.wait_ge(S["bboev"], cc)
                        t.matmul(ps_bo[:], onr[0:1, :], bor[0:1, cc * 512:(cc + 1) * 512],
                                 start=True, stop=True).then_inc(S["bbom"], 1)
                    t.wait_ge(S["bhg"], 16)
                    for cs in range(8):
                        t.wait_ge(S["bfb"], 4 * (cs + 1))
                        for nt in range(16):
                            gf = 16 * cs + nt
                            if gf >= 2:
                                t.wait_ge(S["bfev"], gf - 1)
                            for ert in range(8):
                                mm = t.matmul(ps_f[gf % 2][:],
                                              hgs[:, ert * 2048 + nt * 128:
                                                  ert * 2048 + nt * 128 + 128],
                                              bf_b[cs % 2][:, ert * 512:(ert + 1) * 512],
                                              start=(ert == 0), stop=(ert == 7))
                                if ert == 7:
                                    mm.then_inc(S["bfm"], 1)

                @blk.gpsimd
                def _(g):
                    if exchange:
                        with g.register("rn_d") as rn, g.register("rf0_d") as rf0, \
                                g.register("rf1_d") as rf1, g.register("rt0_d") as rt0, \
                                g.register("rt1_d") as rt1, g.register("rc_d") as rc:
                            g.wait_ge(S["cflw"], 48)
                            g.reg_load(rn, snon[0:1, 0:1])
                            g.reg_mov(rc, 1)
                            with g.While(rc):
                                g.load(rf0, fl_sh[0:1, 0:1])
                                g.load(rf1, fl_sh[1:2, 0:1])
                                g.reg_alu(rt0, rf0, rn, ALU.not_equal)
                                g.reg_alu(rt1, rf1, rn, ALU.not_equal)
                                g.reg_alu(rc, rt0, rt1, ALU.bitwise_or)
                        g.sem_inc(S["dkv"], 1)
                    for c in range(32):
                        cs, qt = divmod(c, 4)
                        g.wait_ge(S[f"bff{c % 2}"], 16 * (c // 2 + 1))
                        if cs >= 2 and qt == 0:
                            g.wait_ge(S["bfm"], 16 * (cs - 1))
                        g.tensor_copy(bf_b[cs % 2][:, qt * 1024:(qt + 1) * 1024],
                                      bf_f[c % 2][:]).then_inc(S["bfb"], 1)

                @blk.vector
                def _(v):
                    for cc in range(8):
                        v.wait_ge(S["bbom"], cc + 1)
                        v.tensor_copy(bo_b[:, cc * 512:(cc + 1) * 512],
                                      ps_bo[:]).then_inc(S["bboev"], 1)
                    for gf in range(128):
                        cs, nt = divmod(gf, 16)
                        v.wait_ge(S["bboev"], 8)
                        v.wait_ge(S["bfm"], gf + 1)
                        if gf >= 2:
                            v.wait_ge(S[f"bfst{gf % 2}"], 16 * (gf // 2))
                        v.scalar_tensor_tensor(stg_f[gf % 2][:], ps_f[gf % 2][:], 1.0,
                                               bo_b[:, cs * 512:(cs + 1) * 512],
                                               ALU.mult, ALU.add).then_inc(S["bfev"], 1)

                @blk.scalar
                def _(s):
                    for gf in range(128):
                        cs, nt = divmod(gf, 16)
                        s.wait_ge(S["bfev"], gf + 1)
                        s.dma_start(out=ap2(ffn_dr, nt * 128 * DM + cs * 512,
                                            [[DM, 128], [1, 512]]),
                                    in_=stg_f[gf % 2][:]).then_inc(S[f"bfst{gf % 2}"], 16)

    if tap == "ffn":
        with nc.Block() as blk:
            @blk.sync
            def _(sy):
                sy.wait_ge(S["bfst0"], 16 * 64)
                sy.wait_ge(S["bfst1"], 16 * 64)
                sy.dma_start(out=taps["ffn"][:, :], in_=ffn_dr[:, :]).then_inc(S["tapd"], 16)
                sy.wait_ge(S["tapd"], 16)

    if upto >= 7:
        with ExitStack() as st:
            vsb = st.enter_context(nc.sbuf_tensor("vsb", [128, 32768], BF16))
            wo_f = [st.enter_context(nc.sbuf_tensor(f"pwo_f{i}", [128, 1024], F32)) for i in range(2)]
            wo_bstg = [st.enter_context(nc.sbuf_tensor(f"pwo_b{i}", [128, 1024], BF16)) for i in range(2)]
            pts = st.enter_context(nc.sbuf_tensor("pts", [128, 16384], BF16))
            zrow = st.enter_context(nc.sbuf_tensor("zrow", [1, 512], F32))
            zb_sb = st.enter_context(nc.sbuf_tensor("zb_sb", [128, 512], F32))
            stg_pv = [st.enter_context(nc.sbuf_tensor(f"stg_pv{i}", [128, 512], BF16)) for i in range(2)]
            ps_s = [st.enter_context(nc.psum_tensor(f"ps_s{i}", [128, 512], F32)) for i in range(2)]
            ps_z = st.enter_context(nc.psum_tensor("ps_z", [1, 512], F32))
            ps_zb = st.enter_context(nc.psum_tensor("ps_zb", [128, 512], F32))
            ps_pv = [st.enter_context(nc.psum_tensor(f"ps_pv{i}", [128, 512], F32)) for i in range(2)]

            with nc.named_scope("phD"), nc.Block() as blk:
                @blk.sync
                def _(sy):
                    sy.wait_ge(S["bfst0"], 16 * 64)
                    sy.wait_ge(S["bfst1"], 16 * 64)
                    sy.wait_ge(S["bfm"], 128)
                    sy.wait_ge(S["bfev"], 128)
                    if exchange:
                        sy.wait_ge(S["dkv"], 1)
                        for blkh in range(2):
                            sy.dma_start(
                                out=ap2(vsb, blkh * 16 * 1024,
                                        [[32768, 128], [1024, 16], [1, 1024]]),
                                in_=ap2(v_sh, blkh * NTOK * DH,
                                        [[DH, 128], [128 * DH, 16], [1, 1024]])).then_inc(S["dv"], 16)
                    else:
                        sy.wait_ge(S["cvst0"], 16 * 32)
                        sy.wait_ge(S["cvst1"], 16 * 32)
                        sy.dma_start(
                            out=ap2(vsb, 0, [[32768, 128], [1024, 32], [1, 1024]]),
                            in_=ap2(v_loc, 0,
                                    [[DH, 128], [128 * DH, 32], [1, 1024]])).then_inc(S["dv"], 16)
                    for c in range(32):
                        cs, qt = divmod(c, 4)
                        if c >= 2:
                            sc, oc = pw("pwc")
                            sy.wait_ge(sc, oc + c - 1)
                        sf, of = pw(f"pwf{c % 2}")
                        sy.dma_start(out=wo_f[c % 2][:],
                                     in_=wo_w[cs, :, qt * 1024:(qt + 1) * 1024]
                                     ).then_inc(sf, 16)

                @blk.tensor
                def _(t):
                    t.wait_ge(S["dk"], 16 * (16 if exchange else 8))
                    t.wait_ge(S["dq"], 16)
                    for qc in range(4):
                        for kt in range(32):
                            gs = 32 * qc + kt
                            if gs >= 2:
                                t.wait_ge(S["dexp"], gs - 1)
                            for dt in range(8):
                                mm = t.matmul(ps_s[gs % 2][:],
                                              ksb[:, dt * 4096 + kt * 128:
                                                  dt * 4096 + kt * 128 + 128],
                                              qsb[:, dt * 2048 + qc * 512:
                                                  dt * 2048 + (qc + 1) * 512],
                                              start=(dt == 0), stop=(dt == 7))
                                if dt == 7:
                                    mm.then_inc(S["dsm"], 1)
                        if qc >= 1:
                            t.wait_ge(S["dzr"], qc)
                        for kt in range(32):
                            gs = 32 * qc + kt
                            t.wait_ge(S["dexp"], gs + 1)
                            mm = t.matmul(ps_z[:], onc[:, 0:1],
                                          pts[:, kt * 512:(kt + 1) * 512],
                                          start=(kt == 0), stop=(kt == 31))
                            if kt == 31:
                                mm.then_inc(S["dzm"], 1)
                        t.wait_ge(S["dzr"], qc + 1)
                        if qc >= 1:
                            t.wait_ge(S["dzbe"], qc)
                        t.matmul(ps_zb[:], onr[0:1, :], zrow[0:1, :],
                                 start=True, stop=True).then_inc(S["dzbm"], 1)
                        if qc == 0:
                            t.wait_ge(S["dv"], 16 * (2 if exchange else 1))
                        for dt in range(8):
                            gp = 8 * qc + dt
                            if gp >= 2:
                                t.wait_ge(S["dpve"], gp - 1)
                            for kt in range(32):
                                mm = t.matmul(ps_pv[gp % 2][:],
                                              vsb[:, kt * 1024 + dt * 128:
                                                  kt * 1024 + dt * 128 + 128],
                                              pts[:, kt * 512:(kt + 1) * 512],
                                              start=(kt == 0), stop=(kt == 31))
                                if kt == 31:
                                    mm.then_inc(S["dpvm"], 1)

                @blk.scalar
                def _(s):
                    s.wait_ge(S["bfst0"], 16 * 64)
                    s.wait_ge(S["bfst1"], 16 * 64)
                    for qc in range(4):
                        for kt in range(32):
                            gs = 32 * qc + kt
                            s.wait_ge(S["dsm"], gs + 1)
                            s.wait_ge(S["dpvm"], 8 * qc)
                            s.activation(pts[:, kt * 512:(kt + 1) * 512], ps_s[gs % 2][:],
                                         AF.Exp, bias=0.0, scale=0.03125).then_inc(S["dexp"], 1)
                        for dt in range(8):
                            gp = 8 * qc + dt
                            s.wait_ge(S["dpve"], gp + 1)
                            s.dma_start(out=ap2(pv_dr, dt * 128 * NTOK + qc * 512,
                                                [[NTOK, 128], [1, 512]]),
                                        in_=stg_pv[gp % 2][:]).then_inc(S[f"dpvst{gp % 2}"], 16)

                @blk.vector
                def _(v):
                    for qc in range(4):
                        v.wait_ge(S["dzm"], qc + 1)
                        v.reciprocal(zrow[0:1, :], ps_z[0:1, :]).then_inc(S["dzr"], 1)
                        v.wait_ge(S["dzbm"], qc + 1)
                        if qc >= 1:
                            v.wait_ge(S["dpve"], 8 * qc)
                        v.tensor_copy(zb_sb[:], ps_zb[:]).then_inc(S["dzbe"], 1)
                        for dt in range(8):
                            gp = 8 * qc + dt
                            v.wait_ge(S["dpvm"], gp + 1)
                            v.wait_ge(S["dzbe"], qc + 1)
                            if gp >= 2:
                                v.wait_ge(S[f"dpvst{gp % 2}"], 16 * (gp // 2))
                            v.scalar_tensor_tensor(stg_pv[gp % 2][:], ps_pv[gp % 2][:], 1.0,
                                                   zb_sb[:], ALU.mult, ALU.mult
                                                   ).then_inc(S["dpve"], 1)
                        for c in range(8 * qc, 8 * qc + 8):
                            sf, of = pw(f"pwf{c % 2}")
                            v.wait_ge(sf, of + 16 * (c // 2 + 1))
                            if c >= 2:
                                ss, os_ = pw(f"pws{c % 2}")
                                v.wait_ge(ss, os_ + 16 * (c // 2))
                            sc, _ = pw("pwc")
                            v.tensor_copy(wo_bstg[c % 2][:],
                                          wo_f[c % 2][:]).then_inc(sc, 1)

                @blk.gpsimd
                def _(g):
                    for c in range(32):
                        cs, qt = divmod(c, 4)
                        sc, oc = pw("pwc")
                        g.wait_ge(sc, oc + c + 1)
                        ss, _ = pw(f"pws{c % 2}")
                        g.dma_start(out=wo_bfd[cs, :, qt * 1024:(qt + 1) * 1024],
                                    in_=wo_bstg[c % 2][:]).then_inc(ss, 16)

    if tap == "pv":
        with nc.Block() as blk:
            @blk.sync
            def _(sy):
                sy.wait_ge(S["dpvst0"], 16 * 16)
                sy.wait_ge(S["dpvst1"], 16 * 16)
                sy.dma_start(out=taps["pv"][:, :], in_=pv_dr[:, :]).then_inc(S["tapd"], 16)
                sy.wait_ge(S["tapd"], 16)

    s2.close()

    if upto >= 8:
        with ExitStack() as st:
            pvs = st.enter_context(nc.sbuf_tensor("pvs", [128, 16384], BF16))
            wo_b = st.enter_context(nc.sbuf_tensor("wo_b", [128, 32768], BF16))
            xt = [st.enter_context(nc.sbuf_tensor(f"xt{i}", [128, 4096], F32)) for i in range(2)]
            ft = [st.enter_context(nc.sbuf_tensor(f"ft{i}", [128, 4096], BF16)) for i in range(2)]
            xf = [st.enter_context(nc.sbuf_tensor(f"xf{i}", [128, 4096], F32)) for i in range(2)]
            ystg = [st.enter_context(nc.sbuf_tensor(f"ystg{i}", [128, 1024], F32)) for i in range(2)]
            ps_p4 = st.enter_context(nc.psum_tensor("ps_p4", [128, 2048], F32))

            with nc.named_scope("phE"), nc.Block() as blk:
                @blk.gpsimd
                def _(g):
                    g.wait_ge(S["dpvm"], 32)
                    g.wait_ge(S["dpve"], 32)

                    def ld_nt(nt):
                        if nt >= 2:
                            g.wait_ge(S["exf"], nt - 1)
                            g.wait_ge(S["exfp"], nt - 1)
                        g.dma_start(out=xt[nt % 2][:],
                                    in_=x_in[nt * 128:(nt + 1) * 128, :]
                                    ).then_inc(S[f"exc{nt % 2}"], 16)
                        g.dma_start(out=ft[nt % 2][:],
                                    in_=ffn_dr[nt * 128:(nt + 1) * 128, :]
                                    ).then_inc(S[f"efc{nt % 2}"], 16)
                    ld_nt(0)
                    ld_nt(1)
                    for nt in range(16):
                        g.wait_ge(S[f"exc{nt % 2}"], 16 * (nt // 2 + 1))
                        g.wait_ge(S[f"efc{nt % 2}"], 16 * (nt // 2 + 1))
                        if nt >= 2:
                            g.wait_ge(S["et1"], 4 * (nt - 1))
                        g.tensor_tensor(xf[nt % 2][:, 0:3072], xt[nt % 2][:, 0:3072],
                                        ft[nt % 2][:, 0:3072],
                                        ALU.add).then_inc(S["exfp"], 1)
                        if nt + 2 < 16:
                            ld_nt(nt + 2)

                @blk.sync
                def _(sy):
                    sy.wait_ge(S["dpvm"], 32)
                    sy.wait_ge(S["dpve"], 32)
                    for cs in range(8):
                        s0, o0 = pw("pws0")
                        s1_, o1 = pw("pws1")
                        sy.wait_ge(s0, o0 + 16 * (2 * cs + 2))
                        sy.wait_ge(s1_, o1 + 16 * (2 * cs + 2))
                        sy.dma_start(out=wo_b[:, cs * 4096:(cs + 1) * 4096],
                                     in_=wo_bfd[cs, :, :]).then_inc(S["ewf0"], 16)


                @blk.tensor
                def _(t):
                    for nt in range(16):
                        if nt % 4 == 0:
                            ch = nt // 4
                            t.wait_ge(S["ep" if ch == 0 else f"ep{ch}"], 16)
                        for cs in range(8):
                            gy = 8 * nt + cs
                            if nt == 0:
                                t.wait_ge(S["ewf0"], 16 * (cs + 1))
                            if gy >= 4:
                                t.wait_ge(S["et1"], gy // 2 - 1)
                            for dt in range(8):
                                mm = t.matmul(ps_p4[:, (gy % 4) * 512:(gy % 4 + 1) * 512],
                                              pvs[:, dt * 2048 + nt * 128:
                                                  dt * 2048 + nt * 128 + 128],
                                              wo_b[:, cs * 4096 + dt * 512:
                                                   cs * 4096 + (dt + 1) * 512],
                                              start=(dt == 0), stop=(dt == 7))
                                if dt == 7:
                                    mm.then_inc(S["epm"], 1)

                @blk.vector
                def _(v):
                    def mkxf(nt):
                        v.wait_ge(S[f"exc{nt % 2}"], 16 * (nt // 2 + 1))
                        v.wait_ge(S[f"efc{nt % 2}"], 16 * (nt // 2 + 1))
                        if nt >= 2:
                            v.wait_ge(S["et1"], 4 * (nt - 1))
                        v.tensor_tensor(xf[nt % 2][:, 3072:4096], xt[nt % 2][:, 3072:4096],
                                        ft[nt % 2][:, 3072:4096],
                                        ALU.add).then_inc(S["exf"], 1)

                    mkxf(0)
                    mkxf(1)
                    for e in range(64):
                        gy = 2 * e
                        nt, cs = divmod(gy, 8)
                        v.wait_ge(S["epm"], gy + 2)
                        v.wait_ge(S["exf"], nt + 1)
                        v.wait_ge(S["exfp"], nt + 1)
                        if e >= 2:
                            v.wait_ge(S[f"eyst{e % 2}"], 16 * (e // 2))
                        v.scalar_tensor_tensor(ystg[e % 2][:],
                                               ps_p4[:, (gy % 4) * 512:(gy % 4) * 512 + 1024],
                                               1.0,
                                               xf[nt % 2][:, cs * 512:(cs + 2) * 512],
                                               ALU.mult, ALU.add).then_inc(S["et1"], 1)
                        if cs == 6 and nt + 2 < 16:
                            mkxf(nt + 2)

                @blk.scalar
                def _(s):
                    s.wait_ge(S["dpvm"], 32)
                    s.wait_ge(S["dpve"], 32)
                    for c in range(4):
                        s.wait_ge(S["dpvst0"], 16 * 4 * (c + 1))
                        s.wait_ge(S["dpvst1"], 16 * 4 * (c + 1))
                        s.dma_start(out=ap2(pvs, c * 512,
                                            [[16384, 128], [2048, 8], [1, 512]]),
                                    in_=ap2(pv_dr, c * 512,
                                            [[NTOK, 128], [128 * NTOK, 8], [1, 512]])
                                    ).then_inc(S["ep" if c == 0 else f"ep{c}"], 16)
                    for e in range(64):
                        gy = 2 * e
                        nt, cs = divmod(gy, 8)
                        s.wait_ge(S["et1"], e + 1)
                        s.dma_start(out=ap2(y_out, nt * 128 * DM + cs * 512,
                                            [[DM, 128], [1, 1024]]),
                                    in_=ystg[e % 2][:]).then_inc(S[f"eyst{e % 2}"], 16)
                    for q in range(2):
                        s.wait_ge(S[f"eyst{q}"], 16 * 32)

    main.close()
    nc.finalize()
    return nc, sorted(taps.keys())


def _host_layout(inputs):
    Wqkv = np.ascontiguousarray(np.asarray(inputs["L0_attn_Wqkv"], dtype=np.float32))
    bqkv = np.asarray(inputs["L0_attn_bqkv"], dtype=np.float32)
    Wo = np.ascontiguousarray(np.asarray(inputs["L0_attn_Wo"], dtype=np.float32))
    bo = np.asarray(inputs["L0_attn_bo"], dtype=np.float32)
    A = np.asarray(inputs["L0_ffn_A"], dtype=np.float32)
    B_ = np.asarray(inputs["L0_ffn_B"], dtype=np.float32)
    Wr = np.asarray(inputs["L0_ffn_Wr"], dtype=np.float32)

    wqk = np.ascontiguousarray(
        Wqkv[:, :2048].reshape(32, 128, 16, 128).transpose(2, 1, 0, 3).reshape(16, 128, 4096))
    wv = np.ascontiguousarray(
        Wqkv[:, 2048:].reshape(32, 128, 2, 512).transpose(2, 1, 0, 3).reshape(2, 128, 16384))
    wrl = np.ascontiguousarray(
        Wr.T.reshape(32, 128, 64).transpose(1, 0, 2).reshape(128, 2048))
    at = np.ascontiguousarray(
        A.reshape(1024, 4096).T.reshape(32, 128, 8, 128).transpose(2, 1, 0, 3).reshape(8, 128, 4096))
    bfl = np.ascontiguousarray(
        B_.transpose(0, 2, 1).reshape(1024, 4096).reshape(8, 128, 8, 512)
        .transpose(2, 1, 0, 3).reshape(8, 128, 4096))
    wo = np.ascontiguousarray(
        Wo.reshape(8, 128, 8, 512).transpose(2, 1, 0, 3).reshape(8, 128, 4096))
    bqk = np.ascontiguousarray(bqkv[:2048].reshape(16, 128).T)
    bvr = np.ascontiguousarray(bqkv[2048:].reshape(1, 1024))
    bor = np.ascontiguousarray(bo.reshape(1, 4096))

    idb = np.eye(128, dtype=BF16NP)
    idf = np.eye(128, dtype=np.float32)
    onc = np.ones((128, 1), dtype=BF16NP)
    onr = np.ones((1, 128), dtype=np.float32)
    emat = np.zeros((64, 1024), dtype=BF16NP)
    for e in range(64):
        emat[e, e * 16:(e + 1) * 16] = 1
    return dict(wqk=wqk, wv=wv, wr=wrl, at=at, bfl=bfl, wo=wo, bqk=bqk,
                bvr=bvr, bor=bor, idb=idb, idf=idf, onc=onc, onr=onr, emat=emat)


EXCHANGE = True


def kernel(**inputs):
    x = np.asarray(inputs["L0_attn_x"], dtype=np.float32)
    common = _host_layout(inputs)
    if EXCHANGE:
        common["non"] = np.full((1, 1), np.random.randint(1, 2**31 - 1), dtype=np.int32)
    in_maps = []
    for c in range(8):
        b, h = c // 2, c % 2
        m = dict(common)
        m["x"] = np.ascontiguousarray(x[b, h * NTOK:(h + 1) * NTOK, :])
        if not EXCHANGE:
            m["xp"] = np.ascontiguousarray(x[b, (1 - h) * NTOK:(2 - h) * NTOK, :])
        in_maps.append(m)
    nc, _ = build_nc(exchange=EXCHANGE)
    res = run_bass_kernel_spmd(nc, in_maps, core_ids=list(range(8)))
    y = np.empty((4, 4096, 4096), dtype=np.float32)
    for c in range(8):
        b, h = c // 2, c % 2
        y[b, h * NTOK:(h + 1) * NTOK, :] = res.results[c]["y"]
    return y
